# version 19
# baseline (speedup 1.0000x reference)
from contextlib import ExitStack

import numpy as np
import ml_dtypes

import concourse.bass as bass
import concourse.mybir as mybir
from concourse.bass_utils import run_bass_kernel_spmd

F32 = mybir.dt.float32
BF16 = mybir.dt.bfloat16
AF = mybir.ActivationFunctionType
ALU = mybir.AluOpType
bfnp = ml_dtypes.bfloat16

T_STEPS = (5, 6, 7)
USE_FIN = True

NT = 1024
D = 2048
NH = 8
NMEM = 2048
MD = 1024
FF = 4096
EPS = 1e-5
LN_HALF = float(np.log(0.5))

ENGS = ("sync", "vector", "scalar", "tensor", "gpsimd")
NDMA = 8


class KB:

    def __init__(self):
        self.q = {e: [] for e in ENGS}
        self.cnt = {e: 0 for e in ENGS}
        self.dma_i = 0
        self.dma_use = [0] * NDMA
        self.res = {}
        self.barrier_deps = []

    def barrier(self):
        self.barrier_deps = [(e, self.cnt[e]) for e in ENGS if self.cnt[e] > 0]
        self.barrier_deps += [(f"dma{i}", 16 * self.dma_use[i]) for i in range(NDMA) if self.dma_use[i] > 0]

    def _res(self, name):
        return self.res.setdefault(name, {"w": None, "r": []})

    def op(self, eng, fn, reads=(), writes=(), deps=(), dma=False):
        dep = list(deps) + self.barrier_deps
        pre = []
        if dma:
            s = self.dma_i % NDMA
            self.dma_i += 1
            if self.dma_use[s] > 0:
                pre.append((f"dma{s}", 16 * self.dma_use[s]))
            self.dma_use[s] += 1
            tok = (f"dma{s}", 16 * self.dma_use[s])
            sem_tag = f"dma{s}"
        else:
            tok = (eng, self.cnt[eng] + 1)
            self.cnt[eng] = tok[1]
            sem_tag = eng
        for nm in reads:
            r = self._res(nm)
            if r["w"] is not None:
                dep.append(r["w"])
        for nm in writes:
            r = self._res(nm)
            if r["w"] is not None:
                dep.append(r["w"])
            dep.extend(r["r"])
        for nm in reads:
            self._res(nm)["r"].append(tok)
        for nm in writes:
            r = self._res(nm)
            r["w"] = tok
            r["r"] = []
        self.q[eng].append((fn, tuple(dep), tuple(pre), sem_tag, tok[1]))
        return tok

    SELF_SYNC = True

    def emit(self, eng_name, eng, sems):
        waited = {}
        for fn, deps, pre, sem_tag, tokval in self.q[eng_name]:
            need = {}
            for (de, dv) in list(deps) + list(pre):
                if de == eng_name and not self.SELF_SYNC:
                    continue
                if dv > waited.get(de, 0):
                    need[de] = max(need.get(de, 0), dv)
            for de in sorted(need):
                eng.wait_ge(sems[de], need[de])
                waited[de] = need[de]
            ins = fn(eng)
            if sem_tag.startswith("dma"):
                ins.then_inc(sems[sem_tag], 16)
            elif hasattr(eng, "engine_nop"):
                eng.engine_nop().then_inc(sems[sem_tag], 1)
            else:
                ins.then_inc(sems[sem_tag], 1)


def build_program(detect_races=True, debug=False):
    nc = bass.Bass("TRN2", target_bir_lowering=False, detect_race_conditions=detect_races)

    def dram(name, shape, dt=F32, kind="ExternalInput"):
        return nc.dram_tensor(name, shape, dt, kind=kind).ap()

    x_h = dram("x", [NT, D])
    mem_h = dram("mem", [NMEM, MD])
    wq_h = dram("wqT", [8, 128, 16 * 128])
    wk_h = dram("wkT", [MD, MD])
    wv_h = dram("wvT", [MD, MD])
    wo_h = dram("woT", [NH * 128, D])
    wg_h = dram("wgT", [128, 32])
    wa_h = [dram("w0aT", [32, 128, 16 * 128]), dram("w1aT", [32, 128, 16 * 128])]
    wb_h = [dram("w0bT", [FF, D]), dram("w1bT", [FF, D])]
    lb_h = dram("log_beta", [1, NH])
    gs_h = dram("g_state", [1, D])
    bs_h = dram("b_state", [1, D])
    gm_h = dram("g_mem", [1, MD])
    bm_h = dram("b_mem", [1, MD])
    ba_h = [dram("b0a", [128, 32]), dram("b1a", [128, 32])]
    bb_h = [dram("b0b", [1, D]), dram("b1b", [1, D])]
    ones_h = dram("ones1", [1, 128])
    eye_h = dram("eye", [128, 128], BF16)
    out_h = dram("out", [NT, D], kind="ExternalOutput")
    dbg = {}
    if debug:
        for nm, sh, dt in (("dbg_KT", [128, NH * NMEM], BF16), ("dbg_Vb", [128, 16 * MD], BF16),
                           ("dbg_xi0", [128, NH * NT], BF16), ("dbg_xi1", [128, NH * NT], BF16),
                           ("dbg_xi2", [128, NH * NT], BF16), ("dbg_xi3", [128, NH * NT], BF16),
                           ("dbg_x1", [128, 8 * D], BF16), ("dbg_acc", [128, 8 * D], BF16),
                           ("dbg_mnT", [128, 8 * NMEM], BF16), ("dbg_ag", [128, 2], F32),
                           ("dbg_colB", [128, 8], F32), ("dbg_msc", [128, MD], F32),
                           ("dbg_mnbf", [128, MD], BF16)):
            dbg[nm] = dram(nm, sh, dt, kind="ExternalOutput")

    S = KB()

    def mk_ln_cols(lnbufs):
        def ln_cols(S, src_ap, src_res, nchunks, par):
            p = par % 2
            bnst, colA, colB = lnbufs[p]
            bn, cA, cB = f"bnst{p}", f"colA{p}", f"colB{p}"
            def st_fn(e):
                for i in range(nchunks):
                    ins = e.bn_stats(bnst[:, i * 6:(i + 1) * 6], src_ap[:, i * 512:(i + 1) * 512])
                return ins
            S.op("vector", st_fn, reads=[src_res], writes=[bn])
            S.op("vector", lambda e: e.bn_aggr(colA[:, 6:8], bnst[:, 0:6 * nchunks]),
                 reads=[bn], writes=[cA])
            S.op("vector", lambda e: e.tensor_scalar(colA[:, 0:1], colA[:, 6:7], -1.0, None, ALU.mult, ALU.bypass),
                 reads=[cA], writes=[cA])
            S.op("vector", lambda e: e.tensor_scalar(colB[:, 0:1], colA[:, 7:8], EPS, None, ALU.add, ALU.bypass),
                 reads=[cA], writes=[cB])
            S.op("scalar", lambda e: e.activation(colB[:, 0:1], colB[:, 0:1], AF.Ln),
                 reads=[cB], writes=[cB])
            S.op("scalar", lambda e: e.activation(colB[:, 0:1], colB[:, 0:1], AF.Exp, scale=-0.5),
                 reads=[cB], writes=[cB])
            return colA, colB, cA, cB
        return ln_cols

    with ExitStack() as ctx:
        def sbuf(stack, name, shape, dt):
            return stack.enter_context(nc.sbuf_tensor(name, shape, dt))

        ones_s = sbuf(ctx, "ones_s", [1, 128], F32)
        eye_s = sbuf(ctx, "eye_s", [128, 128], BF16)
        lb_s = sbuf(ctx, "lb_s", [1, NH], F32)
        ln2_s = sbuf(ctx, "ln2_s", [1, 1], F32)
        beta_bc = sbuf(ctx, "beta_bc", [128, NH], F32)
        vec_s = sbuf(ctx, "vec_s", [1, D], F32)
        lnbufs = [(sbuf(ctx, f"bnst{i}", [128, 24], F32),
                   sbuf(ctx, f"colA{i}", [128, 8], F32),
                   sbuf(ctx, f"colB{i}", [128, 8], F32)) for i in range(2)]
        ln_cols = mk_ln_cols(lnbufs)
        ymax = sbuf(ctx, "ymax", [128, 8], F32)
        ntau_a = sbuf(ctx, "ntau_a", [128, 8], F32)
        ntau_b = sbuf(ctx, "ntau_b", [128, 8], F32)
        ntau_f = sbuf(ctx, "ntau_f", [128, 8], F32)
        src_c = sbuf(ctx, "src_c", [128, 8], F32)
        sr2_c = sbuf(ctx, "sr2_c", [128, 8], F32)
        kc_c = sbuf(ctx, "kc_c", [128, 8], F32)
        c1 = sbuf(ctx, "c1", [128, 8], F32)
        c2 = sbuf(ctx, "c2", [128, 8], F32)
        c3 = sbuf(ctx, "c3", [128, 8], F32)
        mask0_c = sbuf(ctx, "mask0_c", [128, 8], F32)
        mask1_c = sbuf(ctx, "mask1_c", [128, 8], F32)
        x1nat = sbuf(ctx, "x1nat", [128, 8 * D], BF16)

        S.op("sync", lambda e: e.dma_start(out=ones_s[:], in_=ones_h), writes=["ones_s"], dma=True)
        S.op("sync", lambda e: e.dma_start(out=eye_s[:], in_=eye_h), writes=["eye_s"], dma=True)
        S.op("sync", lambda e: e.dma_start(out=lb_s[:], in_=lb_h), writes=["lb_s"], dma=True)
        S.op("vector", lambda e: e.memset(ln2_s[:], LN_HALF), writes=["ln2_s"])
        S.op("scalar", lambda e: e.activation(lb_s[:], lb_s[:], AF.Exp, bias=ln2_s[0:1, 0:1], scale=1.0),
             reads=["ln2_s"], writes=["lb_s"])
        with nc.psum_tensor("p_b0", [128, 8], F32) as p_b0:
            S.op("tensor", lambda e: e.matmul(p_b0[:, 0:8], ones_s[0:1, :], lb_s[0:1, :], start=True, stop=True),
                 reads=["ones_s", "lb_s"], writes=["p_b0"])
            S.op("vector", lambda e: e.tensor_copy(beta_bc[:], p_b0[:, 0:8]),
                 reads=["p_b0"], writes=["beta_bc"])

        S.barrier()
        def bcast(dst, dres, row_h, width, psum, pres, stack=None):
            S.op("sync", lambda e: e.dma_start(out=vec_s[0:1, 0:width], in_=row_h),
                 writes=["vec_s"], dma=True)
            for j0 in range(0, width, 512):
                S.op("tensor",
                     lambda e, j0=j0: e.matmul(psum[:, j0 % 2048:j0 % 2048 + 512], ones_s[0:1, :],
                                               vec_s[0:1, j0:j0 + 512], start=True, stop=True),
                     reads=["ones_s", "vec_s"], writes=[f"{pres}{j0 % 2048}"])
                S.op("vector", lambda e, j0=j0: e.tensor_copy(dst[:, j0:j0 + 512], psum[:, j0 % 2048:j0 % 2048 + 512]),
                     reads=[f"{pres}{j0 % 2048}"], writes=[dres])

        with ExitStack() as attn:
            KT = sbuf(attn, "KT", [128, NH * NMEM], BF16)
            Vb = sbuf(attn, "Vb", [128, 16 * MD], BF16)
            xiT_a = sbuf(attn, "xiT_a", [128, NH * NT], BF16)
            xiT_b = sbuf(attn, "xiT_b", [128, NH * NT], BF16)

            with ExitStack() as p1:
                mnT = sbuf(p1, "mnT", [128, 8 * NMEM], BF16)
                mnT_v = mnT.ap().rearrange("p (mc n) -> p mc n", mc=8)
                gm_bc = sbuf(p1, "gm_bc", [128, MD], BF16)
                bm_bc = sbuf(p1, "bm_bc", [128, MD], BF16)
                with ExitStack() as p1a:
                    mst = sbuf(p1a, "mst", [128, 4 * MD], F32)
                    msc2 = [sbuf(p1a, f"msc{i}", [128, MD], F32) for i in range(2)]
                    mnbf = [sbuf(p1a, f"mnbf{i}", [128, MD], BF16) for i in range(2)]
                    with nc.psum_tensor("p_m", [128, 1024], F32) as p_m:
                        bcast(gm_bc, "gm_bc", gm_h, MD, p_m, "p_m")
                        bcast(bm_bc, "bm_bc", bm_h, MD, p_m, "p_m")
                    S.barrier()
                    with nc.psum_tensor("p_tr1", [128, 1024], BF16) as p_tr1:
                        for half in range(4):
                            mres = "mst"
                            S.op("sync",
                                 lambda e, half=half: e.dma_start(
                                     out=mst.ap().rearrange("p (g m) -> p g m", g=4),
                                     in_=mem_h.rearrange("(g p) m -> p g m", p=128)[:, half * 4:(half + 1) * 4, :]),
                                 writes=[mres], dma=True)
                            for it in range(4):
                                nt_i = half * 4 + it
                                mtile = mst[:, it * MD:(it + 1) * MD]
                                cA_t, cB_t, cAr, cBr = ln_cols(S, mtile, mres, 2, nt_i)
                                msc = msc2[nt_i % 2]
                                mscr = f"msc{nt_i % 2}"
                                S.op("vector", lambda e, mtile=mtile, cA_t=cA_t, msc=msc: e.tensor_scalar(msc[:], mtile, cA_t[:, 0:1], None, ALU.add, ALU.bypass),
                                     reads=[mres, cAr], writes=[mscr])
                                S.op("vector", lambda e, cB_t=cB_t, msc=msc: e.scalar_tensor_tensor(msc[:], msc[:], cB_t[:, 0:1], gm_bc[:], ALU.mult, ALU.mult),
                                     reads=[mscr, cBr, "gm_bc"], writes=[mscr])
                                mb = mnbf[nt_i % 2]
                                mbres = f"mnbf{nt_i % 2}"
                                S.op("vector", lambda e, mb=mb, msc=msc: e.tensor_tensor(mb[:], msc[:], bm_bc[:], ALU.add),
                                     reads=[mscr, "bm_bc"], writes=[mbres])
                                def tr_fn(e, mb=mb):
                                    for j in range(8):
                                        ins = e.matmul(p_tr1[:, j * 128:(j + 1) * 128], mb[:, j * 128:(j + 1) * 128], eye_s[:],
                                                       is_transpose=True, start=(j == 0), stop=(j == 7), skip_group_check=True)
                                    return ins
                                S.op("tensor", tr_fn, reads=[mbres, "eye_s"], writes=["p_tr1"])
                                S.op("vector",
                                     lambda e, nt_i=nt_i: e.tensor_copy(
                                         mnT_v[:, :, nt_i * 128:(nt_i + 1) * 128], p_tr1[:]),
                                     reads=["p_tr1"], writes=["mnT"])
                                if debug and nt_i == 15:
                                    S.op("sync", lambda e: e.dma_start(out=dbg["dbg_ag"], in_=bnag[:]), reads=["bnag"], dma=True)
                                    S.op("sync", lambda e: e.dma_start(out=dbg["dbg_colB"], in_=colB[:]), reads=["colB"], dma=True)
                                    S.op("sync", lambda e: e.dma_start(out=dbg["dbg_msc"], in_=msc[:]), reads=["msc"], dma=True)
                                    S.op("sync", lambda e, mb=mb: e.dma_start(out=dbg["dbg_mnbf"], in_=mb[:]), reads=[mbres], dma=True)
                if debug:
                    S.op("sync", lambda e: e.dma_start(out=dbg["dbg_mnT"], in_=mnT[:]), reads=["mnT"], dma=True)
                S.barrier()
                for wgt_h, name in ((wk_h, "K"), (wv_h, "V")):
                    S.barrier()
                    with ExitStack() as pw:
                        wst = sbuf(pw, "wst_" + name, [128, 4 * MD], F32)
                        wbf = sbuf(pw, "wbf_" + name, [128, 8 * MD], BF16)
                        for hf in range(2):
                            S.op("sync",
                                 lambda e, wgt_h=wgt_h, hf=hf: e.dma_start(
                                     out=wst.ap().rearrange("p (mc f) -> p mc f", mc=4),
                                     in_=wgt_h.rearrange("(mc p) f -> p mc f", p=128)[:, hf * 4:(hf + 1) * 4, :]),
                                 writes=["wst"], dma=True)
                            for i in range(4):
                                S.op("vector",
                                     lambda e, i=i, hf=hf: e.tensor_copy(
                                         wbf[:, (hf * 4 + i) * MD:(hf * 4 + i + 1) * MD], wst[:, i * MD:(i + 1) * MD]),
                                     reads=["wst"], writes=["wbf"])
                        with nc.psum_tensor("p_kv_" + name, [128, 1024], F32) as p_kv:
                            if name == "K":
                                for f in range(8):
                                    for ncol in range(4):
                                        def mm_fn(e, f=f, ncol=ncol):
                                            for mc in range(8):
                                                ins = e.matmul(p_kv[:, (ncol % 2) * 512:(ncol % 2) * 512 + 512],
                                                               wbf[:, mc * MD + f * 128:mc * MD + (f + 1) * 128],
                                                               mnT_v[:, mc, ncol * 512:(ncol + 1) * 512],
                                                               start=(mc == 0), stop=(mc == 7))
                                            return ins
                                        pr = f"p_kv{ncol % 2}"
                                        S.op("tensor", mm_fn, reads=["wbf", "mnT"], writes=[pr])
                                        S.op("vector",
                                             lambda e, f=f, ncol=ncol: e.tensor_copy(
                                                 KT[:, f * NMEM + ncol * 512:f * NMEM + (ncol + 1) * 512],
                                                 p_kv[:, (ncol % 2) * 512:(ncol % 2) * 512 + 512]),
                                             reads=[pr], writes=["KT"])
                            else:
                                for ntile in range(16):
                                    for fcol in range(2):
                                        def mm_fn(e, ntile=ntile, fcol=fcol):
                                            for mc in range(8):
                                                ins = e.matmul(p_kv[:, fcol * 512:(fcol + 1) * 512],
                                                               mnT_v[:, mc, ntile * 128:(ntile + 1) * 128],
                                                               wbf[:, mc * MD + fcol * 512:mc * MD + (fcol + 1) * 512],
                                                               start=(mc == 0), stop=(mc == 7))
                                            return ins
                                        pr = f"p_kv{fcol}"
                                        S.op("tensor", mm_fn, reads=["wbf", "mnT"], writes=[pr])
                                        S.op("vector",
                                             lambda e, ntile=ntile, fcol=fcol: e.tensor_copy(
                                                 Vb[:, ntile * MD + fcol * 512:ntile * MD + (fcol + 1) * 512],
                                                 p_kv[:, fcol * 512:(fcol + 1) * 512]),
                                             reads=[pr], writes=["Vb"])

            if debug:
                S.op("sync", lambda e: e.dma_start(out=dbg["dbg_KT"], in_=KT[:]), reads=["KT"], dma=True)
                S.op("sync", lambda e: e.dma_start(out=dbg["dbg_Vb"], in_=Vb[:]), reads=["Vb"], dma=True)
            S.barrier()
            with ExitStack() as p2:
                gs_bc = sbuf(p2, "gs_bc", [128, D], BF16)
                bs_bc = sbuf(p2, "bs_bc", [128, D], BF16)
                lnxTh = sbuf(p2, "lnxTh", [128, 16 * 512], BF16)
                lnxT_v = lnxTh.ap().rearrange("p (cc t) -> p cc t", cc=16)
                xst = [sbuf(p2, f"xst{i}", [128, D], F32) for i in range(2)]
                lnbf = [sbuf(p2, f"lnbf{i}", [128, D], BF16) for i in range(2)]
                wqst = sbuf(p2, "wqst", [128, 16 * 128], F32)
                wqbf = sbuf(p2, "wqbf", [128, 16 * 128], BF16)
                with nc.psum_tensor("p_bb", [128, 2048], F32) as p_bb:
                    bcast(gs_bc, "gs_bc", gs_h, D, p_bb, "p_bb")
                    bcast(bs_bc, "bs_bc", bs_h, D, p_bb, "p_bb")
                S.barrier()
                with (nc.psum_tensor("p_tr2", [128, 2048], BF16) as p_tr2,
                      nc.psum_tensor("p_q", [128, 1024], F32) as p_q):
                    for half in range(2):
                        for kk in range(4):
                            k = half * 4 + kk
                            xs = xst[kk % 2]
                            xres = f"xst{kk % 2}"
                            S.op("sync", lambda e, xs=xs, k=k: e.dma_start(out=xs[:], in_=x_h[k * 128:(k + 1) * 128, :]),
                                 writes=[xres], dma=True)
                            cA_t, cB_t, cAr, cBr = ln_cols(S, xs, xres, 4, k)
                            S.op("vector", lambda e, xs=xs, cA_t=cA_t: e.tensor_scalar(xs[:], xs[:], cA_t[:, 0:1], None, ALU.add, ALU.bypass),
                                 reads=[xres, cAr], writes=[xres])
                            S.op("vector", lambda e, xs=xs, cB_t=cB_t: e.scalar_tensor_tensor(xs[:], xs[:], cB_t[:, 0:1], gs_bc[:], ALU.mult, ALU.mult),
                                 reads=[xres, cBr, "gs_bc"], writes=[xres])
                            lb2 = lnbf[kk % 2]
                            lbres = f"lnbf{kk % 2}"
                            S.op("vector", lambda e, lb2=lb2, xs=xs: e.tensor_tensor(lb2[:], xs[:], bs_bc[:], ALU.add),
                                 reads=[xres, "bs_bc"], writes=[lbres])
                            def tr_fn(e, lb2=lb2):
                                for j in range(16):
                                    ins = e.matmul(p_tr2[:, j * 128:(j + 1) * 128], lb2[:, j * 128:(j + 1) * 128], eye_s[:],
                                                   is_transpose=True, start=(j % 8 == 0), stop=(j % 8 == 7), skip_group_check=True)
                                return ins
                            S.op("tensor", tr_fn, reads=[lbres, "eye_s"], writes=["p_tr2"])
                            S.op("vector",
                                 lambda e, kk=kk: e.tensor_copy(lnxT_v[:, :, kk * 128:(kk + 1) * 128], p_tr2[:]),
                                 reads=["p_tr2"], writes=["lnxTh"])
                        for h in range(8):
                            S.op("sync", lambda e, h=h: e.dma_start(out=wqst[:], in_=wq_h[h]),
                                 writes=["wqst"], dma=True)
                            for i in range(4):
                                S.op("vector", lambda e, i=i: e.tensor_copy(wqbf[:, i * 512:(i + 1) * 512], wqst[:, i * 512:(i + 1) * 512]),
                                     reads=["wqst"], writes=["wqbf"])
                            def mm_fn(e):
                                for cc in range(16):
                                    ins = e.matmul(p_q[:, 0:512],
                                                   wqbf[:, cc * 128:(cc + 1) * 128],
                                                   lnxT_v[:, cc, 0:512],
                                                   start=(cc == 0), stop=(cc == 15))
                                return ins
                            S.op("tensor", mm_fn, reads=["wqbf", "lnxTh"], writes=["p_q"])
                            S.op("vector",
                                 lambda e, h=h, half=half: e.tensor_copy(
                                     xiT_a[:, h * NT + half * 512:h * NT + half * 512 + 512], p_q[:, 0:512]),
                                 reads=["p_q"], writes=[f"xiT_a_h{h}"])

            S.barrier()
            with ExitStack() as p3:
                y_buf = sbuf(p3, "y_buf", [128, 8 * NMEM], BF16)
                r_buf = [sbuf(p3, f"r_buf{i}", [128, NMEM], BF16) for i in range(2)]
                sdump = sbuf(p3, "sdump", [128, NMEM], BF16)
                w_fin = [sbuf(p3, f"w_fin{i}", [128, NMEM], BF16) for i in range(2)]
                wT_buf = sbuf(p3, "wT_buf", [128, 16 * 512], BF16)
                wT_v = wT_buf.ap().rearrange("p (nc t) -> p nc t", nc=16)
                with (nc.psum_tensor("p_sc", [128, 2048], F32) as p_sc,
                      nc.psum_tensor("p_wt", [128, 2048], BF16) as p_wt,
                      nc.psum_tensor("p_ap", [128, 1024], F32) as p_ap):
                    def scores_pass0(step, h, k, xiT_src, src_res):
                        def mm_fn(e, h=h, k=k, xiT_src=xiT_src):
                            for j in range(4):
                                ins = e.matmul(p_sc[:, j * 512:(j + 1) * 512],
                                               xiT_src[:, h * NT + k * 128:h * NT + (k + 1) * 128],
                                               KT[:, h * NMEM + j * 512:h * NMEM + (j + 1) * 512],
                                               start=True, stop=True)
                            return ins
                        S.op("tensor", mm_fn, reads=[f"{src_res}{h}", "KT"], writes=["p_sc"])
                        S.op("vector",
                             lambda e, h=h, k=k: e.tensor_scalar(
                                 y_buf[:, k * NMEM:(k + 1) * NMEM], p_sc[:],
                                 beta_bc[:, h:h + 1], None, ALU.mult, ALU.max,
                                 accum_out=ymax[:, k:k + 1]),
                             reads=["p_sc", "beta_bc"], writes=[f"y{k}", "ymax"])

                    def final_relu(h, k, fin, fin_res):
                        rb = r_buf[k % 2]
                        S.op("vector",
                             lambda e, k=k, rb=rb, fin=fin: e.tensor_scalar(
                                 rb[:], y_buf[:, k * NMEM:(k + 1) * NMEM],
                                 fin[:, k:k + 1], 0.0, ALU.add, ALU.max),
                             reads=[f"y{k}", fin_res], writes=[f"rb{k % 2}"])

                    def final_sq_tr(h, k):
                        rb = r_buf[k % 2]
                        wf = w_fin[k % 2]
                        S.op("scalar",
                             lambda e, rb=rb, wf=wf: e.activation(wf[:], rb[:], AF.Square),
                             reads=[f"rb{k % 2}"], writes=[f"wf{k % 2}"])
                        def tr_fn(e, wf=wf):
                            for j in range(16):
                                ins = e.matmul(p_wt[:, j * 128:(j + 1) * 128], wf[:, j * 128:(j + 1) * 128], eye_s[:],
                                               is_transpose=True, start=(j % 8 == 0), stop=(j % 8 == 7),
                                               skip_group_check=True)
                            return ins
                        S.op("tensor", tr_fn, reads=[f"wf{k % 2}", "eye_s"], writes=["p_wt"])

                    def final_evac(h, k):
                        S.op("scalar",
                             lambda e, k=k: e.activation(
                                 wT_v[:, :, (k % 4) * 128:(k % 4 + 1) * 128], p_wt[:], AF.Copy),
                             reads=["p_wt"], writes=["wT"])

                    def apply_half(h, half, xiT_dst, dst_res):
                        def mm_fn(e, h=h):
                            for ncv in range(16):
                                ins = e.matmul(p_ap[:, (half % 2) * 512:(half % 2) * 512 + 512],
                                               Vb[:, ncv * MD + h * 128:ncv * MD + (h + 1) * 128],
                                               wT_v[:, ncv, 0:512],
                                               start=(ncv == 0), stop=(ncv == 15))
                            return ins
                        S.op("tensor", mm_fn, reads=["Vb", "wT"], writes=[f"p_ap{half % 2}"])
                        S.op("vector",
                             lambda e, h=h, half=half, xiT_dst=xiT_dst: e.tensor_copy(
                                 xiT_dst[:, h * NT + half * 512:h * NT + (half + 1) * 512],
                                 p_ap[:, (half % 2) * 512:(half % 2) * 512 + 512]),
                             reads=[f"p_ap{half % 2}"], writes=[f"{dst_res}{h}"])

                    for step in range(3):
                        T = T_STEPS[step]
                        if step % 2 == 0:
                            xiT_src, src_res, xiT_dst, dst_res = xiT_a, "xiT_a_h", xiT_b, "xiT_b_h"
                        else:
                            xiT_src, src_res, xiT_dst, dst_res = xiT_b, "xiT_b_h", xiT_a, "xiT_a_h"
                        for h in range(8):
                            if h == 0:
                                for k in range(8):
                                    scores_pass0(step, h, k, xiT_src, src_res)
                            S.op("vector",
                                 lambda e: e.tensor_scalar(ntau_a[:, 0:8], ymax[:, 0:8], -1.0, 1.0, ALU.mult, ALU.add),
                                 reads=["ymax"], writes=["ntau_a"])
                            cur, cur_res = ntau_a, "ntau_a"
                            nxt, nxt_res = ntau_b, "ntau_b"
                            for j in range(T):
                                for k in range(8):
                                    rb = r_buf[k % 2]
                                    rres = f"rb{k % 2}"
                                    S.op("scalar",
                                         lambda e, k=k, rb=rb, cur=cur: e.activation(
                                             rb[:], y_buf[:, k * NMEM:(k + 1) * NMEM], AF.Relu,
                                             bias=cur[:, k:k + 1], scale=1.0, accum_out=src_c[:, k:k + 1]),
                                         reads=[f"y{k}", cur_res], writes=[rres, "src_c"])
                                    if k == 7:
                                        S.op("scalar",
                                             lambda e, k=k, rb=rb: e.activation(
                                                 sdump[:], rb[:], AF.Square, accum_out=sr2_c[:, k:k + 1]),
                                             reads=[rres], writes=["sdump", "sr2_c"])
                                    else:
                                        S.op("vector",
                                             lambda e, k=k, rb=rb, cur=cur: e.scalar_tensor_tensor(
                                                 sdump[:], y_buf[:, k * NMEM:(k + 1) * NMEM],
                                                 cur[:, k:k + 1], rb[:], ALU.add, ALU.mult,
                                                 accum_out=sr2_c[:, k:k + 1]),
                                             reads=[f"y{k}", cur_res, rres], writes=["sdump", "sr2_c"])
                                S.op("vector", lambda e: e.tensor_scalar(c1[:, 0:8], src_c[:, 0:8], 1e-30, None, ALU.max, ALU.bypass),
                                     reads=["src_c"], writes=["c1"])
                                S.op("vector", lambda e: e.reciprocal(c1[:, 0:8], c1[:, 0:8]),
                                     reads=["c1"], writes=["c1"])
                                S.op("vector", lambda e: e.scalar_tensor_tensor(c2[:, 0:8], sr2_c[:, 0:8], -1.0, c1[:, 0:8], ALU.add, ALU.mult),
                                     reads=["sr2_c", "c1"], writes=["c2"])
                                S.op("vector",
                                     lambda e, cur=cur, nxt=nxt: e.scalar_tensor_tensor(
                                         nxt[:, 0:8], c2[:, 0:8], -0.5, cur[:, 0:8], ALU.mult, ALU.add),
                                     reads=["c2", cur_res], writes=[nxt_res])
                                cur, nxt = nxt, cur
                                cur_res, nxt_res = nxt_res, cur_res
                            nxt_head = h + 1 if h < 7 else None
                            final_relu(h, 0, cur, cur_res)
                            for k in range(8):
                                final_sq_tr(h, k)
                                if k < 7:
                                    final_relu(h, k + 1, cur, cur_res)
                                final_evac(h, k)
                                if nxt_head is not None:
                                    scores_pass0(step, nxt_head, k, xiT_src, src_res)
                                if k == 3:
                                    apply_half(h, 0, xiT_dst, dst_res)
                            apply_half(h, 1, xiT_dst, dst_res)
                    xiT_fin = xiT_b if len(T_STEPS) % 2 == 1 else xiT_a
                    xiT_fin_res = "xiT_b_h" if len(T_STEPS) % 2 == 1 else "xiT_a_h"

            S.barrier()
            with ExitStack() as p4:
                woT_sb = sbuf(p4, "woT_sb", [128, 8 * D], BF16)
                wost = sbuf(p4, "wost", [128, D], F32)
                xrs = [sbuf(p4, f"xrs{i}", [128, D], F32) for i in range(2)]
                x1p = sbuf(p4, "x1p", [128, D], F32)
                x1c = sbuf(p4, "x1c", [128, D], BF16)
                with nc.psum_tensor("p_wo", [128, 2048], F32) as p_wo:
                    for f in range(8):
                        S.op("sync", lambda e, f=f: e.dma_start(out=wost[:], in_=wo_h[f * 128:(f + 1) * 128, :]),
                             writes=["wost"], dma=True)
                        S.op("vector", lambda e, f=f: e.tensor_copy(woT_sb[:, f * D:(f + 1) * D], wost[:]),
                             reads=["wost"], writes=["woT_sb"])
                    for k in range(8):
                        S.op("sync", lambda e, k=k: e.dma_start(out=xrs[k % 2][:], in_=x_h[k * 128:(k + 1) * 128, :]),
                             writes=[f"xrs{k % 2}"], dma=True)
                        def mm_fn(e, k=k):
                            for f in range(8):
                                for j in range(4):
                                    ins = e.matmul(p_wo[:, j * 512:(j + 1) * 512],
                                                   xiT_fin[:, f * NT + k * 128:f * NT + (k + 1) * 128],
                                                   woT_sb[:, f * D + j * 512:f * D + (j + 1) * 512],
                                                   start=(f == 0), stop=(f == 7))
                            return ins
                        S.op("tensor", mm_fn,
                             reads=[f"{xiT_fin_res}{hh}" for hh in range(8)] + ["woT_sb"], writes=["p_wo"])
                        S.op("vector", lambda e, k=k: e.tensor_tensor(x1p[:], xrs[k % 2][:], p_wo[:], ALU.add),
                             reads=[f"xrs{k % 2}", "p_wo"], writes=["x1p"])
                        cA_t, cB_t, cAr, cBr = ln_cols(S, x1p, "x1p", 4, k)
                        S.op("vector", lambda e, cA_t=cA_t: e.tensor_scalar(x1c[:], x1p[:], cA_t[:, 0:1], None, ALU.add, ALU.bypass),
                             reads=["x1p", cAr], writes=["x1c"])
                        S.op("vector",
                             lambda e, k=k, cB_t=cB_t: e.tensor_scalar(x1nat[:, k * D:(k + 1) * D], x1c[:], cB_t[:, 0:1], None, ALU.mult, ALU.bypass),
                             reads=["x1c", cBr], writes=[f"x1_{k}"])

        if debug:
            S.op("sync", lambda e: e.dma_start(out=dbg["dbg_x1"], in_=x1nat[:]),
                 reads=[f"x1_{kk}" for kk in range(8)], dma=True)
        S.barrier()
        with ExitStack() as p5:
            x1T = sbuf(p5, "x1T", [128, 16 * NT], BF16)
            x1T_v = x1T.ap().rearrange("p (cc t) -> p cc t", cc=16)
            acc = sbuf(p5, "acc", [128, 8 * D], BF16)
            b0b_bc = sbuf(p5, "b0b_bc", [128, D], BF16)
            b1b_bc = sbuf(p5, "b1b_bc", [128, D], BF16)
            bd_bc = sbuf(p5, "bd_bc", [128, D], BF16)
            with nc.psum_tensor("p_tr3", [128, 2048], BF16) as p_tr3:
                for k in range(8):
                    def tr_fn(e, k=k):
                        for j in range(16):
                            ins = e.matmul(p_tr3[:, j * 128:(j + 1) * 128],
                                           x1nat[:, k * D + j * 128:k * D + (j + 1) * 128], eye_s[:],
                                           is_transpose=True, start=(j % 8 == 0), stop=(j % 8 == 7),
                                           skip_group_check=True)
                        return ins
                    S.op("tensor", tr_fn, reads=[f"x1_{k}", "eye_s"], writes=["p_tr3"])
                    S.op("vector",
                         lambda e, k=k: e.tensor_copy(x1T_v[:, :, k * 128:(k + 1) * 128], p_tr3[:]),
                         reads=["p_tr3"], writes=["x1T"])
            S.barrier()
            with nc.psum_tensor("p_bc2", [128, 2048], F32) as p_bc2:
                bcast(b0b_bc, "b0b_bc", bb_h[0], D, p_bc2, "p_bc2")
                bcast(b1b_bc, "b1b_bc", bb_h[1], D, p_bc2, "p_bc2")
            S.barrier()
            S.op("vector", lambda e: e.tensor_tensor(bd_bc[:], b0b_bc[:], b1b_bc[:], ALU.subtract),
                 reads=["b0b_bc", "b1b_bc"], writes=["bd_bc"])
            S.barrier()
            with ExitStack() as p5g:
                wgst = sbuf(p5g, "wgst", [128, 16 * 2], F32)
                wgbf = sbuf(p5g, "wgbf", [128, 16 * 2], BF16)
                S.op("sync", lambda e: e.dma_start(out=wgst[:], in_=wg_h),
                     writes=["wgst"], dma=True)
                S.op("vector", lambda e: e.tensor_copy(wgbf[:], wgst[:]), reads=["wgst"], writes=["wgbf"])
                with nc.psum_tensor("p_g", [128, 2], F32) as p_g:
                    for k in range(8):
                        def mm_fn(e, k=k):
                            for cc in range(16):
                                ins = e.matmul(p_g[:, 0:2], x1T_v[:, cc, k * 128:(k + 1) * 128],
                                               wgbf[:, cc * 2:(cc + 1) * 2], start=(cc == 0), stop=(cc == 15))
                            return ins
                        S.op("tensor", mm_fn, reads=["x1T", "wgbf"], writes=["p_g"])
                        S.op("vector", lambda e: e.tensor_copy(c2[:, 0:2], p_g[:, 0:2]),
                             reads=["p_g"], writes=["c2"])
                        S.op("vector",
                             lambda e, k=k: e.tensor_tensor(mask0_c[:, k:k + 1], c2[:, 0:1], c2[:, 1:2], ALU.is_ge),
                             reads=["c2"], writes=["mask0_c"])
            S.barrier()
            S.op("vector", lambda e: e.tensor_scalar(mask1_c[:, 0:8], mask0_c[:, 0:8], -1.0, 1.0, ALU.mult, ALU.add),
                 reads=["mask0_c"], writes=["mask1_c"])

            S.barrier()
            with ExitStack() as p5e:
                h_sup = sbuf(p5e, "h_sup", [128, 8 * NT], BF16)
                wast = [sbuf(p5e, f"wast{i}", [128, 2048], F32) for i in range(2)]
                wabf = sbuf(p5e, "wabf", [128, 2048], BF16)
                wbst = [sbuf(p5e, f"wbst{i}", [128, D], F32) for i in range(2)]
                wbbf = sbuf(p5e, "wbbf", [128, 8 * D], BF16)
                ba_sb = [sbuf(p5e, f"ba_sb{i}", [128, 32], F32) for i in range(2)]
                S.op("sync", lambda e: e.dma_start(out=ba_sb[0][:], in_=ba_h[0]), writes=["ba0"], dma=True)
                S.op("sync", lambda e: e.dma_start(out=ba_sb[1][:], in_=ba_h[1]), writes=["ba1"], dma=True)
                with (nc.psum_tensor("p_h", [128, 1024], F32) as p_h,
                      nc.psum_tensor("p_o", [128, 2048], F32) as p_o):
                    for e_idx in range(2):
                        mcol = mask0_c if e_idx == 0 else mask1_c
                        mres = "mask0_c" if e_idx == 0 else "mask1_c"
                        actf = AF.Gelu if e_idx == 0 else AF.Relu
                        basb = ba_sb[e_idx]
                        bares = f"ba{e_idx}"
                        for fs in range(4):
                            for fc8 in range(8):
                                fc = fs * 8 + fc8
                                wa_i = fc % 2
                                S.op("sync",
                                     lambda e, fc=fc, wa_i=wa_i, e_idx=e_idx: e.dma_start(
                                         out=wast[wa_i][:], in_=wa_h[e_idx][fc]),
                                     writes=[f"wast{wa_i}"], dma=True)
                                S.op("vector", lambda e, wa_i=wa_i: e.tensor_copy(wabf[:], wast[wa_i][:]),
                                     reads=[f"wast{wa_i}"], writes=["wabf"])
                                for th in range(2):
                                    def mm_fn(e, th=th):
                                        for cc in range(16):
                                            ins = e.matmul(p_h[:, th * 512:(th + 1) * 512],
                                                           wabf[:, cc * 128:(cc + 1) * 128],
                                                           x1T_v[:, cc, th * 512:(th + 1) * 512],
                                                           start=(cc == 0), stop=(cc == 15))
                                        return ins
                                    pr = f"p_h{th}"
                                    S.op("tensor", mm_fn, reads=["wabf", "x1T"], writes=[pr])
                                S.op("scalar",
                                     lambda e, fc=fc, fc8=fc8, actf=actf, basb=basb: e.activation(
                                         h_sup[:, fc8 * NT:(fc8 + 1) * NT], p_h[:], actf,
                                         bias=basb[:, fc:fc + 1], scale=1.0),
                                     reads=["p_h0", "p_h1", bares], writes=[f"hs{fc8}"])
                            for fc8 in range(8):
                                fc = fs * 8 + fc8
                                wb_i = fc8 % 2
                                S.op("sync",
                                     lambda e, fc=fc, wb_i=wb_i, e_idx=e_idx: e.dma_start(
                                         out=wbst[wb_i][:], in_=wb_h[e_idx][fc * 128:(fc + 1) * 128, :]),
                                     writes=[f"wbst{wb_i}"], dma=True)
                                S.op("vector",
                                     lambda e, fc8=fc8, wb_i=wb_i: e.tensor_copy(wbbf[:, fc8 * D:(fc8 + 1) * D], wbst[wb_i][:]),
                                     reads=[f"wbst{wb_i}"], writes=[f"wbbf{fc8}"])
                            for k in range(8):
                                def mm_fn(e, k=k):
                                    for fc8 in range(8):
                                        for j in range(4):
                                            ins = e.matmul(p_o[:, j * 512:(j + 1) * 512],
                                                           h_sup[:, fc8 * NT + k * 128:fc8 * NT + (k + 1) * 128],
                                                           wbbf[:, fc8 * D + j * 512:fc8 * D + (j + 1) * 512],
                                                           start=(fc8 == 0), stop=(fc8 == 7))
                                    return ins
                                S.op("tensor", mm_fn,
                                     reads=[f"hs{i}" for i in range(8)] + [f"wbbf{i}" for i in range(8)],
                                     writes=["p_o"])
                                if e_idx == 0 and fs == 0:
                                    S.op("vector",
                                         lambda e, k=k, mcol=mcol: e.tensor_scalar(
                                             acc[:, k * D:(k + 1) * D], p_o[:], mcol[:, k:k + 1], None, ALU.mult, ALU.bypass),
                                         reads=["p_o", mres], writes=[f"acc{k}"])
                                else:
                                    S.op("vector",
                                         lambda e, k=k, mcol=mcol: e.scalar_tensor_tensor(
                                             acc[:, k * D:(k + 1) * D], p_o[:], mcol[:, k:k + 1],
                                             acc[:, k * D:(k + 1) * D], ALU.mult, ALU.add),
                                         reads=["p_o", mres, f"acc{k}"], writes=[f"acc{k}"])
            if debug:
                S.op("sync", lambda e: e.dma_start(out=dbg["dbg_acc"], in_=acc[:]),
                     reads=[f"acc{kk}" for kk in range(8)], dma=True)
            S.barrier()
            with ExitStack() as p5f:
                x2p = sbuf(p5f, "x2p", [128, D], F32)
                bmix = sbuf(p5f, "bmix", [128, D], BF16)
                outt = sbuf(p5f, "outt", [128, D], F32)
                for k in range(8):
                    S.op("vector",
                         lambda e, k=k: e.scalar_tensor_tensor(bmix[:], bd_bc[:], mask0_c[:, k:k + 1], b1b_bc[:],
                                                               ALU.mult, ALU.add),
                         reads=["bd_bc", "mask0_c", "b1b_bc"], writes=["bmix"])
                    S.op("vector", lambda e, k=k: e.tensor_tensor(x2p[:], x1nat[:, k * D:(k + 1) * D], acc[:, k * D:(k + 1) * D], ALU.add),
                         reads=[f"x1_{k}", f"acc{k}"], writes=["x2p"])
                    S.op("vector", lambda e: e.tensor_tensor(x2p[:], x2p[:], bmix[:], ALU.add),
                         reads=["x2p", "bmix"], writes=["x2p"])
                    cA_t, cB_t, cAr, cBr = ln_cols(S, x2p, "x2p", 4, k)
                    S.op("vector", lambda e, cA_t=cA_t: e.tensor_scalar(x2p[:], x2p[:], cA_t[:, 0:1], None, ALU.add, ALU.bypass),
                         reads=["x2p", cAr], writes=["x2p"])
                    S.op("vector", lambda e, cB_t=cB_t: e.tensor_scalar(outt[:], x2p[:], cB_t[:, 0:1], None, ALU.mult, ALU.bypass),
                         reads=["x2p", cBr], writes=["outt"])
                    S.op("sync", lambda e, k=k: e.dma_start(out=out_h[k * 128:(k + 1) * 128, :], in_=outt[:]),
                         reads=["outt"], writes=[f"store{k}"], dma=True)

        with ExitStack() as semctx:
            sems = {}
            for e in ENGS:
                sems[e] = semctx.enter_context(nc.semaphore(f"s_{e}"))
            for i in range(NDMA):
                sems[f"dma{i}"] = semctx.enter_context(nc.semaphore(f"s_dma{i}"))
            block = semctx.enter_context(nc.Block())
            dma_final = [(f"dma{i}", 16 * S.dma_use[i]) for i in range(NDMA) if S.dma_use[i]]

            @block.sync
            def _(sync):
                S.emit("sync", sync, sems)
                for nm, val in dma_final:
                    sync.wait_ge(sems[nm], val)

            @block.vector
            def _(vector):
                S.emit("vector", vector, sems)

            @block.scalar
            def _(scalar):
                S.emit("scalar", scalar, sems)

            @block.tensor
            def _(tensor):
                S.emit("tensor", tensor, sems)

            if S.q["gpsimd"]:
                @block.gpsimd
                def _(gpsimd):
                    S.emit("gpsimd", gpsimd, sems)
    return nc


_CACHE = {}


def kernel(**inputs):
    f32 = np.float32
    x = np.asarray(inputs["L0_attn_hidden"], f32)
    mem = np.ascontiguousarray(np.asarray(inputs["L0_attn_memory"], f32))
    wq = np.asarray(inputs["L0_attn_Wq"], f32)
    wk = np.asarray(inputs["L0_attn_Wk"], f32)
    wv = np.asarray(inputs["L0_attn_Wv"], f32)
    wo = np.asarray(inputs["L0_attn_Wo"], f32)
    lb = np.ascontiguousarray(np.asarray(inputs["L0_attn_log_beta"], f32).reshape(1, NH))
    gs = np.ascontiguousarray(np.asarray(inputs["L0_attn_g_state"], f32).reshape(1, D))
    bs = np.ascontiguousarray(np.asarray(inputs["L0_attn_b_state"], f32).reshape(1, D))
    gm = np.ascontiguousarray(np.asarray(inputs["L0_attn_g_mem"], f32).reshape(1, MD))
    bm = np.ascontiguousarray(np.asarray(inputs["L0_attn_b_mem"], f32).reshape(1, MD))
    wg = np.asarray(inputs["L0_ffn_Wg"], f32)
    w0a = np.asarray(inputs["L0_ffn_W0a"], f32)
    w1a = np.asarray(inputs["L0_ffn_W1a"], f32)
    w0b = np.asarray(inputs["L0_ffn_W0b"], f32)
    w1b = np.asarray(inputs["L0_ffn_W1b"], f32)
    b0a = np.asarray(inputs["L0_ffn_b0a"], f32)
    b1a = np.asarray(inputs["L0_ffn_b1a"], f32)
    b0b = np.ascontiguousarray(np.asarray(inputs["L0_ffn_b0b"], f32).reshape(1, D))
    b1b = np.ascontiguousarray(np.asarray(inputs["L0_ffn_b1b"], f32).reshape(1, D))

    wqT = np.ascontiguousarray(wq.T.reshape(16, 128, 8, 128).transpose(2, 1, 0, 3).reshape(8, 128, 2048))
    wkT = np.ascontiguousarray(wk.T)
    wvT = np.ascontiguousarray(wv.T)
    woT = np.ascontiguousarray(wo.T)
    wgT = np.ascontiguousarray(wg.T.reshape(16, 128, 2).transpose(1, 0, 2).reshape(128, 32))
    w0aT = np.ascontiguousarray(w0a.T.reshape(16, 128, 32, 128).transpose(2, 1, 0, 3).reshape(32, 128, 2048))
    w1aT = np.ascontiguousarray(w1a.T.reshape(16, 128, 32, 128).transpose(2, 1, 0, 3).reshape(32, 128, 2048))
    w0bT = np.ascontiguousarray(w0b.T)
    w1bT = np.ascontiguousarray(w1b.T)
    b0a_c = np.ascontiguousarray(b0a.reshape(32, 128).T)
    b1a_c = np.ascontiguousarray(b1a.reshape(32, 128).T)
    ones1 = np.ones((1, 128), f32)
    eye = np.eye(128).astype(bfnp)

    if "nc" not in _CACHE:
        _CACHE["nc"] = build_program()
    nc = _CACHE["nc"]

    in_maps = []
    for c in range(8):
        b, sh = c // 2, c % 2
        xs = np.ascontiguousarray(x[b, sh * NT:(sh + 1) * NT, :])
        in_maps.append({
            "x": xs, "mem": mem, "wqT": wqT, "wkT": wkT, "wvT": wvT, "woT": woT,
            "wgT": wgT, "w0aT": w0aT, "w1aT": w1aT, "w0bT": w0bT, "w1bT": w1bT,
            "log_beta": lb, "g_state": gs, "b_state": bs, "g_mem": gm, "b_mem": bm,
            "b0a": b0a_c, "b1a": b1a_c, "b0b": b0b, "b1b": b1b,
            "ones1": ones1, "eye": eye,
        })
    res = run_bass_kernel_spmd(nc, in_maps, core_ids=list(range(8)), trace_cores=list(range(8)))
    out = np.empty((4, 2048, D), np.float32)
    for c in range(8):
        b, sh = c // 2, c % 2
        out[b, sh * NT:(sh + 1) * NT, :] = res.results[c]["out"]
    return out


# version 20
# speedup vs baseline: 1.0307x; 1.0307x over previous
from contextlib import ExitStack

import numpy as np
import ml_dtypes

import concourse.bass as bass
import concourse.mybir as mybir
from concourse.bass_utils import run_bass_kernel_spmd

F32 = mybir.dt.float32
BF16 = mybir.dt.bfloat16
AF = mybir.ActivationFunctionType
ALU = mybir.AluOpType
bfnp = ml_dtypes.bfloat16

T_STEPS = (5, 6, 7)
USE_FIN = True

NT = 1024
D = 2048
NH = 8
NMEM = 2048
MD = 1024
FF = 4096
EPS = 1e-5
LN_HALF = float(np.log(0.5))

ENGS = ("sync", "vector", "scalar", "tensor", "gpsimd")
NDMA = 8


class KB:

    def __init__(self):
        self.q = {e: [] for e in ENGS}
        self.cnt = {e: 0 for e in ENGS}
        self.dma_i = 0
        self.dma_use = [0] * NDMA
        self.res = {}
        self.barrier_deps = []

    def barrier(self):
        self.barrier_deps = [(e, self.cnt[e]) for e in ENGS if self.cnt[e] > 0]
        self.barrier_deps += [(f"dma{i}", 16 * self.dma_use[i]) for i in range(NDMA) if self.dma_use[i] > 0]

    def _res(self, name):
        return self.res.setdefault(name, {"w": None, "r": []})

    def op(self, eng, fn, reads=(), writes=(), deps=(), dma=False):
        dep = list(deps) + self.barrier_deps
        pre = []
        if dma:
            s = self.dma_i % NDMA
            self.dma_i += 1
            if self.dma_use[s] > 0:
                pre.append((f"dma{s}", 16 * self.dma_use[s]))
            self.dma_use[s] += 1
            tok = (f"dma{s}", 16 * self.dma_use[s])
            sem_tag = f"dma{s}"
        else:
            tok = (eng, self.cnt[eng] + 1)
            self.cnt[eng] = tok[1]
            sem_tag = eng
        for nm in reads:
            r = self._res(nm)
            if r["w"] is not None:
                dep.append(r["w"])
        for nm in writes:
            r = self._res(nm)
            if r["w"] is not None:
                dep.append(r["w"])
            dep.extend(r["r"])
        for nm in reads:
            self._res(nm)["r"].append(tok)
        for nm in writes:
            r = self._res(nm)
            r["w"] = tok
            r["r"] = []
        self.q[eng].append((fn, tuple(dep), tuple(pre), sem_tag, tok[1]))
        return tok

    SELF_SYNC = True

    def emit(self, eng_name, eng, sems):
        waited = {}
        for fn, deps, pre, sem_tag, tokval in self.q[eng_name]:
            need = {}
            for (de, dv) in list(deps) + list(pre):
                if de == eng_name and not self.SELF_SYNC:
                    continue
                if dv > waited.get(de, 0):
                    need[de] = max(need.get(de, 0), dv)
            for de in sorted(need):
                eng.wait_ge(sems[de], need[de])
                waited[de] = need[de]
            ins = fn(eng)
            if sem_tag.startswith("dma"):
                ins.then_inc(sems[sem_tag], 16)
            elif hasattr(eng, "engine_nop"):
                eng.engine_nop().then_inc(sems[sem_tag], 1)
            else:
                ins.then_inc(sems[sem_tag], 1)


def build_program(detect_races=True, debug=False):
    nc = bass.Bass("TRN2", target_bir_lowering=False, detect_race_conditions=detect_races)

    def dram(name, shape, dt=F32, kind="ExternalInput"):
        return nc.dram_tensor(name, shape, dt, kind=kind).ap()

    x_h = dram("x", [NT, D])
    mem_h = dram("mem", [NMEM, MD])
    wq_h = dram("wqT", [8, 128, 16 * 128])
    wk_h = dram("wkT", [MD, MD])
    wv_h = dram("wvT", [MD, MD])
    wo_h = dram("woT", [NH * 128, D])
    wg_h = dram("wgT", [128, 32])
    wa_h = [dram("w0aT", [32, 128, 16 * 128]), dram("w1aT", [32, 128, 16 * 128])]
    wb_h = [dram("w0bT", [FF, D]), dram("w1bT", [FF, D])]
    lb_h = dram("log_beta", [1, NH])
    gs_h = dram("g_state", [1, D])
    bs_h = dram("b_state", [1, D])
    gm_h = dram("g_mem", [1, MD])
    bm_h = dram("b_mem", [1, MD])
    ba_h = [dram("b0a", [128, 32]), dram("b1a", [128, 32])]
    bb_h = [dram("b0b", [1, D]), dram("b1b", [1, D])]
    ones_h = dram("ones1", [1, 128])
    eye_h = dram("eye", [128, 128], BF16)
    out_h = dram("out", [NT, D], kind="ExternalOutput")
    dbg = {}
    if debug:
        for nm, sh, dt in (("dbg_KT", [128, NH * NMEM], BF16), ("dbg_Vb", [128, 16 * MD], BF16),
                           ("dbg_xi0", [128, NH * NT], BF16), ("dbg_xi1", [128, NH * NT], BF16),
                           ("dbg_xi2", [128, NH * NT], BF16), ("dbg_xi3", [128, NH * NT], BF16),
                           ("dbg_x1", [128, 8 * D], BF16), ("dbg_acc", [128, 8 * D], BF16),
                           ("dbg_mnT", [128, 8 * NMEM], BF16), ("dbg_ag", [128, 2], F32),
                           ("dbg_colB", [128, 8], F32), ("dbg_msc", [128, MD], F32),
                           ("dbg_mnbf", [128, MD], BF16)):
            dbg[nm] = dram(nm, sh, dt, kind="ExternalOutput")

    S = KB()

    def mk_ln_cols(lnbufs):
        def ln_cols(S, src_ap, src_res, nchunks, par):
            p = par % 2
            bnst, colA, colB = lnbufs[p]
            bn, cA, cB = f"bnst{p}", f"colA{p}", f"colB{p}"
            def st_fn(e):
                for i in range(nchunks):
                    ins = e.bn_stats(bnst[:, i * 6:(i + 1) * 6], src_ap[:, i * 512:(i + 1) * 512])
                return ins
            S.op("vector", st_fn, reads=[src_res], writes=[bn])
            S.op("vector", lambda e: e.bn_aggr(colA[:, 6:8], bnst[:, 0:6 * nchunks]),
                 reads=[bn], writes=[cA])
            S.op("vector", lambda e: e.tensor_scalar(colA[:, 0:1], colA[:, 6:7], -1.0, None, ALU.mult, ALU.bypass),
                 reads=[cA], writes=[cA])
            S.op("vector", lambda e: e.tensor_scalar(colB[:, 0:1], colA[:, 7:8], EPS, None, ALU.add, ALU.bypass),
                 reads=[cA], writes=[cB])
            S.op("scalar", lambda e: e.activation(colB[:, 0:1], colB[:, 0:1], AF.Ln),
                 reads=[cB], writes=[cB])
            S.op("scalar", lambda e: e.activation(colB[:, 0:1], colB[:, 0:1], AF.Exp, scale=-0.5),
                 reads=[cB], writes=[cB])
            return colA, colB, cA, cB
        return ln_cols

    with ExitStack() as ctx:
        def sbuf(stack, name, shape, dt):
            return stack.enter_context(nc.sbuf_tensor(name, shape, dt))

        ones_s = sbuf(ctx, "ones_s", [1, 128], F32)
        eye_s = sbuf(ctx, "eye_s", [128, 128], BF16)
        lb_s = sbuf(ctx, "lb_s", [1, NH], F32)
        ln2_s = sbuf(ctx, "ln2_s", [1, 1], F32)
        beta_bc = sbuf(ctx, "beta_bc", [128, NH], F32)
        vec_s = sbuf(ctx, "vec_s", [1, D], F32)
        lnbufs = [(sbuf(ctx, f"bnst{i}", [128, 24], F32),
                   sbuf(ctx, f"colA{i}", [128, 8], F32),
                   sbuf(ctx, f"colB{i}", [128, 8], F32)) for i in range(2)]
        ln_cols = mk_ln_cols(lnbufs)
        ymax = sbuf(ctx, "ymax", [128, 8], F32)
        ntau_a = sbuf(ctx, "ntau_a", [128, 8], F32)
        ntau_b = sbuf(ctx, "ntau_b", [128, 8], F32)
        ntau_f = sbuf(ctx, "ntau_f", [128, 8], F32)
        ntau_g = sbuf(ctx, "ntau_g", [128, 8], F32)
        src_c = sbuf(ctx, "src_c", [128, 8], F32)
        sr2_c = sbuf(ctx, "sr2_c", [128, 8], F32)
        kc_c = sbuf(ctx, "kc_c", [128, 8], F32)
        c1 = sbuf(ctx, "c1", [128, 8], F32)
        c2 = sbuf(ctx, "c2", [128, 8], F32)
        c3 = sbuf(ctx, "c3", [128, 8], F32)
        mask0_c = sbuf(ctx, "mask0_c", [128, 8], F32)
        mask1_c = sbuf(ctx, "mask1_c", [128, 8], F32)
        x1nat = sbuf(ctx, "x1nat", [128, 8 * D], BF16)

        S.op("sync", lambda e: e.dma_start(out=ones_s[:], in_=ones_h), writes=["ones_s"], dma=True)
        S.op("sync", lambda e: e.dma_start(out=eye_s[:], in_=eye_h), writes=["eye_s"], dma=True)
        S.op("sync", lambda e: e.dma_start(out=lb_s[:], in_=lb_h), writes=["lb_s"], dma=True)
        S.op("vector", lambda e: e.memset(ln2_s[:], LN_HALF), writes=["ln2_s"])
        S.op("scalar", lambda e: e.activation(lb_s[:], lb_s[:], AF.Exp, bias=ln2_s[0:1, 0:1], scale=1.0),
             reads=["ln2_s"], writes=["lb_s"])
        with nc.psum_tensor("p_b0", [128, 8], F32) as p_b0:
            S.op("tensor", lambda e: e.matmul(p_b0[:, 0:8], ones_s[0:1, :], lb_s[0:1, :], start=True, stop=True),
                 reads=["ones_s", "lb_s"], writes=["p_b0"])
            S.op("vector", lambda e: e.tensor_copy(beta_bc[:], p_b0[:, 0:8]),
                 reads=["p_b0"], writes=["beta_bc"])

        S.barrier()
        def bcast(dst, dres, row_h, width, psum, pres, stack=None):
            S.op("sync", lambda e: e.dma_start(out=vec_s[0:1, 0:width], in_=row_h),
                 writes=["vec_s"], dma=True)
            for j0 in range(0, width, 512):
                S.op("tensor",
                     lambda e, j0=j0: e.matmul(psum[:, j0 % 2048:j0 % 2048 + 512], ones_s[0:1, :],
                                               vec_s[0:1, j0:j0 + 512], start=True, stop=True),
                     reads=["ones_s", "vec_s"], writes=[f"{pres}{j0 % 2048}"])
                S.op("vector", lambda e, j0=j0: e.tensor_copy(dst[:, j0:j0 + 512], psum[:, j0 % 2048:j0 % 2048 + 512]),
                     reads=[f"{pres}{j0 % 2048}"], writes=[dres])

        with ExitStack() as attn:
            KT = sbuf(attn, "KT", [128, NH * NMEM], BF16)
            Vb = sbuf(attn, "Vb", [128, 16 * MD], BF16)
            xiT_a = sbuf(attn, "xiT_a", [128, NH * NT], BF16)
            xiT_b = sbuf(attn, "xiT_b", [128, NH * NT], BF16)

            with ExitStack() as p1:
                mnT = sbuf(p1, "mnT", [128, 8 * NMEM], BF16)
                mnT_v = mnT.ap().rearrange("p (mc n) -> p mc n", mc=8)
                gm_bc = sbuf(p1, "gm_bc", [128, MD], BF16)
                bm_bc = sbuf(p1, "bm_bc", [128, MD], BF16)
                with ExitStack() as p1a:
                    mst = sbuf(p1a, "mst", [128, 4 * MD], F32)
                    msc2 = [sbuf(p1a, f"msc{i}", [128, MD], F32) for i in range(2)]
                    mnbf = [sbuf(p1a, f"mnbf{i}", [128, MD], BF16) for i in range(2)]
                    with nc.psum_tensor("p_m", [128, 1024], F32) as p_m:
                        bcast(gm_bc, "gm_bc", gm_h, MD, p_m, "p_m")
                        bcast(bm_bc, "bm_bc", bm_h, MD, p_m, "p_m")
                    S.barrier()
                    with nc.psum_tensor("p_tr1", [128, 1024], BF16) as p_tr1:
                        for half in range(4):
                            mres = "mst"
                            S.op("sync",
                                 lambda e, half=half: e.dma_start(
                                     out=mst.ap().rearrange("p (g m) -> p g m", g=4),
                                     in_=mem_h.rearrange("(g p) m -> p g m", p=128)[:, half * 4:(half + 1) * 4, :]),
                                 writes=[mres], dma=True)
                            for it in range(4):
                                nt_i = half * 4 + it
                                mtile = mst[:, it * MD:(it + 1) * MD]
                                cA_t, cB_t, cAr, cBr = ln_cols(S, mtile, mres, 2, nt_i)
                                msc = msc2[nt_i % 2]
                                mscr = f"msc{nt_i % 2}"
                                S.op("vector", lambda e, mtile=mtile, cA_t=cA_t, msc=msc: e.tensor_scalar(msc[:], mtile, cA_t[:, 0:1], None, ALU.add, ALU.bypass),
                                     reads=[mres, cAr], writes=[mscr])
                                S.op("vector", lambda e, cB_t=cB_t, msc=msc: e.scalar_tensor_tensor(msc[:], msc[:], cB_t[:, 0:1], gm_bc[:], ALU.mult, ALU.mult),
                                     reads=[mscr, cBr, "gm_bc"], writes=[mscr])
                                mb = mnbf[nt_i % 2]
                                mbres = f"mnbf{nt_i % 2}"
                                S.op("vector", lambda e, mb=mb, msc=msc: e.tensor_tensor(mb[:], msc[:], bm_bc[:], ALU.add),
                                     reads=[mscr, "bm_bc"], writes=[mbres])
                                def tr_fn(e, mb=mb):
                                    for j in range(8):
                                        ins = e.matmul(p_tr1[:, j * 128:(j + 1) * 128], mb[:, j * 128:(j + 1) * 128], eye_s[:],
                                                       is_transpose=True, start=(j == 0), stop=(j == 7), skip_group_check=True)
                                    return ins
                                S.op("tensor", tr_fn, reads=[mbres, "eye_s"], writes=["p_tr1"])
                                S.op("vector",
                                     lambda e, nt_i=nt_i: e.tensor_copy(
                                         mnT_v[:, :, nt_i * 128:(nt_i + 1) * 128], p_tr1[:]),
                                     reads=["p_tr1"], writes=["mnT"])
                                if debug and nt_i == 15:
                                    S.op("sync", lambda e: e.dma_start(out=dbg["dbg_ag"], in_=bnag[:]), reads=["bnag"], dma=True)
                                    S.op("sync", lambda e: e.dma_start(out=dbg["dbg_colB"], in_=colB[:]), reads=["colB"], dma=True)
                                    S.op("sync", lambda e: e.dma_start(out=dbg["dbg_msc"], in_=msc[:]), reads=["msc"], dma=True)
                                    S.op("sync", lambda e, mb=mb: e.dma_start(out=dbg["dbg_mnbf"], in_=mb[:]), reads=[mbres], dma=True)
                if debug:
                    S.op("sync", lambda e: e.dma_start(out=dbg["dbg_mnT"], in_=mnT[:]), reads=["mnT"], dma=True)
                S.barrier()
                for wgt_h, name in ((wk_h, "K"), (wv_h, "V")):
                    S.barrier()
                    with ExitStack() as pw:
                        wst = sbuf(pw, "wst_" + name, [128, 4 * MD], F32)
                        wbf = sbuf(pw, "wbf_" + name, [128, 8 * MD], BF16)
                        for hf in range(2):
                            S.op("sync",
                                 lambda e, wgt_h=wgt_h, hf=hf: e.dma_start(
                                     out=wst.ap().rearrange("p (mc f) -> p mc f", mc=4),
                                     in_=wgt_h.rearrange("(mc p) f -> p mc f", p=128)[:, hf * 4:(hf + 1) * 4, :]),
                                 writes=["wst"], dma=True)
                            for i in range(4):
                                S.op("vector",
                                     lambda e, i=i, hf=hf: e.tensor_copy(
                                         wbf[:, (hf * 4 + i) * MD:(hf * 4 + i + 1) * MD], wst[:, i * MD:(i + 1) * MD]),
                                     reads=["wst"], writes=["wbf"])
                        with nc.psum_tensor("p_kv_" + name, [128, 1024], F32) as p_kv:
                            if name == "K":
                                for f in range(8):
                                    for ncol in range(4):
                                        def mm_fn(e, f=f, ncol=ncol):
                                            for mc in range(8):
                                                ins = e.matmul(p_kv[:, (ncol % 2) * 512:(ncol % 2) * 512 + 512],
                                                               wbf[:, mc * MD + f * 128:mc * MD + (f + 1) * 128],
                                                               mnT_v[:, mc, ncol * 512:(ncol + 1) * 512],
                                                               start=(mc == 0), stop=(mc == 7))
                                            return ins
                                        pr = f"p_kv{ncol % 2}"
                                        S.op("tensor", mm_fn, reads=["wbf", "mnT"], writes=[pr])
                                        S.op("vector",
                                             lambda e, f=f, ncol=ncol: e.tensor_copy(
                                                 KT[:, f * NMEM + ncol * 512:f * NMEM + (ncol + 1) * 512],
                                                 p_kv[:, (ncol % 2) * 512:(ncol % 2) * 512 + 512]),
                                             reads=[pr], writes=["KT"])
                            else:
                                for ntile in range(16):
                                    for fcol in range(2):
                                        def mm_fn(e, ntile=ntile, fcol=fcol):
                                            for mc in range(8):
                                                ins = e.matmul(p_kv[:, fcol * 512:(fcol + 1) * 512],
                                                               mnT_v[:, mc, ntile * 128:(ntile + 1) * 128],
                                                               wbf[:, mc * MD + fcol * 512:mc * MD + (fcol + 1) * 512],
                                                               start=(mc == 0), stop=(mc == 7))
                                            return ins
                                        pr = f"p_kv{fcol}"
                                        S.op("tensor", mm_fn, reads=["wbf", "mnT"], writes=[pr])
                                        S.op("vector",
                                             lambda e, ntile=ntile, fcol=fcol: e.tensor_copy(
                                                 Vb[:, ntile * MD + fcol * 512:ntile * MD + (fcol + 1) * 512],
                                                 p_kv[:, fcol * 512:(fcol + 1) * 512]),
                                             reads=[pr], writes=["Vb"])

            if debug:
                S.op("sync", lambda e: e.dma_start(out=dbg["dbg_KT"], in_=KT[:]), reads=["KT"], dma=True)
                S.op("sync", lambda e: e.dma_start(out=dbg["dbg_Vb"], in_=Vb[:]), reads=["Vb"], dma=True)
            S.barrier()
            with ExitStack() as p2:
                gs_bc = sbuf(p2, "gs_bc", [128, D], BF16)
                bs_bc = sbuf(p2, "bs_bc", [128, D], BF16)
                lnxTh = sbuf(p2, "lnxTh", [128, 16 * 512], BF16)
                lnxT_v = lnxTh.ap().rearrange("p (cc t) -> p cc t", cc=16)
                xst = [sbuf(p2, f"xst{i}", [128, D], F32) for i in range(2)]
                lnbf = [sbuf(p2, f"lnbf{i}", [128, D], BF16) for i in range(2)]
                wqst = sbuf(p2, "wqst", [128, 16 * 128], F32)
                wqbf = sbuf(p2, "wqbf", [128, 16 * 128], BF16)
                with nc.psum_tensor("p_bb", [128, 2048], F32) as p_bb:
                    bcast(gs_bc, "gs_bc", gs_h, D, p_bb, "p_bb")
                    bcast(bs_bc, "bs_bc", bs_h, D, p_bb, "p_bb")
                S.barrier()
                with (nc.psum_tensor("p_tr2", [128, 2048], BF16) as p_tr2,
                      nc.psum_tensor("p_q", [128, 1024], F32) as p_q):
                    for half in range(2):
                        for kk in range(4):
                            k = half * 4 + kk
                            xs = xst[kk % 2]
                            xres = f"xst{kk % 2}"
                            S.op("sync", lambda e, xs=xs, k=k: e.dma_start(out=xs[:], in_=x_h[k * 128:(k + 1) * 128, :]),
                                 writes=[xres], dma=True)
                            cA_t, cB_t, cAr, cBr = ln_cols(S, xs, xres, 4, k)
                            S.op("vector", lambda e, xs=xs, cA_t=cA_t: e.tensor_scalar(xs[:], xs[:], cA_t[:, 0:1], None, ALU.add, ALU.bypass),
                                 reads=[xres, cAr], writes=[xres])
                            S.op("vector", lambda e, xs=xs, cB_t=cB_t: e.scalar_tensor_tensor(xs[:], xs[:], cB_t[:, 0:1], gs_bc[:], ALU.mult, ALU.mult),
                                 reads=[xres, cBr, "gs_bc"], writes=[xres])
                            lb2 = lnbf[kk % 2]
                            lbres = f"lnbf{kk % 2}"
                            S.op("vector", lambda e, lb2=lb2, xs=xs: e.tensor_tensor(lb2[:], xs[:], bs_bc[:], ALU.add),
                                 reads=[xres, "bs_bc"], writes=[lbres])
                            def tr_fn(e, lb2=lb2):
                                for j in range(16):
                                    ins = e.matmul(p_tr2[:, j * 128:(j + 1) * 128], lb2[:, j * 128:(j + 1) * 128], eye_s[:],
                                                   is_transpose=True, start=(j % 8 == 0), stop=(j % 8 == 7), skip_group_check=True)
                                return ins
                            S.op("tensor", tr_fn, reads=[lbres, "eye_s"], writes=["p_tr2"])
                            S.op("vector",
                                 lambda e, kk=kk: e.tensor_copy(lnxT_v[:, :, kk * 128:(kk + 1) * 128], p_tr2[:]),
                                 reads=["p_tr2"], writes=["lnxTh"])
                        for h in range(8):
                            S.op("sync", lambda e, h=h: e.dma_start(out=wqst[:], in_=wq_h[h]),
                                 writes=["wqst"], dma=True)
                            for i in range(4):
                                S.op("vector", lambda e, i=i: e.tensor_copy(wqbf[:, i * 512:(i + 1) * 512], wqst[:, i * 512:(i + 1) * 512]),
                                     reads=["wqst"], writes=["wqbf"])
                            def mm_fn(e):
                                for cc in range(16):
                                    ins = e.matmul(p_q[:, 0:512],
                                                   wqbf[:, cc * 128:(cc + 1) * 128],
                                                   lnxT_v[:, cc, 0:512],
                                                   start=(cc == 0), stop=(cc == 15))
                                return ins
                            S.op("tensor", mm_fn, reads=["wqbf", "lnxTh"], writes=["p_q"])
                            S.op("vector",
                                 lambda e, h=h, half=half: e.tensor_copy(
                                     xiT_a[:, h * NT + half * 512:h * NT + half * 512 + 512], p_q[:, 0:512]),
                                 reads=["p_q"], writes=[f"xiT_a_h{h}"])

            S.barrier()
            with ExitStack() as p3:
                y_buf = sbuf(p3, "y_buf", [128, 8 * NMEM], BF16)
                r_buf = [sbuf(p3, f"r_buf{i}", [128, NMEM], BF16) for i in range(2)]
                sdump = sbuf(p3, "sdump", [128, NMEM], BF16)
                w_fin = [sbuf(p3, f"w_fin{i}", [128, NMEM], BF16) for i in range(2)]
                wT_buf = sbuf(p3, "wT_buf", [128, 16 * 512], BF16)
                wT_v = wT_buf.ap().rearrange("p (nc t) -> p nc t", nc=16)
                with (nc.psum_tensor("p_sc", [128, 2048], F32) as p_sc,
                      nc.psum_tensor("p_wt", [128, 2048], BF16) as p_wt,
                      nc.psum_tensor("p_ap", [128, 1024], F32) as p_ap):
                    def scores_pass0(step, h, k, xiT_src, src_res):
                        def mm_fn(e, h=h, k=k, xiT_src=xiT_src):
                            for j in range(4):
                                ins = e.matmul(p_sc[:, j * 512:(j + 1) * 512],
                                               xiT_src[:, h * NT + k * 128:h * NT + (k + 1) * 128],
                                               KT[:, h * NMEM + j * 512:h * NMEM + (j + 1) * 512],
                                               start=True, stop=True)
                            return ins
                        S.op("tensor", mm_fn, reads=[f"{src_res}{h}", "KT"], writes=["p_sc"])
                        S.op("vector",
                             lambda e, h=h, k=k: e.tensor_scalar(
                                 y_buf[:, k * NMEM:(k + 1) * NMEM], p_sc[:],
                                 beta_bc[:, h:h + 1], None, ALU.mult, ALU.max,
                                 accum_out=ymax[:, k:k + 1]),
                             reads=["p_sc", "beta_bc"], writes=[f"y{k}", f"ymax{k // 4}"])

                    def final_relu(h, k, fin, fin_res):
                        rb = r_buf[k % 2]
                        S.op("vector",
                             lambda e, k=k, rb=rb, fin=fin: e.tensor_scalar(
                                 rb[:], y_buf[:, k * NMEM:(k + 1) * NMEM],
                                 fin[:, k:k + 1], 0.0, ALU.add, ALU.max),
                             reads=[f"y{k}", fin_res], writes=[f"rb{k % 2}"])

                    def final_sq_tr(h, k):
                        rb = r_buf[k % 2]
                        wf = w_fin[k % 2]
                        S.op("scalar",
                             lambda e, rb=rb, wf=wf: e.activation(wf[:], rb[:], AF.Square),
                             reads=[f"rb{k % 2}"], writes=[f"wf{k % 2}"])
                        def tr_fn(e, wf=wf):
                            for j in range(16):
                                ins = e.matmul(p_wt[:, j * 128:(j + 1) * 128], wf[:, j * 128:(j + 1) * 128], eye_s[:],
                                               is_transpose=True, start=(j % 8 == 0), stop=(j % 8 == 7),
                                               skip_group_check=True)
                            return ins
                        S.op("tensor", tr_fn, reads=[f"wf{k % 2}", "eye_s"], writes=["p_wt"])

                    def final_evac(h, k):
                        S.op("scalar",
                             lambda e, k=k: e.activation(
                                 wT_v[:, :, (k % 4) * 128:(k % 4 + 1) * 128], p_wt[:], AF.Copy),
                             reads=["p_wt"], writes=["wT"])

                    def apply_half(h, half, xiT_dst, dst_res):
                        def mm_fn(e, h=h):
                            for ncv in range(16):
                                ins = e.matmul(p_ap[:, (half % 2) * 512:(half % 2) * 512 + 512],
                                               Vb[:, ncv * MD + h * 128:ncv * MD + (h + 1) * 128],
                                               wT_v[:, ncv, 0:512],
                                               start=(ncv == 0), stop=(ncv == 15))
                            return ins
                        S.op("tensor", mm_fn, reads=["Vb", "wT"], writes=[f"p_ap{half % 2}"])
                        S.op("vector",
                             lambda e, h=h, half=half, xiT_dst=xiT_dst: e.tensor_copy(
                                 xiT_dst[:, h * NT + half * 512:h * NT + (half + 1) * 512],
                                 p_ap[:, (half % 2) * 512:(half % 2) * 512 + 512]),
                             reads=[f"p_ap{half % 2}"], writes=[f"{dst_res}{h}"])

                    for step in range(3):
                        T = T_STEPS[step]
                        if step % 2 == 0:
                            xiT_src, src_res, xiT_dst, dst_res = xiT_a, "xiT_a_h", xiT_b, "xiT_b_h"
                        else:
                            xiT_src, src_res, xiT_dst, dst_res = xiT_b, "xiT_b_h", xiT_a, "xiT_a_h"
                        for h in range(8):
                            if h == 0:
                                for k in range(8):
                                    scores_pass0(step, h, k, xiT_src, src_res)
                            cur = {0: (ntau_a, "ntauA_a"), 1: (ntau_f, "ntauB_a")}
                            nxt = {0: (ntau_b, "ntauA_b"), 1: (ntau_g, "ntauB_b")}
                            for half in range(2):
                                lo = half * 4
                                S.op("vector",
                                     lambda e, half=half, lo=lo, t=cur[half][0]: e.tensor_scalar(
                                         t[:, lo:lo + 4], ymax[:, lo:lo + 4], -1.0, 1.0, ALU.mult, ALU.add),
                                     reads=[f"ymax{half}"], writes=[cur[half][1]])
                            for j in range(T):
                                for half in range(2):
                                    lo = half * 4
                                    ct, cres = cur[half]
                                    for kk in range(4):
                                        k = lo + kk
                                        rb = r_buf[k % 2]
                                        rres = f"rb{k % 2}"
                                        S.op("scalar",
                                             lambda e, k=k, rb=rb, ct=ct: e.activation(
                                                 rb[:], y_buf[:, k * NMEM:(k + 1) * NMEM], AF.Relu,
                                                 bias=ct[:, k:k + 1], scale=1.0, accum_out=src_c[:, k:k + 1]),
                                             reads=[f"y{k}", cres], writes=[rres, f"srcc{half}"])
                                        if k == 7:
                                            S.op("scalar",
                                                 lambda e, k=k, rb=rb: e.activation(
                                                     sdump[:], rb[:], AF.Square, accum_out=sr2_c[:, k:k + 1]),
                                                 reads=[rres], writes=["sdump", f"sr2c{half}"])
                                        else:
                                            S.op("vector",
                                                 lambda e, k=k, rb=rb, ct=ct: e.scalar_tensor_tensor(
                                                     sdump[:], y_buf[:, k * NMEM:(k + 1) * NMEM],
                                                     ct[:, k:k + 1], rb[:], ALU.add, ALU.mult,
                                                     accum_out=sr2_c[:, k:k + 1]),
                                                 reads=[f"y{k}", cres, rres], writes=["sdump", f"sr2c{half}"])
                                    nt, nres = nxt[half]
                                    S.op("vector", lambda e, lo=lo: e.tensor_scalar(c1[:, lo:lo + 4], src_c[:, lo:lo + 4], 1e-30, None, ALU.max, ALU.bypass),
                                         reads=[f"srcc{half}"], writes=[f"c1{half}"])
                                    S.op("vector", lambda e, lo=lo: e.reciprocal(c1[:, lo:lo + 4], c1[:, lo:lo + 4]),
                                         reads=[f"c1{half}"], writes=[f"c1{half}"])
                                    S.op("vector", lambda e, lo=lo: e.scalar_tensor_tensor(c2[:, lo:lo + 4], sr2_c[:, lo:lo + 4], -1.0, c1[:, lo:lo + 4], ALU.add, ALU.mult),
                                         reads=[f"sr2c{half}", f"c1{half}"], writes=[f"c2{half}"])
                                    S.op("vector",
                                         lambda e, lo=lo, ct=ct, nt=nt: e.scalar_tensor_tensor(
                                             nt[:, lo:lo + 4], c2[:, lo:lo + 4], -0.5, ct[:, lo:lo + 4], ALU.mult, ALU.add),
                                         reads=[f"c2{half}", cres], writes=[nres])
                                    cur[half], nxt[half] = nxt[half], cur[half]
                            nxt_head = h + 1 if h < 7 else None
                            final_relu(h, 0, cur[0][0], cur[0][1])
                            for k in range(8):
                                final_sq_tr(h, k)
                                if k < 7:
                                    kn = k + 1
                                    final_relu(h, kn, cur[kn // 4][0], cur[kn // 4][1])
                                final_evac(h, k)
                                if nxt_head is not None:
                                    scores_pass0(step, nxt_head, k, xiT_src, src_res)
                                if k == 3:
                                    apply_half(h, 0, xiT_dst, dst_res)
                            apply_half(h, 1, xiT_dst, dst_res)
                    xiT_fin = xiT_b if len(T_STEPS) % 2 == 1 else xiT_a
                    xiT_fin_res = "xiT_b_h" if len(T_STEPS) % 2 == 1 else "xiT_a_h"

            S.barrier()
            with ExitStack() as p4:
                woT_sb = sbuf(p4, "woT_sb", [128, 8 * D], BF16)
                wost = sbuf(p4, "wost", [128, D], F32)
                xrs = [sbuf(p4, f"xrs{i}", [128, D], F32) for i in range(2)]
                x1p = sbuf(p4, "x1p", [128, D], F32)
                x1c = sbuf(p4, "x1c", [128, D], BF16)
                with nc.psum_tensor("p_wo", [128, 2048], F32) as p_wo:
                    for f in range(8):
                        S.op("sync", lambda e, f=f: e.dma_start(out=wost[:], in_=wo_h[f * 128:(f + 1) * 128, :]),
                             writes=["wost"], dma=True)
                        S.op("vector", lambda e, f=f: e.tensor_copy(woT_sb[:, f * D:(f + 1) * D], wost[:]),
                             reads=["wost"], writes=["woT_sb"])
                    for k in range(8):
                        S.op("sync", lambda e, k=k: e.dma_start(out=xrs[k % 2][:], in_=x_h[k * 128:(k + 1) * 128, :]),
                             writes=[f"xrs{k % 2}"], dma=True)
                        def mm_fn(e, k=k):
                            for f in range(8):
                                for j in range(4):
                                    ins = e.matmul(p_wo[:, j * 512:(j + 1) * 512],
                                                   xiT_fin[:, f * NT + k * 128:f * NT + (k + 1) * 128],
                                                   woT_sb[:, f * D + j * 512:f * D + (j + 1) * 512],
                                                   start=(f == 0), stop=(f == 7))
                            return ins
                        S.op("tensor", mm_fn,
                             reads=[f"{xiT_fin_res}{hh}" for hh in range(8)] + ["woT_sb"], writes=["p_wo"])
                        S.op("vector", lambda e, k=k: e.tensor_tensor(x1p[:], xrs[k % 2][:], p_wo[:], ALU.add),
                             reads=[f"xrs{k % 2}", "p_wo"], writes=["x1p"])
                        cA_t, cB_t, cAr, cBr = ln_cols(S, x1p, "x1p", 4, k)
                        S.op("vector", lambda e, cA_t=cA_t: e.tensor_scalar(x1c[:], x1p[:], cA_t[:, 0:1], None, ALU.add, ALU.bypass),
                             reads=["x1p", cAr], writes=["x1c"])
                        S.op("vector",
                             lambda e, k=k, cB_t=cB_t: e.tensor_scalar(x1nat[:, k * D:(k + 1) * D], x1c[:], cB_t[:, 0:1], None, ALU.mult, ALU.bypass),
                             reads=["x1c", cBr], writes=[f"x1_{k}"])

        if debug:
            S.op("sync", lambda e: e.dma_start(out=dbg["dbg_x1"], in_=x1nat[:]),
                 reads=[f"x1_{kk}" for kk in range(8)], dma=True)
        S.barrier()
        with ExitStack() as p5:
            x1T = sbuf(p5, "x1T", [128, 16 * NT], BF16)
            x1T_v = x1T.ap().rearrange("p (cc t) -> p cc t", cc=16)
            acc = sbuf(p5, "acc", [128, 8 * D], BF16)
            b0b_bc = sbuf(p5, "b0b_bc", [128, D], BF16)
            b1b_bc = sbuf(p5, "b1b_bc", [128, D], BF16)
            bd_bc = sbuf(p5, "bd_bc", [128, D], BF16)
            with nc.psum_tensor("p_tr3", [128, 2048], BF16) as p_tr3:
                for k in range(8):
                    def tr_fn(e, k=k):
                        for j in range(16):
                            ins = e.matmul(p_tr3[:, j * 128:(j + 1) * 128],
                                           x1nat[:, k * D + j * 128:k * D + (j + 1) * 128], eye_s[:],
                                           is_transpose=True, start=(j % 8 == 0), stop=(j % 8 == 7),
                                           skip_group_check=True)
                        return ins
                    S.op("tensor", tr_fn, reads=[f"x1_{k}", "eye_s"], writes=["p_tr3"])
                    S.op("vector",
                         lambda e, k=k: e.tensor_copy(x1T_v[:, :, k * 128:(k + 1) * 128], p_tr3[:]),
                         reads=["p_tr3"], writes=["x1T"])
            S.barrier()
            with nc.psum_tensor("p_bc2", [128, 2048], F32) as p_bc2:
                bcast(b0b_bc, "b0b_bc", bb_h[0], D, p_bc2, "p_bc2")
                bcast(b1b_bc, "b1b_bc", bb_h[1], D, p_bc2, "p_bc2")
            S.barrier()
            S.op("vector", lambda e: e.tensor_tensor(bd_bc[:], b0b_bc[:], b1b_bc[:], ALU.subtract),
                 reads=["b0b_bc", "b1b_bc"], writes=["bd_bc"])
            S.barrier()
            with ExitStack() as p5g:
                wgst = sbuf(p5g, "wgst", [128, 16 * 2], F32)
                wgbf = sbuf(p5g, "wgbf", [128, 16 * 2], BF16)
                S.op("sync", lambda e: e.dma_start(out=wgst[:], in_=wg_h),
                     writes=["wgst"], dma=True)
                S.op("vector", lambda e: e.tensor_copy(wgbf[:], wgst[:]), reads=["wgst"], writes=["wgbf"])
                with nc.psum_tensor("p_g", [128, 2], F32) as p_g:
                    for k in range(8):
                        def mm_fn(e, k=k):
                            for cc in range(16):
                                ins = e.matmul(p_g[:, 0:2], x1T_v[:, cc, k * 128:(k + 1) * 128],
                                               wgbf[:, cc * 2:(cc + 1) * 2], start=(cc == 0), stop=(cc == 15))
                            return ins
                        S.op("tensor", mm_fn, reads=["x1T", "wgbf"], writes=["p_g"])
                        S.op("vector", lambda e: e.tensor_copy(c2[:, 0:2], p_g[:, 0:2]),
                             reads=["p_g"], writes=["c2"])
                        S.op("vector",
                             lambda e, k=k: e.tensor_tensor(mask0_c[:, k:k + 1], c2[:, 0:1], c2[:, 1:2], ALU.is_ge),
                             reads=["c2"], writes=["mask0_c"])
            S.barrier()
            S.op("vector", lambda e: e.tensor_scalar(mask1_c[:, 0:8], mask0_c[:, 0:8], -1.0, 1.0, ALU.mult, ALU.add),
                 reads=["mask0_c"], writes=["mask1_c"])

            S.barrier()
            with ExitStack() as p5e:
                h_sup = sbuf(p5e, "h_sup", [128, 8 * NT], BF16)
                wast = [sbuf(p5e, f"wast{i}", [128, 2048], F32) for i in range(2)]
                wabf = sbuf(p5e, "wabf", [128, 2048], BF16)
                wbst = [sbuf(p5e, f"wbst{i}", [128, D], F32) for i in range(2)]
                wbbf = sbuf(p5e, "wbbf", [128, 8 * D], BF16)
                ba_sb = [sbuf(p5e, f"ba_sb{i}", [128, 32], F32) for i in range(2)]
                S.op("sync", lambda e: e.dma_start(out=ba_sb[0][:], in_=ba_h[0]), writes=["ba0"], dma=True)
                S.op("sync", lambda e: e.dma_start(out=ba_sb[1][:], in_=ba_h[1]), writes=["ba1"], dma=True)
                with (nc.psum_tensor("p_h", [128, 1024], F32) as p_h,
                      nc.psum_tensor("p_o", [128, 2048], F32) as p_o):
                    for e_idx in range(2):
                        mcol = mask0_c if e_idx == 0 else mask1_c
                        mres = "mask0_c" if e_idx == 0 else "mask1_c"
                        actf = AF.Gelu if e_idx == 0 else AF.Relu
                        basb = ba_sb[e_idx]
                        bares = f"ba{e_idx}"
                        for fs in range(4):
                            for fc8 in range(8):
                                fc = fs * 8 + fc8
                                wa_i = fc % 2
                                S.op("sync",
                                     lambda e, fc=fc, wa_i=wa_i, e_idx=e_idx: e.dma_start(
                                         out=wast[wa_i][:], in_=wa_h[e_idx][fc]),
                                     writes=[f"wast{wa_i}"], dma=True)
                                S.op("vector", lambda e, wa_i=wa_i: e.tensor_copy(wabf[:], wast[wa_i][:]),
                                     reads=[f"wast{wa_i}"], writes=["wabf"])
                                for th in range(2):
                                    def mm_fn(e, th=th):
                                        for cc in range(16):
                                            ins = e.matmul(p_h[:, th * 512:(th + 1) * 512],
                                                           wabf[:, cc * 128:(cc + 1) * 128],
                                                           x1T_v[:, cc, th * 512:(th + 1) * 512],
                                                           start=(cc == 0), stop=(cc == 15))
                                        return ins
                                    pr = f"p_h{th}"
                                    S.op("tensor", mm_fn, reads=["wabf", "x1T"], writes=[pr])
                                S.op("scalar",
                                     lambda e, fc=fc, fc8=fc8, actf=actf, basb=basb: e.activation(
                                         h_sup[:, fc8 * NT:(fc8 + 1) * NT], p_h[:], actf,
                                         bias=basb[:, fc:fc + 1], scale=1.0),
                                     reads=["p_h0", "p_h1", bares], writes=[f"hs{fc8}"])
                            for fc8 in range(8):
                                fc = fs * 8 + fc8
                                wb_i = fc8 % 2
                                S.op("sync",
                                     lambda e, fc=fc, wb_i=wb_i, e_idx=e_idx: e.dma_start(
                                         out=wbst[wb_i][:], in_=wb_h[e_idx][fc * 128:(fc + 1) * 128, :]),
                                     writes=[f"wbst{wb_i}"], dma=True)
                                S.op("vector",
                                     lambda e, fc8=fc8, wb_i=wb_i: e.tensor_copy(wbbf[:, fc8 * D:(fc8 + 1) * D], wbst[wb_i][:]),
                                     reads=[f"wbst{wb_i}"], writes=[f"wbbf{fc8}"])
                            for k in range(8):
                                for j in range(4):
                                    def mm_fn(e, k=k, j=j):
                                        for fc8 in range(8):
                                            ins = e.matmul(p_o[:, j * 512:(j + 1) * 512],
                                                           h_sup[:, fc8 * NT + k * 128:fc8 * NT + (k + 1) * 128],
                                                           wbbf[:, fc8 * D + j * 512:fc8 * D + (j + 1) * 512],
                                                           start=(fc8 == 0), stop=(fc8 == 7))
                                        return ins
                                    S.op("tensor", mm_fn,
                                         reads=[f"hs{i}" for i in range(8)] + [f"wbbf{i}" for i in range(8)],
                                         writes=[f"p_o{j}"])
                                    if e_idx == 0 and fs == 0:
                                        S.op("vector",
                                             lambda e, k=k, j=j, mcol=mcol: e.tensor_scalar(
                                                 acc[:, k * D + j * 512:k * D + (j + 1) * 512], p_o[:, j * 512:(j + 1) * 512],
                                                 mcol[:, k:k + 1], None, ALU.mult, ALU.bypass),
                                             reads=[f"p_o{j}", mres], writes=[f"acc{k}"])
                                    else:
                                        S.op("vector",
                                             lambda e, k=k, j=j, mcol=mcol: e.scalar_tensor_tensor(
                                                 acc[:, k * D + j * 512:k * D + (j + 1) * 512], p_o[:, j * 512:(j + 1) * 512],
                                                 mcol[:, k:k + 1],
                                                 acc[:, k * D + j * 512:k * D + (j + 1) * 512], ALU.mult, ALU.add),
                                             reads=[f"p_o{j}", mres, f"acc{k}"], writes=[f"acc{k}"])
            if debug:
                S.op("sync", lambda e: e.dma_start(out=dbg["dbg_acc"], in_=acc[:]),
                     reads=[f"acc{kk}" for kk in range(8)], dma=True)
            S.barrier()
            with ExitStack() as p5f:
                x2p = sbuf(p5f, "x2p", [128, D], F32)
                bmix = sbuf(p5f, "bmix", [128, D], BF16)
                outt = sbuf(p5f, "outt", [128, D], F32)
                for k in range(8):
                    S.op("vector",
                         lambda e, k=k: e.scalar_tensor_tensor(bmix[:], bd_bc[:], mask0_c[:, k:k + 1], b1b_bc[:],
                                                               ALU.mult, ALU.add),
                         reads=["bd_bc", "mask0_c", "b1b_bc"], writes=["bmix"])
                    S.op("vector", lambda e, k=k: e.tensor_tensor(x2p[:], x1nat[:, k * D:(k + 1) * D], acc[:, k * D:(k + 1) * D], ALU.add),
                         reads=[f"x1_{k}", f"acc{k}"], writes=["x2p"])
                    S.op("vector", lambda e: e.tensor_tensor(x2p[:], x2p[:], bmix[:], ALU.add),
                         reads=["x2p", "bmix"], writes=["x2p"])
                    cA_t, cB_t, cAr, cBr = ln_cols(S, x2p, "x2p", 4, k)
                    S.op("vector", lambda e, cA_t=cA_t: e.tensor_scalar(x2p[:], x2p[:], cA_t[:, 0:1], None, ALU.add, ALU.bypass),
                         reads=["x2p", cAr], writes=["x2p"])
                    S.op("vector", lambda e, cB_t=cB_t: e.tensor_scalar(outt[:], x2p[:], cB_t[:, 0:1], None, ALU.mult, ALU.bypass),
                         reads=["x2p", cBr], writes=["outt"])
                    S.op("sync", lambda e, k=k: e.dma_start(out=out_h[k * 128:(k + 1) * 128, :], in_=outt[:]),
                         reads=["outt"], writes=[f"store{k}"], dma=True)

        with ExitStack() as semctx:
            sems = {}
            for e in ENGS:
                sems[e] = semctx.enter_context(nc.semaphore(f"s_{e}"))
            for i in range(NDMA):
                sems[f"dma{i}"] = semctx.enter_context(nc.semaphore(f"s_dma{i}"))
            block = semctx.enter_context(nc.Block())
            dma_final = [(f"dma{i}", 16 * S.dma_use[i]) for i in range(NDMA) if S.dma_use[i]]

            @block.sync
            def _(sync):
                S.emit("sync", sync, sems)
                for nm, val in dma_final:
                    sync.wait_ge(sems[nm], val)

            @block.vector
            def _(vector):
                S.emit("vector", vector, sems)

            @block.scalar
            def _(scalar):
                S.emit("scalar", scalar, sems)

            @block.tensor
            def _(tensor):
                S.emit("tensor", tensor, sems)

            if S.q["gpsimd"]:
                @block.gpsimd
                def _(gpsimd):
                    S.emit("gpsimd", gpsimd, sems)
    return nc


_CACHE = {}


def kernel(**inputs):
    f32 = np.float32
    x = np.asarray(inputs["L0_attn_hidden"], f32)
    mem = np.ascontiguousarray(np.asarray(inputs["L0_attn_memory"], f32))
    wq = np.asarray(inputs["L0_attn_Wq"], f32)
    wk = np.asarray(inputs["L0_attn_Wk"], f32)
    wv = np.asarray(inputs["L0_attn_Wv"], f32)
    wo = np.asarray(inputs["L0_attn_Wo"], f32)
    lb = np.ascontiguousarray(np.asarray(inputs["L0_attn_log_beta"], f32).reshape(1, NH))
    gs = np.ascontiguousarray(np.asarray(inputs["L0_attn_g_state"], f32).reshape(1, D))
    bs = np.ascontiguousarray(np.asarray(inputs["L0_attn_b_state"], f32).reshape(1, D))
    gm = np.ascontiguousarray(np.asarray(inputs["L0_attn_g_mem"], f32).reshape(1, MD))
    bm = np.ascontiguousarray(np.asarray(inputs["L0_attn_b_mem"], f32).reshape(1, MD))
    wg = np.asarray(inputs["L0_ffn_Wg"], f32)
    w0a = np.asarray(inputs["L0_ffn_W0a"], f32)
    w1a = np.asarray(inputs["L0_ffn_W1a"], f32)
    w0b = np.asarray(inputs["L0_ffn_W0b"], f32)
    w1b = np.asarray(inputs["L0_ffn_W1b"], f32)
    b0a = np.asarray(inputs["L0_ffn_b0a"], f32)
    b1a = np.asarray(inputs["L0_ffn_b1a"], f32)
    b0b = np.ascontiguousarray(np.asarray(inputs["L0_ffn_b0b"], f32).reshape(1, D))
    b1b = np.ascontiguousarray(np.asarray(inputs["L0_ffn_b1b"], f32).reshape(1, D))

    wqT = np.ascontiguousarray(wq.T.reshape(16, 128, 8, 128).transpose(2, 1, 0, 3).reshape(8, 128, 2048))
    wkT = np.ascontiguousarray(wk.T)
    wvT = np.ascontiguousarray(wv.T)
    woT = np.ascontiguousarray(wo.T)
    wgT = np.ascontiguousarray(wg.T.reshape(16, 128, 2).transpose(1, 0, 2).reshape(128, 32))
    w0aT = np.ascontiguousarray(w0a.T.reshape(16, 128, 32, 128).transpose(2, 1, 0, 3).reshape(32, 128, 2048))
    w1aT = np.ascontiguousarray(w1a.T.reshape(16, 128, 32, 128).transpose(2, 1, 0, 3).reshape(32, 128, 2048))
    w0bT = np.ascontiguousarray(w0b.T)
    w1bT = np.ascontiguousarray(w1b.T)
    b0a_c = np.ascontiguousarray(b0a.reshape(32, 128).T)
    b1a_c = np.ascontiguousarray(b1a.reshape(32, 128).T)
    ones1 = np.ones((1, 128), f32)
    eye = np.eye(128).astype(bfnp)

    if "nc" not in _CACHE:
        _CACHE["nc"] = build_program()
    nc = _CACHE["nc"]

    in_maps = []
    for c in range(8):
        b, sh = c // 2, c % 2
        xs = np.ascontiguousarray(x[b, sh * NT:(sh + 1) * NT, :])
        in_maps.append({
            "x": xs, "mem": mem, "wqT": wqT, "wkT": wkT, "wvT": wvT, "woT": woT,
            "wgT": wgT, "w0aT": w0aT, "w1aT": w1aT, "w0bT": w0bT, "w1bT": w1bT,
            "log_beta": lb, "g_state": gs, "b_state": bs, "g_mem": gm, "b_mem": bm,
            "b0a": b0a_c, "b1a": b1a_c, "b0b": b0b, "b1b": b1b,
            "ones1": ones1, "eye": eye,
        })
    res = run_bass_kernel_spmd(nc, in_maps, core_ids=list(range(8)), trace_cores=list(range(8)))
    out = np.empty((4, 2048, D), np.float32)
    for c in range(8):
        b, sh = c // 2, c % 2
        out[b, sh * NT:(sh + 1) * NT, :] = res.results[c]["out"]
    return out


# version 21
# speedup vs baseline: 1.0384x; 1.0075x over previous
from contextlib import ExitStack

import numpy as np
import ml_dtypes

import concourse.bass as bass
import concourse.mybir as mybir
from concourse.bass_utils import run_bass_kernel_spmd

F32 = mybir.dt.float32
BF16 = mybir.dt.bfloat16
AF = mybir.ActivationFunctionType
ALU = mybir.AluOpType
bfnp = ml_dtypes.bfloat16

T_STEPS = (5, 6, 7)
USE_FIN = True

NT = 1024
D = 2048
NH = 8
NMEM = 2048
MD = 1024
FF = 4096
EPS = 1e-5
LN_HALF = float(np.log(0.5))

ENGS = ("sync", "vector", "scalar", "tensor", "gpsimd")
NDMA = 8


class KB:

    def __init__(self):
        self.q = {e: [] for e in ENGS}
        self.cnt = {e: 0 for e in ENGS}
        self.dma_i = 0
        self.dma_use = [0] * NDMA
        self.res = {}
        self.barrier_deps = []

    def barrier(self):
        self.barrier_deps = [(e, self.cnt[e]) for e in ENGS if self.cnt[e] > 0]
        self.barrier_deps += [(f"dma{i}", 16 * self.dma_use[i]) for i in range(NDMA) if self.dma_use[i] > 0]

    def _res(self, name):
        return self.res.setdefault(name, {"w": None, "r": []})

    def op(self, eng, fn, reads=(), writes=(), deps=(), dma=False):
        dep = list(deps) + self.barrier_deps
        pre = []
        if dma:
            s = self.dma_i % NDMA
            self.dma_i += 1
            if self.dma_use[s] > 0:
                pre.append((f"dma{s}", 16 * self.dma_use[s]))
            self.dma_use[s] += 1
            tok = (f"dma{s}", 16 * self.dma_use[s])
            sem_tag = f"dma{s}"
        else:
            tok = (eng, self.cnt[eng] + 1)
            self.cnt[eng] = tok[1]
            sem_tag = eng
        for nm in reads:
            r = self._res(nm)
            if r["w"] is not None:
                dep.append(r["w"])
        for nm in writes:
            r = self._res(nm)
            if r["w"] is not None:
                dep.append(r["w"])
            dep.extend(r["r"])
        for nm in reads:
            self._res(nm)["r"].append(tok)
        for nm in writes:
            r = self._res(nm)
            r["w"] = tok
            r["r"] = []
        self.q[eng].append((fn, tuple(dep), tuple(pre), sem_tag, tok[1]))
        return tok

    SELF_SYNC = True

    def emit(self, eng_name, eng, sems):
        waited = {}
        for fn, deps, pre, sem_tag, tokval in self.q[eng_name]:
            need = {}
            for (de, dv) in list(deps) + list(pre):
                if de == eng_name and not self.SELF_SYNC:
                    continue
                if dv > waited.get(de, 0):
                    need[de] = max(need.get(de, 0), dv)
            for de in sorted(need):
                eng.wait_ge(sems[de], need[de])
                waited[de] = need[de]
            ins = fn(eng)
            if sem_tag.startswith("dma"):
                ins.then_inc(sems[sem_tag], 16)
            elif hasattr(eng, "engine_nop"):
                eng.engine_nop().then_inc(sems[sem_tag], 1)
            else:
                ins.then_inc(sems[sem_tag], 1)


def build_program(detect_races=True, debug=False):
    nc = bass.Bass("TRN2", target_bir_lowering=False, detect_race_conditions=detect_races)

    def dram(name, shape, dt=F32, kind="ExternalInput"):
        return nc.dram_tensor(name, shape, dt, kind=kind).ap()

    x_h = dram("x", [NT, D])
    mem_h = dram("mem", [NMEM, MD])
    wq_h = dram("wqT", [8, 128, 16 * 128])
    wk_h = dram("wkT", [MD, MD])
    wv_h = dram("wvT", [MD, MD])
    wo_h = dram("woT", [NH * 128, D])
    wg_h = dram("wgT", [128, 32])
    wa_h = [dram("w0aT", [32, 128, 16 * 128]), dram("w1aT", [32, 128, 16 * 128])]
    wb_h = [dram("w0bT", [FF, D]), dram("w1bT", [FF, D])]
    lb_h = dram("log_beta", [1, NH])
    gs_h = dram("g_state", [1, D])
    bs_h = dram("b_state", [1, D])
    gm_h = dram("g_mem", [1, MD])
    bm_h = dram("b_mem", [1, MD])
    ba_h = [dram("b0a", [128, 32]), dram("b1a", [128, 32])]
    bb_h = [dram("b0b", [1, D]), dram("b1b", [1, D])]
    ones_h = dram("ones1", [1, 128])
    eye_h = dram("eye", [128, 128], BF16)
    out_h = dram("out", [NT, D], kind="ExternalOutput")
    dbg = {}
    if debug:
        for nm, sh, dt in (("dbg_KT", [128, NH * NMEM], BF16), ("dbg_Vb", [128, 16 * MD], BF16),
                           ("dbg_xi0", [128, NH * NT], BF16), ("dbg_xi1", [128, NH * NT], BF16),
                           ("dbg_xi2", [128, NH * NT], BF16), ("dbg_xi3", [128, NH * NT], BF16),
                           ("dbg_x1", [128, 8 * D], BF16), ("dbg_acc", [128, 8 * D], BF16),
                           ("dbg_mnT", [128, 8 * NMEM], BF16), ("dbg_ag", [128, 2], F32),
                           ("dbg_colB", [128, 8], F32), ("dbg_msc", [128, MD], F32),
                           ("dbg_mnbf", [128, MD], BF16)):
            dbg[nm] = dram(nm, sh, dt, kind="ExternalOutput")

    S = KB()

    def mk_ln_cols(lnbufs):
        def ln_cols(S, src_ap, src_res, nchunks, par):
            p = par % 2
            bnst, colA, colB = lnbufs[p]
            bn, cA, cB = f"bnst{p}", f"colA{p}", f"colB{p}"
            def st_fn(e):
                for i in range(nchunks):
                    ins = e.bn_stats(bnst[:, i * 6:(i + 1) * 6], src_ap[:, i * 512:(i + 1) * 512])
                return ins
            S.op("vector", st_fn, reads=[src_res], writes=[bn])
            S.op("vector", lambda e: e.bn_aggr(colA[:, 6:8], bnst[:, 0:6 * nchunks]),
                 reads=[bn], writes=[cA])
            S.op("vector", lambda e: e.tensor_scalar(colA[:, 0:1], colA[:, 6:7], -1.0, None, ALU.mult, ALU.bypass),
                 reads=[cA], writes=[cA])
            S.op("vector", lambda e: e.tensor_scalar(colB[:, 0:1], colA[:, 7:8], EPS, None, ALU.add, ALU.bypass),
                 reads=[cA], writes=[cB])
            S.op("scalar", lambda e: e.activation(colB[:, 0:1], colB[:, 0:1], AF.Ln),
                 reads=[cB], writes=[cB])
            S.op("scalar", lambda e: e.activation(colB[:, 0:1], colB[:, 0:1], AF.Exp, scale=-0.5),
                 reads=[cB], writes=[cB])
            return colA, colB, cA, cB
        return ln_cols

    with ExitStack() as ctx:
        def sbuf(stack, name, shape, dt):
            return stack.enter_context(nc.sbuf_tensor(name, shape, dt))

        ones_s = sbuf(ctx, "ones_s", [1, 128], F32)
        eye_s = sbuf(ctx, "eye_s", [128, 128], BF16)
        lb_s = sbuf(ctx, "lb_s", [1, NH], F32)
        ln2_s = sbuf(ctx, "ln2_s", [1, 1], F32)
        beta_bc = sbuf(ctx, "beta_bc", [128, NH], F32)
        lnbufs = [(sbuf(ctx, f"bnst{i}", [128, 24], F32),
                   sbuf(ctx, f"colA{i}", [128, 8], F32),
                   sbuf(ctx, f"colB{i}", [128, 8], F32)) for i in range(2)]
        ln_cols = mk_ln_cols(lnbufs)
        ymax = sbuf(ctx, "ymax", [128, 8], F32)
        ntau_a = sbuf(ctx, "ntau_a", [128, 8], F32)
        ntau_b = sbuf(ctx, "ntau_b", [128, 8], F32)
        ntau_f = sbuf(ctx, "ntau_f", [128, 8], F32)
        ntau_g = sbuf(ctx, "ntau_g", [128, 8], F32)
        src_c = sbuf(ctx, "src_c", [128, 8], F32)
        sr2_c = sbuf(ctx, "sr2_c", [128, 8], F32)
        kc_c = sbuf(ctx, "kc_c", [128, 8], F32)
        c1 = sbuf(ctx, "c1", [128, 8], F32)
        c2 = sbuf(ctx, "c2", [128, 8], F32)
        c3 = sbuf(ctx, "c3", [128, 8], F32)
        mask0_c = sbuf(ctx, "mask0_c", [128, 8], F32)
        mask1_c = sbuf(ctx, "mask1_c", [128, 8], F32)
        x1nat = sbuf(ctx, "x1nat", [128, 8 * D], BF16)

        S.op("sync", lambda e: e.dma_start(out=ones_s[:], in_=ones_h), writes=["ones_s"], dma=True)
        S.op("sync", lambda e: e.dma_start(out=eye_s[:], in_=eye_h), writes=["eye_s"], dma=True)
        S.op("sync", lambda e: e.dma_start(out=lb_s[:], in_=lb_h), writes=["lb_s"], dma=True)
        S.op("vector", lambda e: e.memset(ln2_s[:], LN_HALF), writes=["ln2_s"])
        S.op("scalar", lambda e: e.activation(lb_s[:], lb_s[:], AF.Exp, bias=ln2_s[0:1, 0:1], scale=1.0),
             reads=["ln2_s"], writes=["lb_s"])
        with nc.psum_tensor("p_b0", [128, 8], F32) as p_b0:
            S.op("tensor", lambda e: e.matmul(p_b0[:, 0:8], ones_s[0:1, :], lb_s[0:1, :], start=True, stop=True),
                 reads=["ones_s", "lb_s"], writes=["p_b0"])
            S.op("vector", lambda e: e.tensor_copy(beta_bc[:], p_b0[:, 0:8]),
                 reads=["p_b0"], writes=["beta_bc"])

        S.barrier()
        def bcast(dst, dres, row_h, width, psum, pres, vec_s, vres):
            S.op("sync", lambda e: e.dma_start(out=vec_s[0:1, 0:width], in_=row_h),
                 writes=[vres], dma=True)
            for j0 in range(0, width, 512):
                S.op("tensor",
                     lambda e, j0=j0: e.matmul(psum[:, j0 % 2048:j0 % 2048 + 512], ones_s[0:1, :],
                                               vec_s[0:1, j0:j0 + 512], start=True, stop=True),
                     reads=["ones_s", vres], writes=[f"{pres}{j0 % 2048}"])
                S.op("vector", lambda e, j0=j0: e.tensor_copy(dst[:, j0:j0 + 512], psum[:, j0 % 2048:j0 % 2048 + 512]),
                     reads=[f"{pres}{j0 % 2048}"], writes=[dres])

        with ExitStack() as attn:
            KT = sbuf(attn, "KT", [128, NH * NMEM], BF16)
            Vb = sbuf(attn, "Vb", [128, 16 * MD], BF16)
            xiT_a = sbuf(attn, "xiT_a", [128, NH * NT], BF16)
            xiT_b = sbuf(attn, "xiT_b", [128, NH * NT], BF16)

            with ExitStack() as p1:
                mnT = sbuf(p1, "mnT", [128, 8 * NMEM], BF16)
                mnT_v = mnT.ap().rearrange("p (mc n) -> p mc n", mc=8)
                gm_bc = sbuf(p1, "gm_bc", [128, MD], BF16)
                bm_bc = sbuf(p1, "bm_bc", [128, MD], BF16)
                with ExitStack() as p1a:
                    mst = sbuf(p1a, "mst", [128, 4 * MD], F32)
                    msc2 = [sbuf(p1a, f"msc{i}", [128, MD], F32) for i in range(2)]
                    mnbf = [sbuf(p1a, f"mnbf{i}", [128, MD], BF16) for i in range(2)]
                    vec1 = sbuf(p1a, "vec1", [1, MD], F32)
                    with nc.psum_tensor("p_m", [128, 1024], F32) as p_m:
                        bcast(gm_bc, "gm_bc", gm_h, MD, p_m, "p_m", vec1, "vec1")
                        bcast(bm_bc, "bm_bc", bm_h, MD, p_m, "p_m", vec1, "vec1")
                    S.barrier()
                    with nc.psum_tensor("p_tr1", [128, 1024], BF16) as p_tr1:
                        for half in range(4):
                            mres = "mst"
                            S.op("sync",
                                 lambda e, half=half: e.dma_start(
                                     out=mst.ap().rearrange("p (g m) -> p g m", g=4),
                                     in_=mem_h.rearrange("(g p) m -> p g m", p=128)[:, half * 4:(half + 1) * 4, :]),
                                 writes=[mres], dma=True)
                            for it in range(4):
                                nt_i = half * 4 + it
                                mtile = mst[:, it * MD:(it + 1) * MD]
                                cA_t, cB_t, cAr, cBr = ln_cols(S, mtile, mres, 2, nt_i)
                                msc = msc2[nt_i % 2]
                                mscr = f"msc{nt_i % 2}"
                                S.op("vector", lambda e, mtile=mtile, cA_t=cA_t, msc=msc: e.tensor_scalar(msc[:], mtile, cA_t[:, 0:1], None, ALU.add, ALU.bypass),
                                     reads=[mres, cAr], writes=[mscr])
                                S.op("vector", lambda e, cB_t=cB_t, msc=msc: e.scalar_tensor_tensor(msc[:], msc[:], cB_t[:, 0:1], gm_bc[:], ALU.mult, ALU.mult),
                                     reads=[mscr, cBr, "gm_bc"], writes=[mscr])
                                mb = mnbf[nt_i % 2]
                                mbres = f"mnbf{nt_i % 2}"
                                S.op("vector", lambda e, mb=mb, msc=msc: e.tensor_tensor(mb[:], msc[:], bm_bc[:], ALU.add),
                                     reads=[mscr, "bm_bc"], writes=[mbres])
                                def tr_fn(e, mb=mb):
                                    for j in range(8):
                                        ins = e.matmul(p_tr1[:, j * 128:(j + 1) * 128], mb[:, j * 128:(j + 1) * 128], eye_s[:],
                                                       is_transpose=True, start=(j == 0), stop=(j == 7), skip_group_check=True)
                                    return ins
                                S.op("tensor", tr_fn, reads=[mbres, "eye_s"], writes=["p_tr1"])
                                S.op("vector",
                                     lambda e, nt_i=nt_i: e.tensor_copy(
                                         mnT_v[:, :, nt_i * 128:(nt_i + 1) * 128], p_tr1[:]),
                                     reads=["p_tr1"], writes=["mnT"])
                                if debug and nt_i == 15:
                                    S.op("sync", lambda e: e.dma_start(out=dbg["dbg_ag"], in_=bnag[:]), reads=["bnag"], dma=True)
                                    S.op("sync", lambda e: e.dma_start(out=dbg["dbg_colB"], in_=colB[:]), reads=["colB"], dma=True)
                                    S.op("sync", lambda e: e.dma_start(out=dbg["dbg_msc"], in_=msc[:]), reads=["msc"], dma=True)
                                    S.op("sync", lambda e, mb=mb: e.dma_start(out=dbg["dbg_mnbf"], in_=mb[:]), reads=[mbres], dma=True)
                if debug:
                    S.op("sync", lambda e: e.dma_start(out=dbg["dbg_mnT"], in_=mnT[:]), reads=["mnT"], dma=True)
                S.barrier()
                for wgt_h, name in ((wk_h, "K"), (wv_h, "V")):
                    S.barrier()
                    with ExitStack() as pw:
                        wst = sbuf(pw, "wst_" + name, [128, 4 * MD], F32)
                        wbf = sbuf(pw, "wbf_" + name, [128, 8 * MD], BF16)
                        for hf in range(2):
                            S.op("sync",
                                 lambda e, wgt_h=wgt_h, hf=hf: e.dma_start(
                                     out=wst.ap().rearrange("p (mc f) -> p mc f", mc=4),
                                     in_=wgt_h.rearrange("(mc p) f -> p mc f", p=128)[:, hf * 4:(hf + 1) * 4, :]),
                                 writes=["wst"], dma=True)
                            for i in range(4):
                                S.op("vector",
                                     lambda e, i=i, hf=hf: e.tensor_copy(
                                         wbf[:, (hf * 4 + i) * MD:(hf * 4 + i + 1) * MD], wst[:, i * MD:(i + 1) * MD]),
                                     reads=["wst"], writes=["wbf"])
                        with nc.psum_tensor("p_kv_" + name, [128, 1024], F32) as p_kv:
                            if name == "K":
                                for f in range(8):
                                    for ncol in range(4):
                                        def mm_fn(e, f=f, ncol=ncol):
                                            for mc in range(8):
                                                ins = e.matmul(p_kv[:, (ncol % 2) * 512:(ncol % 2) * 512 + 512],
                                                               wbf[:, mc * MD + f * 128:mc * MD + (f + 1) * 128],
                                                               mnT_v[:, mc, ncol * 512:(ncol + 1) * 512],
                                                               start=(mc == 0), stop=(mc == 7))
                                            return ins
                                        pr = f"p_kv{ncol % 2}"
                                        S.op("tensor", mm_fn, reads=["wbf", "mnT"], writes=[pr])
                                        S.op("vector",
                                             lambda e, f=f, ncol=ncol: e.tensor_copy(
                                                 KT[:, f * NMEM + ncol * 512:f * NMEM + (ncol + 1) * 512],
                                                 p_kv[:, (ncol % 2) * 512:(ncol % 2) * 512 + 512]),
                                             reads=[pr], writes=["KT"])
                            else:
                                for ntile in range(16):
                                    for fcol in range(2):
                                        def mm_fn(e, ntile=ntile, fcol=fcol):
                                            for mc in range(8):
                                                ins = e.matmul(p_kv[:, fcol * 512:(fcol + 1) * 512],
                                                               mnT_v[:, mc, ntile * 128:(ntile + 1) * 128],
                                                               wbf[:, mc * MD + fcol * 512:mc * MD + (fcol + 1) * 512],
                                                               start=(mc == 0), stop=(mc == 7))
                                            return ins
                                        pr = f"p_kv{fcol}"
                                        S.op("tensor", mm_fn, reads=["wbf", "mnT"], writes=[pr])
                                        S.op("vector",
                                             lambda e, ntile=ntile, fcol=fcol: e.tensor_copy(
                                                 Vb[:, ntile * MD + fcol * 512:ntile * MD + (fcol + 1) * 512],
                                                 p_kv[:, fcol * 512:(fcol + 1) * 512]),
                                             reads=[pr], writes=["Vb"])

            if debug:
                S.op("sync", lambda e: e.dma_start(out=dbg["dbg_KT"], in_=KT[:]), reads=["KT"], dma=True)
                S.op("sync", lambda e: e.dma_start(out=dbg["dbg_Vb"], in_=Vb[:]), reads=["Vb"], dma=True)
            S.barrier()
            with ExitStack() as p2:
                gs_bc = sbuf(p2, "gs_bc", [128, D], BF16)
                bs_bc = sbuf(p2, "bs_bc", [128, D], BF16)
                lnxTh = sbuf(p2, "lnxTh", [128, 16 * 512], BF16)
                lnxT_v = lnxTh.ap().rearrange("p (cc t) -> p cc t", cc=16)
                xst = [sbuf(p2, f"xst{i}", [128, D], F32) for i in range(2)]
                lnbf = [sbuf(p2, f"lnbf{i}", [128, D], BF16) for i in range(2)]
                wqst = sbuf(p2, "wqst", [128, 16 * 128], F32)
                wqbf = sbuf(p2, "wqbf", [128, 16 * 128], BF16)
                vec2 = sbuf(p2, "vec2", [1, D], F32)
                with nc.psum_tensor("p_bb", [128, 2048], F32) as p_bb:
                    bcast(gs_bc, "gs_bc", gs_h, D, p_bb, "p_bb", vec2, "vec2")
                    bcast(bs_bc, "bs_bc", bs_h, D, p_bb, "p_bb", vec2, "vec2")
                S.barrier()
                with (nc.psum_tensor("p_tr2", [128, 2048], BF16) as p_tr2,
                      nc.psum_tensor("p_q", [128, 1024], F32) as p_q):
                    for half in range(2):
                        for kk in range(4):
                            k = half * 4 + kk
                            xs = xst[kk % 2]
                            xres = f"xst{kk % 2}"
                            S.op("sync", lambda e, xs=xs, k=k: e.dma_start(out=xs[:], in_=x_h[k * 128:(k + 1) * 128, :]),
                                 writes=[xres], dma=True)
                            cA_t, cB_t, cAr, cBr = ln_cols(S, xs, xres, 4, k)
                            S.op("vector", lambda e, xs=xs, cA_t=cA_t: e.tensor_scalar(xs[:], xs[:], cA_t[:, 0:1], None, ALU.add, ALU.bypass),
                                 reads=[xres, cAr], writes=[xres])
                            S.op("vector", lambda e, xs=xs, cB_t=cB_t: e.scalar_tensor_tensor(xs[:], xs[:], cB_t[:, 0:1], gs_bc[:], ALU.mult, ALU.mult),
                                 reads=[xres, cBr, "gs_bc"], writes=[xres])
                            lb2 = lnbf[kk % 2]
                            lbres = f"lnbf{kk % 2}"
                            S.op("vector", lambda e, lb2=lb2, xs=xs: e.tensor_tensor(lb2[:], xs[:], bs_bc[:], ALU.add),
                                 reads=[xres, "bs_bc"], writes=[lbres])
                            def tr_fn(e, lb2=lb2):
                                for j in range(16):
                                    ins = e.matmul(p_tr2[:, j * 128:(j + 1) * 128], lb2[:, j * 128:(j + 1) * 128], eye_s[:],
                                                   is_transpose=True, start=(j % 8 == 0), stop=(j % 8 == 7), skip_group_check=True)
                                return ins
                            S.op("tensor", tr_fn, reads=[lbres, "eye_s"], writes=["p_tr2"])
                            S.op("vector",
                                 lambda e, kk=kk: e.tensor_copy(lnxT_v[:, :, kk * 128:(kk + 1) * 128], p_tr2[:]),
                                 reads=["p_tr2"], writes=["lnxTh"])
                        for h in range(8):
                            S.op("sync", lambda e, h=h: e.dma_start(out=wqst[:], in_=wq_h[h]),
                                 writes=["wqst"], dma=True)
                            for i in range(4):
                                S.op("vector", lambda e, i=i: e.tensor_copy(wqbf[:, i * 512:(i + 1) * 512], wqst[:, i * 512:(i + 1) * 512]),
                                     reads=["wqst"], writes=["wqbf"])
                            def mm_fn(e):
                                for cc in range(16):
                                    ins = e.matmul(p_q[:, 0:512],
                                                   wqbf[:, cc * 128:(cc + 1) * 128],
                                                   lnxT_v[:, cc, 0:512],
                                                   start=(cc == 0), stop=(cc == 15))
                                return ins
                            S.op("tensor", mm_fn, reads=["wqbf", "lnxTh"], writes=["p_q"])
                            S.op("vector",
                                 lambda e, h=h, half=half: e.tensor_copy(
                                     xiT_a[:, h * NT + half * 512:h * NT + half * 512 + 512], p_q[:, 0:512]),
                                 reads=["p_q"], writes=[f"xiT_a_h{h}"])

            S.barrier()
            with ExitStack() as p3:
                y_buf = sbuf(p3, "y_buf", [128, 8 * NMEM], BF16)
                r_buf = [sbuf(p3, f"r_buf{i}", [128, NMEM], BF16) for i in range(4)]
                w_fin = [sbuf(p3, f"w_fin{i}", [128, NMEM], BF16) for i in range(2)]
                wT_buf = sbuf(p3, "wT_buf", [128, 16 * 512], BF16)
                wT_v = wT_buf.ap().rearrange("p (nc t) -> p nc t", nc=16)
                with (nc.psum_tensor("p_sc", [128, 2048], F32) as p_sc,
                      nc.psum_tensor("p_wt", [128, 2048], BF16) as p_wt,
                      nc.psum_tensor("p_ap", [128, 1024], F32) as p_ap):
                    def scores_pass0(step, h, k, xiT_src, src_res):
                        def mm_fn(e, h=h, k=k, xiT_src=xiT_src):
                            for j in range(4):
                                ins = e.matmul(p_sc[:, j * 512:(j + 1) * 512],
                                               xiT_src[:, h * NT + k * 128:h * NT + (k + 1) * 128],
                                               KT[:, h * NMEM + j * 512:h * NMEM + (j + 1) * 512],
                                               start=True, stop=True)
                            return ins
                        S.op("tensor", mm_fn, reads=[f"{src_res}{h}", "KT"], writes=["p_sc"])
                        S.op("vector",
                             lambda e, h=h, k=k: e.tensor_scalar(
                                 y_buf[:, k * NMEM:(k + 1) * NMEM], p_sc[:],
                                 beta_bc[:, h:h + 1], None, ALU.mult, ALU.max,
                                 accum_out=ymax[:, k:k + 1]),
                             reads=["p_sc", "beta_bc"], writes=[f"y{k}", f"ymax{k // 4}"])

                    def final_relu(h, k, fin, fin_res):
                        rb = r_buf[k % 4]
                        S.op("vector",
                             lambda e, k=k, rb=rb, fin=fin: e.tensor_scalar(
                                 rb[:], y_buf[:, k * NMEM:(k + 1) * NMEM],
                                 fin[:, k:k + 1], 0.0, ALU.add, ALU.max),
                             reads=[f"y{k}", fin_res], writes=[f"rb{k % 4}"])

                    def final_sq_tr(h, k):
                        rb = r_buf[k % 4]
                        wf = w_fin[k % 2]
                        S.op("scalar",
                             lambda e, rb=rb, wf=wf: e.activation(wf[:], rb[:], AF.Square),
                             reads=[f"rb{k % 4}"], writes=[f"wf{k % 2}"])
                        def tr_fn(e, wf=wf):
                            for j in range(16):
                                ins = e.matmul(p_wt[:, j * 128:(j + 1) * 128], wf[:, j * 128:(j + 1) * 128], eye_s[:],
                                               is_transpose=True, start=(j % 8 == 0), stop=(j % 8 == 7),
                                               skip_group_check=True)
                            return ins
                        S.op("tensor", tr_fn, reads=[f"wf{k % 2}", "eye_s"], writes=["p_wt"])

                    def final_evac(h, k):
                        S.op("scalar",
                             lambda e, k=k: e.activation(
                                 wT_v[:, :, (k % 4) * 128:(k % 4 + 1) * 128], p_wt[:], AF.Copy),
                             reads=["p_wt"], writes=["wT"])

                    def apply_half(h, half, xiT_dst, dst_res):
                        def mm_fn(e, h=h):
                            for ncv in range(16):
                                ins = e.matmul(p_ap[:, (half % 2) * 512:(half % 2) * 512 + 512],
                                               Vb[:, ncv * MD + h * 128:ncv * MD + (h + 1) * 128],
                                               wT_v[:, ncv, 0:512],
                                               start=(ncv == 0), stop=(ncv == 15))
                            return ins
                        S.op("tensor", mm_fn, reads=["Vb", "wT"], writes=[f"p_ap{half % 2}"])
                        S.op("vector",
                             lambda e, h=h, half=half, xiT_dst=xiT_dst: e.tensor_copy(
                                 xiT_dst[:, h * NT + half * 512:h * NT + (half + 1) * 512],
                                 p_ap[:, (half % 2) * 512:(half % 2) * 512 + 512]),
                             reads=[f"p_ap{half % 2}"], writes=[f"{dst_res}{h}"])

                    for step in range(3):
                        T = T_STEPS[step]
                        if step % 2 == 0:
                            xiT_src, src_res, xiT_dst, dst_res = xiT_a, "xiT_a_h", xiT_b, "xiT_b_h"
                        else:
                            xiT_src, src_res, xiT_dst, dst_res = xiT_b, "xiT_b_h", xiT_a, "xiT_a_h"
                        for h in range(8):
                            if h == 0:
                                for k in range(8):
                                    scores_pass0(step, h, k, xiT_src, src_res)
                            cur = {0: (ntau_a, "ntauA_a"), 1: (ntau_f, "ntauB_a")}
                            nxt = {0: (ntau_b, "ntauA_b"), 1: (ntau_g, "ntauB_b")}
                            for half in range(2):
                                lo = half * 4
                                S.op("vector",
                                     lambda e, half=half, lo=lo, t=cur[half][0]: e.tensor_scalar(
                                         t[:, lo:lo + 4], ymax[:, lo:lo + 4], -1.0, 1.0, ALU.mult, ALU.add),
                                     reads=[f"ymax{half}"], writes=[cur[half][1]])
                            for j in range(T):
                                for half in range(2):
                                    lo = half * 4
                                    ct, cres = cur[half]
                                    for kk in range(4):
                                        k = lo + kk
                                        rb = r_buf[k % 4]
                                        rres = f"rb{k % 4}"
                                        S.op("scalar",
                                             lambda e, k=k, rb=rb, ct=ct: e.activation(
                                                 rb[:], y_buf[:, k * NMEM:(k + 1) * NMEM], AF.Relu,
                                                 bias=ct[:, k:k + 1], scale=1.0, accum_out=src_c[:, k:k + 1]),
                                             reads=[f"y{k}", cres], writes=[rres, f"srcc{half}"])
                                        if k == 7:
                                            S.op("scalar",
                                                 lambda e, k=k, rb=rb: e.activation(
                                                     w_fin[1][:], rb[:], AF.Square, accum_out=sr2_c[:, k:k + 1]),
                                                 reads=[rres], writes=["wf1", f"sr2c{half}"])
                                        else:
                                            S.op("vector",
                                                 lambda e, k=k, rb=rb, ct=ct: e.scalar_tensor_tensor(
                                                     w_fin[0][:], y_buf[:, k * NMEM:(k + 1) * NMEM],
                                                     ct[:, k:k + 1], rb[:], ALU.add, ALU.mult,
                                                     accum_out=sr2_c[:, k:k + 1]),
                                                 reads=[f"y{k}", cres, rres], writes=["wf0", f"sr2c{half}"])
                                    nt, nres = nxt[half]
                                    S.op("vector", lambda e, lo=lo: e.tensor_scalar(c1[:, lo:lo + 4], src_c[:, lo:lo + 4], 1e-30, None, ALU.max, ALU.bypass),
                                         reads=[f"srcc{half}"], writes=[f"c1{half}"])
                                    S.op("vector", lambda e, lo=lo: e.reciprocal(c1[:, lo:lo + 4], c1[:, lo:lo + 4]),
                                         reads=[f"c1{half}"], writes=[f"c1{half}"])
                                    S.op("vector", lambda e, lo=lo: e.scalar_tensor_tensor(c2[:, lo:lo + 4], sr2_c[:, lo:lo + 4], -1.0, c1[:, lo:lo + 4], ALU.add, ALU.mult),
                                         reads=[f"sr2c{half}", f"c1{half}"], writes=[f"c2{half}"])
                                    S.op("vector",
                                         lambda e, lo=lo, ct=ct, nt=nt: e.scalar_tensor_tensor(
                                             nt[:, lo:lo + 4], c2[:, lo:lo + 4], -0.5, ct[:, lo:lo + 4], ALU.mult, ALU.add),
                                         reads=[f"c2{half}", cres], writes=[nres])
                                    cur[half], nxt[half] = nxt[half], cur[half]
                            nxt_head = h + 1 if h < 7 else None
                            final_relu(h, 0, cur[0][0], cur[0][1])
                            for k in range(8):
                                final_sq_tr(h, k)
                                if k < 7:
                                    kn = k + 1
                                    final_relu(h, kn, cur[kn // 4][0], cur[kn // 4][1])
                                final_evac(h, k)
                                if nxt_head is not None:
                                    scores_pass0(step, nxt_head, k, xiT_src, src_res)
                                if k == 3:
                                    apply_half(h, 0, xiT_dst, dst_res)
                            apply_half(h, 1, xiT_dst, dst_res)
                    xiT_fin = xiT_b if len(T_STEPS) % 2 == 1 else xiT_a
                    xiT_fin_res = "xiT_b_h" if len(T_STEPS) % 2 == 1 else "xiT_a_h"

            S.barrier()
            with ExitStack() as p4:
                woT_sb = sbuf(p4, "woT_sb", [128, 8 * D], BF16)
                wost = sbuf(p4, "wost", [128, D], F32)
                xrs = [sbuf(p4, f"xrs{i}", [128, D], F32) for i in range(2)]
                x1p = sbuf(p4, "x1p", [128, D], F32)
                x1c = sbuf(p4, "x1c", [128, D], BF16)
                with nc.psum_tensor("p_wo", [128, 2048], F32) as p_wo:
                    for f in range(8):
                        S.op("sync", lambda e, f=f: e.dma_start(out=wost[:], in_=wo_h[f * 128:(f + 1) * 128, :]),
                             writes=["wost"], dma=True)
                        S.op("vector", lambda e, f=f: e.tensor_copy(woT_sb[:, f * D:(f + 1) * D], wost[:]),
                             reads=["wost"], writes=["woT_sb"])
                    for k in range(8):
                        S.op("sync", lambda e, k=k: e.dma_start(out=xrs[k % 2][:], in_=x_h[k * 128:(k + 1) * 128, :]),
                             writes=[f"xrs{k % 2}"], dma=True)
                        def mm_fn(e, k=k):
                            for f in range(8):
                                for j in range(4):
                                    ins = e.matmul(p_wo[:, j * 512:(j + 1) * 512],
                                                   xiT_fin[:, f * NT + k * 128:f * NT + (k + 1) * 128],
                                                   woT_sb[:, f * D + j * 512:f * D + (j + 1) * 512],
                                                   start=(f == 0), stop=(f == 7))
                            return ins
                        S.op("tensor", mm_fn,
                             reads=[f"{xiT_fin_res}{hh}" for hh in range(8)] + ["woT_sb"], writes=["p_wo"])
                        S.op("vector", lambda e, k=k: e.tensor_tensor(x1p[:], xrs[k % 2][:], p_wo[:], ALU.add),
                             reads=[f"xrs{k % 2}", "p_wo"], writes=["x1p"])
                        cA_t, cB_t, cAr, cBr = ln_cols(S, x1p, "x1p", 4, k)
                        S.op("vector", lambda e, cA_t=cA_t: e.tensor_scalar(x1c[:], x1p[:], cA_t[:, 0:1], None, ALU.add, ALU.bypass),
                             reads=["x1p", cAr], writes=["x1c"])
                        S.op("vector",
                             lambda e, k=k, cB_t=cB_t: e.tensor_scalar(x1nat[:, k * D:(k + 1) * D], x1c[:], cB_t[:, 0:1], None, ALU.mult, ALU.bypass),
                             reads=["x1c", cBr], writes=[f"x1_{k}"])

        if debug:
            S.op("sync", lambda e: e.dma_start(out=dbg["dbg_x1"], in_=x1nat[:]),
                 reads=[f"x1_{kk}" for kk in range(8)], dma=True)
        S.barrier()
        with ExitStack() as p5:
            x1T = sbuf(p5, "x1T", [128, 16 * NT], BF16)
            x1T_v = x1T.ap().rearrange("p (cc t) -> p cc t", cc=16)
            acc = sbuf(p5, "acc", [128, 8 * D], BF16)
            b0b_bc = sbuf(p5, "b0b_bc", [128, D], BF16)
            b1b_bc = sbuf(p5, "b1b_bc", [128, D], BF16)
            bd_bc = sbuf(p5, "bd_bc", [128, D], BF16)
            with nc.psum_tensor("p_tr3", [128, 2048], BF16) as p_tr3:
                for k in range(8):
                    def tr_fn(e, k=k):
                        for j in range(16):
                            ins = e.matmul(p_tr3[:, j * 128:(j + 1) * 128],
                                           x1nat[:, k * D + j * 128:k * D + (j + 1) * 128], eye_s[:],
                                           is_transpose=True, start=(j % 8 == 0), stop=(j % 8 == 7),
                                           skip_group_check=True)
                        return ins
                    S.op("tensor", tr_fn, reads=[f"x1_{k}", "eye_s"], writes=["p_tr3"])
                    S.op("vector",
                         lambda e, k=k: e.tensor_copy(x1T_v[:, :, k * 128:(k + 1) * 128], p_tr3[:]),
                         reads=["p_tr3"], writes=["x1T"])
            S.barrier()
            vec5 = sbuf(p5, "vec5", [1, D], F32)
            with nc.psum_tensor("p_bc2", [128, 2048], F32) as p_bc2:
                bcast(b0b_bc, "b0b_bc", bb_h[0], D, p_bc2, "p_bc2", vec5, "vec5")
                bcast(b1b_bc, "b1b_bc", bb_h[1], D, p_bc2, "p_bc2", vec5, "vec5")
            S.barrier()
            S.op("vector", lambda e: e.tensor_tensor(bd_bc[:], b0b_bc[:], b1b_bc[:], ALU.subtract),
                 reads=["b0b_bc", "b1b_bc"], writes=["bd_bc"])
            S.barrier()
            with ExitStack() as p5g:
                wgst = sbuf(p5g, "wgst", [128, 16 * 2], F32)
                wgbf = sbuf(p5g, "wgbf", [128, 16 * 2], BF16)
                S.op("sync", lambda e: e.dma_start(out=wgst[:], in_=wg_h),
                     writes=["wgst"], dma=True)
                S.op("vector", lambda e: e.tensor_copy(wgbf[:], wgst[:]), reads=["wgst"], writes=["wgbf"])
                with nc.psum_tensor("p_g", [128, 2], F32) as p_g:
                    for k in range(8):
                        def mm_fn(e, k=k):
                            for cc in range(16):
                                ins = e.matmul(p_g[:, 0:2], x1T_v[:, cc, k * 128:(k + 1) * 128],
                                               wgbf[:, cc * 2:(cc + 1) * 2], start=(cc == 0), stop=(cc == 15))
                            return ins
                        S.op("tensor", mm_fn, reads=["x1T", "wgbf"], writes=["p_g"])
                        S.op("vector", lambda e: e.tensor_copy(c2[:, 0:2], p_g[:, 0:2]),
                             reads=["p_g"], writes=["c2"])
                        S.op("vector",
                             lambda e, k=k: e.tensor_tensor(mask0_c[:, k:k + 1], c2[:, 0:1], c2[:, 1:2], ALU.is_ge),
                             reads=["c2"], writes=["mask0_c"])
            S.barrier()
            S.op("vector", lambda e: e.tensor_scalar(mask1_c[:, 0:8], mask0_c[:, 0:8], -1.0, 1.0, ALU.mult, ALU.add),
                 reads=["mask0_c"], writes=["mask1_c"])

            S.barrier()
            with ExitStack() as p5e:
                h_sup = sbuf(p5e, "h_sup", [128, 8 * NT], BF16)
                wast = [sbuf(p5e, f"wast{i}", [128, 2048], F32) for i in range(2)]
                wabf = sbuf(p5e, "wabf", [128, 2048], BF16)
                wbst = [sbuf(p5e, f"wbst{i}", [128, D], F32) for i in range(2)]
                wbbf = sbuf(p5e, "wbbf", [128, 8 * D], BF16)
                ba_sb = [sbuf(p5e, f"ba_sb{i}", [128, 32], F32) for i in range(2)]
                S.op("sync", lambda e: e.dma_start(out=ba_sb[0][:], in_=ba_h[0]), writes=["ba0"], dma=True)
                S.op("sync", lambda e: e.dma_start(out=ba_sb[1][:], in_=ba_h[1]), writes=["ba1"], dma=True)
                with (nc.psum_tensor("p_h", [128, 1024], F32) as p_h,
                      nc.psum_tensor("p_o", [128, 2048], F32) as p_o):
                    for e_idx in range(2):
                        mcol = mask0_c if e_idx == 0 else mask1_c
                        mres = "mask0_c" if e_idx == 0 else "mask1_c"
                        actf = AF.Gelu if e_idx == 0 else AF.Relu
                        basb = ba_sb[e_idx]
                        bares = f"ba{e_idx}"
                        for fs in range(4):
                            for fc8 in range(8):
                                fc = fs * 8 + fc8
                                wa_i = fc % 2
                                S.op("sync",
                                     lambda e, fc=fc, wa_i=wa_i, e_idx=e_idx: e.dma_start(
                                         out=wast[wa_i][:], in_=wa_h[e_idx][fc]),
                                     writes=[f"wast{wa_i}"], dma=True)
                                S.op("vector", lambda e, wa_i=wa_i: e.tensor_copy(wabf[:], wast[wa_i][:]),
                                     reads=[f"wast{wa_i}"], writes=["wabf"])
                                for th in range(2):
                                    def mm_fn(e, th=th):
                                        for cc in range(16):
                                            ins = e.matmul(p_h[:, th * 512:(th + 1) * 512],
                                                           wabf[:, cc * 128:(cc + 1) * 128],
                                                           x1T_v[:, cc, th * 512:(th + 1) * 512],
                                                           start=(cc == 0), stop=(cc == 15))
                                        return ins
                                    pr = f"p_h{th}"
                                    S.op("tensor", mm_fn, reads=["wabf", "x1T"], writes=[pr])
                                S.op("scalar",
                                     lambda e, fc=fc, fc8=fc8, actf=actf, basb=basb: e.activation(
                                         h_sup[:, fc8 * NT:(fc8 + 1) * NT], p_h[:], actf,
                                         bias=basb[:, fc:fc + 1], scale=1.0),
                                     reads=["p_h0", "p_h1", bares], writes=[f"hs{fc8}"])
                            for fc8 in range(8):
                                fc = fs * 8 + fc8
                                wb_i = fc8 % 2
                                S.op("sync",
                                     lambda e, fc=fc, wb_i=wb_i, e_idx=e_idx: e.dma_start(
                                         out=wbst[wb_i][:], in_=wb_h[e_idx][fc * 128:(fc + 1) * 128, :]),
                                     writes=[f"wbst{wb_i}"], dma=True)
                                S.op("vector",
                                     lambda e, fc8=fc8, wb_i=wb_i: e.tensor_copy(wbbf[:, fc8 * D:(fc8 + 1) * D], wbst[wb_i][:]),
                                     reads=[f"wbst{wb_i}"], writes=[f"wbbf{fc8}"])
                            for k in range(8):
                                for j in range(4):
                                    def mm_fn(e, k=k, j=j):
                                        for fc8 in range(8):
                                            ins = e.matmul(p_o[:, j * 512:(j + 1) * 512],
                                                           h_sup[:, fc8 * NT + k * 128:fc8 * NT + (k + 1) * 128],
                                                           wbbf[:, fc8 * D + j * 512:fc8 * D + (j + 1) * 512],
                                                           start=(fc8 == 0), stop=(fc8 == 7))
                                        return ins
                                    S.op("tensor", mm_fn,
                                         reads=[f"hs{i}" for i in range(8)] + [f"wbbf{i}" for i in range(8)],
                                         writes=[f"p_o{j}"])
                                    if e_idx == 0 and fs == 0:
                                        S.op("vector",
                                             lambda e, k=k, j=j, mcol=mcol: e.tensor_scalar(
                                                 acc[:, k * D + j * 512:k * D + (j + 1) * 512], p_o[:, j * 512:(j + 1) * 512],
                                                 mcol[:, k:k + 1], None, ALU.mult, ALU.bypass),
                                             reads=[f"p_o{j}", mres], writes=[f"acc{k}"])
                                    else:
                                        S.op("vector",
                                             lambda e, k=k, j=j, mcol=mcol: e.scalar_tensor_tensor(
                                                 acc[:, k * D + j * 512:k * D + (j + 1) * 512], p_o[:, j * 512:(j + 1) * 512],
                                                 mcol[:, k:k + 1],
                                                 acc[:, k * D + j * 512:k * D + (j + 1) * 512], ALU.mult, ALU.add),
                                             reads=[f"p_o{j}", mres, f"acc{k}"], writes=[f"acc{k}"])
            if debug:
                S.op("sync", lambda e: e.dma_start(out=dbg["dbg_acc"], in_=acc[:]),
                     reads=[f"acc{kk}" for kk in range(8)], dma=True)
            S.barrier()
            with ExitStack() as p5f:
                x2p = sbuf(p5f, "x2p", [128, D], F32)
                bmix = sbuf(p5f, "bmix", [128, D], BF16)
                outt = sbuf(p5f, "outt", [128, D], F32)
                for k in range(8):
                    S.op("vector",
                         lambda e, k=k: e.scalar_tensor_tensor(bmix[:], bd_bc[:], mask0_c[:, k:k + 1], b1b_bc[:],
                                                               ALU.mult, ALU.add),
                         reads=["bd_bc", "mask0_c", "b1b_bc"], writes=["bmix"])
                    S.op("vector", lambda e, k=k: e.tensor_tensor(x2p[:], x1nat[:, k * D:(k + 1) * D], acc[:, k * D:(k + 1) * D], ALU.add),
                         reads=[f"x1_{k}", f"acc{k}"], writes=["x2p"])
                    S.op("vector", lambda e: e.tensor_tensor(x2p[:], x2p[:], bmix[:], ALU.add),
                         reads=["x2p", "bmix"], writes=["x2p"])
                    cA_t, cB_t, cAr, cBr = ln_cols(S, x2p, "x2p", 4, k)
                    S.op("vector", lambda e, cA_t=cA_t: e.tensor_scalar(x2p[:], x2p[:], cA_t[:, 0:1], None, ALU.add, ALU.bypass),
                         reads=["x2p", cAr], writes=["x2p"])
                    S.op("vector", lambda e, cB_t=cB_t: e.tensor_scalar(outt[:], x2p[:], cB_t[:, 0:1], None, ALU.mult, ALU.bypass),
                         reads=["x2p", cBr], writes=["outt"])
                    S.op("sync", lambda e, k=k: e.dma_start(out=out_h[k * 128:(k + 1) * 128, :], in_=outt[:]),
                         reads=["outt"], writes=[f"store{k}"], dma=True)

        with ExitStack() as semctx:
            sems = {}
            for e in ENGS:
                sems[e] = semctx.enter_context(nc.semaphore(f"s_{e}"))
            for i in range(NDMA):
                sems[f"dma{i}"] = semctx.enter_context(nc.semaphore(f"s_dma{i}"))
            block = semctx.enter_context(nc.Block())
            dma_final = [(f"dma{i}", 16 * S.dma_use[i]) for i in range(NDMA) if S.dma_use[i]]

            @block.sync
            def _(sync):
                S.emit("sync", sync, sems)
                for nm, val in dma_final:
                    sync.wait_ge(sems[nm], val)

            @block.vector
            def _(vector):
                S.emit("vector", vector, sems)

            @block.scalar
            def _(scalar):
                S.emit("scalar", scalar, sems)

            @block.tensor
            def _(tensor):
                S.emit("tensor", tensor, sems)

            if S.q["gpsimd"]:
                @block.gpsimd
                def _(gpsimd):
                    S.emit("gpsimd", gpsimd, sems)
    return nc


_CACHE = {}


def kernel(**inputs):
    f32 = np.float32
    x = np.asarray(inputs["L0_attn_hidden"], f32)
    mem = np.ascontiguousarray(np.asarray(inputs["L0_attn_memory"], f32))
    wq = np.asarray(inputs["L0_attn_Wq"], f32)
    wk = np.asarray(inputs["L0_attn_Wk"], f32)
    wv = np.asarray(inputs["L0_attn_Wv"], f32)
    wo = np.asarray(inputs["L0_attn_Wo"], f32)
    lb = np.ascontiguousarray(np.asarray(inputs["L0_attn_log_beta"], f32).reshape(1, NH))
    gs = np.ascontiguousarray(np.asarray(inputs["L0_attn_g_state"], f32).reshape(1, D))
    bs = np.ascontiguousarray(np.asarray(inputs["L0_attn_b_state"], f32).reshape(1, D))
    gm = np.ascontiguousarray(np.asarray(inputs["L0_attn_g_mem"], f32).reshape(1, MD))
    bm = np.ascontiguousarray(np.asarray(inputs["L0_attn_b_mem"], f32).reshape(1, MD))
    wg = np.asarray(inputs["L0_ffn_Wg"], f32)
    w0a = np.asarray(inputs["L0_ffn_W0a"], f32)
    w1a = np.asarray(inputs["L0_ffn_W1a"], f32)
    w0b = np.asarray(inputs["L0_ffn_W0b"], f32)
    w1b = np.asarray(inputs["L0_ffn_W1b"], f32)
    b0a = np.asarray(inputs["L0_ffn_b0a"], f32)
    b1a = np.asarray(inputs["L0_ffn_b1a"], f32)
    b0b = np.ascontiguousarray(np.asarray(inputs["L0_ffn_b0b"], f32).reshape(1, D))
    b1b = np.ascontiguousarray(np.asarray(inputs["L0_ffn_b1b"], f32).reshape(1, D))

    wqT = np.ascontiguousarray(wq.T.reshape(16, 128, 8, 128).transpose(2, 1, 0, 3).reshape(8, 128, 2048))
    wkT = np.ascontiguousarray(wk.T)
    wvT = np.ascontiguousarray(wv.T)
    woT = np.ascontiguousarray(wo.T)
    wgT = np.ascontiguousarray(wg.T.reshape(16, 128, 2).transpose(1, 0, 2).reshape(128, 32))
    w0aT = np.ascontiguousarray(w0a.T.reshape(16, 128, 32, 128).transpose(2, 1, 0, 3).reshape(32, 128, 2048))
    w1aT = np.ascontiguousarray(w1a.T.reshape(16, 128, 32, 128).transpose(2, 1, 0, 3).reshape(32, 128, 2048))
    w0bT = np.ascontiguousarray(w0b.T)
    w1bT = np.ascontiguousarray(w1b.T)
    b0a_c = np.ascontiguousarray(b0a.reshape(32, 128).T)
    b1a_c = np.ascontiguousarray(b1a.reshape(32, 128).T)
    ones1 = np.ones((1, 128), f32)
    eye = np.eye(128).astype(bfnp)

    if "nc" not in _CACHE:
        _CACHE["nc"] = build_program()
    nc = _CACHE["nc"]

    in_maps = []
    for c in range(8):
        b, sh = c // 2, c % 2
        xs = np.ascontiguousarray(x[b, sh * NT:(sh + 1) * NT, :])
        in_maps.append({
            "x": xs, "mem": mem, "wqT": wqT, "wkT": wkT, "wvT": wvT, "woT": woT,
            "wgT": wgT, "w0aT": w0aT, "w1aT": w1aT, "w0bT": w0bT, "w1bT": w1bT,
            "log_beta": lb, "g_state": gs, "b_state": bs, "g_mem": gm, "b_mem": bm,
            "b0a": b0a_c, "b1a": b1a_c, "b0b": b0b, "b1b": b1b,
            "ones1": ones1, "eye": eye,
        })
    res = run_bass_kernel_spmd(nc, in_maps, core_ids=list(range(8)), trace_cores=list(range(8)))
    out = np.empty((4, 2048, D), np.float32)
    for c in range(8):
        b, sh = c // 2, c % 2
        out[b, sh * NT:(sh + 1) * NT, :] = res.results[c]["out"]
    return out


# version 22
# speedup vs baseline: 1.1244x; 1.0828x over previous
from contextlib import ExitStack

import numpy as np
import ml_dtypes

import concourse.bass as bass
import concourse.mybir as mybir
from concourse.bass_utils import run_bass_kernel_spmd

F32 = mybir.dt.float32
BF16 = mybir.dt.bfloat16
AF = mybir.ActivationFunctionType
ALU = mybir.AluOpType
bfnp = ml_dtypes.bfloat16

T_STEPS = (5, 6, 7)
USE_FIN = True

NT = 1024
D = 2048
NH = 8
NMEM = 2048
MD = 1024
FF = 4096
EPS = 1e-5
LN_HALF = float(np.log(0.5))

ENGS = ("sync", "vector", "scalar", "tensor", "gpsimd")
NDMA = 8


class KB:

    def __init__(self):
        self.q = {e: [] for e in ENGS}
        self.cnt = {e: 0 for e in ENGS}
        self.dma_i = 0
        self.dma_use = [0] * NDMA
        self.res = {}
        self.barrier_deps = []

    def barrier(self):
        self.barrier_deps = [(e, self.cnt[e]) for e in ENGS if self.cnt[e] > 0]
        self.barrier_deps += [(f"dma{i}", 16 * self.dma_use[i]) for i in range(NDMA) if self.dma_use[i] > 0]

    def _res(self, name):
        return self.res.setdefault(name, {"w": None, "r": []})

    def op(self, eng, fn, reads=(), writes=(), deps=(), dma=False):
        dep = list(deps) + self.barrier_deps
        pre = []
        if dma:
            s = self.dma_i % NDMA
            self.dma_i += 1
            if self.dma_use[s] > 0:
                pre.append((f"dma{s}", 16 * self.dma_use[s]))
            self.dma_use[s] += 1
            tok = (f"dma{s}", 16 * self.dma_use[s])
            sem_tag = f"dma{s}"
        else:
            tok = (eng, self.cnt[eng] + 1)
            self.cnt[eng] = tok[1]
            sem_tag = eng
        for nm in reads:
            r = self._res(nm)
            if r["w"] is not None:
                dep.append(r["w"])
        for nm in writes:
            r = self._res(nm)
            if r["w"] is not None:
                dep.append(r["w"])
            dep.extend(r["r"])
        for nm in reads:
            self._res(nm)["r"].append(tok)
        for nm in writes:
            r = self._res(nm)
            r["w"] = tok
            r["r"] = []
        self.q[eng].append((fn, tuple(dep), tuple(pre), sem_tag, tok[1]))
        return tok

    SELF_SYNC = True

    def emit(self, eng_name, eng, sems):
        waited = {}
        for fn, deps, pre, sem_tag, tokval in self.q[eng_name]:
            need = {}
            for (de, dv) in list(deps) + list(pre):
                if de == eng_name and not self.SELF_SYNC:
                    continue
                if dv > waited.get(de, 0):
                    need[de] = max(need.get(de, 0), dv)
            for de in sorted(need):
                eng.wait_ge(sems[de], need[de])
                waited[de] = need[de]
            ins = fn(eng)
            if sem_tag.startswith("dma"):
                ins.then_inc(sems[sem_tag], 16)
            elif hasattr(eng, "engine_nop"):
                eng.engine_nop().then_inc(sems[sem_tag], 1)
            else:
                ins.then_inc(sems[sem_tag], 1)


def build_program(detect_races=True, debug=False):
    nc = bass.Bass("TRN2", target_bir_lowering=False, detect_race_conditions=detect_races)

    def dram(name, shape, dt=F32, kind="ExternalInput"):
        return nc.dram_tensor(name, shape, dt, kind=kind).ap()

    x_h = dram("x", [NT, D])
    mem_h = dram("mem", [NMEM, MD])
    wq_h = dram("wqT", [8, 128, 16 * 128])
    wk_h = dram("wkT", [MD, MD])
    wv_h = dram("wvT", [MD, MD])
    wo_h = dram("woT", [NH * 128, D])
    wg_h = dram("wgT", [128, 32])
    wa_h = [dram("w0aT", [32, 128, 16 * 128]), dram("w1aT", [32, 128, 16 * 128])]
    wb_h = [dram("w0bT", [FF, D]), dram("w1bT", [FF, D])]
    lb_h = dram("log_beta", [1, NH])
    gs_h = dram("g_state", [1, D])
    bs_h = dram("b_state", [1, D])
    gm_h = dram("g_mem", [1, MD])
    bm_h = dram("b_mem", [1, MD])
    ba_h = [dram("b0a", [128, 32]), dram("b1a", [128, 32])]
    bb_h = [dram("b0b", [1, D]), dram("b1b", [1, D])]
    ones_h = dram("ones1", [1, 128])
    eye_h = dram("eye", [128, 128], BF16)
    out_h = dram("out", [NT, D], kind="ExternalOutput")
    dbg = {}
    if debug:
        for nm, sh, dt in (("dbg_KT", [128, NH * NMEM], BF16), ("dbg_Vb", [128, 16 * MD], BF16),
                           ("dbg_xi0", [128, NH * NT], BF16), ("dbg_xi1", [128, NH * NT], BF16),
                           ("dbg_xi2", [128, NH * NT], BF16), ("dbg_xi3", [128, NH * NT], BF16),
                           ("dbg_x1", [128, 8 * D], BF16), ("dbg_acc", [128, 8 * D], BF16),
                           ("dbg_mnT", [128, 8 * NMEM], BF16), ("dbg_ag", [128, 2], F32),
                           ("dbg_colB", [128, 8], F32), ("dbg_msc", [128, MD], F32),
                           ("dbg_mnbf", [128, MD], BF16)):
            dbg[nm] = dram(nm, sh, dt, kind="ExternalOutput")

    S = KB()

    def mk_ln_cols(lnbufs):
        def ln_cols(S, src_ap, src_res, nchunks, par):
            p = par % 2
            bnst, colA, colB = lnbufs[p]
            bn, cA, cB = f"bnst{p}", f"colA{p}", f"colB{p}"
            def st_fn(e):
                for i in range(nchunks):
                    ins = e.bn_stats(bnst[:, i * 6:(i + 1) * 6], src_ap[:, i * 512:(i + 1) * 512])
                return ins
            S.op("vector", st_fn, reads=[src_res], writes=[bn])
            S.op("vector", lambda e: e.bn_aggr(colA[:, 6:8], bnst[:, 0:6 * nchunks]),
                 reads=[bn], writes=[cA])
            S.op("vector", lambda e: e.tensor_scalar(colA[:, 0:1], colA[:, 6:7], -1.0, None, ALU.mult, ALU.bypass),
                 reads=[cA], writes=[cA])
            S.op("vector", lambda e: e.tensor_scalar(colB[:, 0:1], colA[:, 7:8], EPS, None, ALU.add, ALU.bypass),
                 reads=[cA], writes=[cB])
            S.op("scalar", lambda e: e.activation(colB[:, 0:1], colB[:, 0:1], AF.Ln),
                 reads=[cB], writes=[cB])
            S.op("scalar", lambda e: e.activation(colB[:, 0:1], colB[:, 0:1], AF.Exp, scale=-0.5),
                 reads=[cB], writes=[cB])
            return colA, colB, cA, cB
        return ln_cols

    with ExitStack() as ctx:
        def sbuf(stack, name, shape, dt):
            return stack.enter_context(nc.sbuf_tensor(name, shape, dt))

        ones_s = sbuf(ctx, "ones_s", [1, 128], F32)
        eye_s = sbuf(ctx, "eye_s", [128, 128], BF16)
        lb_s = sbuf(ctx, "lb_s", [1, NH], F32)
        ln2_s = sbuf(ctx, "ln2_s", [1, 1], F32)
        beta_bc = sbuf(ctx, "beta_bc", [128, NH], F32)
        lnbufs = [(sbuf(ctx, f"bnst{i}", [128, 24], F32),
                   sbuf(ctx, f"colA{i}", [128, 8], F32),
                   sbuf(ctx, f"colB{i}", [128, 8], F32)) for i in range(2)]
        ln_cols = mk_ln_cols(lnbufs)
        ymax = sbuf(ctx, "ymax", [128, 8], F32)
        ntau_a = sbuf(ctx, "ntau_a", [128, 8], F32)
        ntau_b = sbuf(ctx, "ntau_b", [128, 8], F32)
        ntau_f = sbuf(ctx, "ntau_f", [128, 8], F32)
        ntau_g = sbuf(ctx, "ntau_g", [128, 8], F32)
        src_c = sbuf(ctx, "src_c", [128, 8], F32)
        sr2_c = sbuf(ctx, "sr2_c", [128, 8], F32)
        kc_c = sbuf(ctx, "kc_c", [128, 8], F32)
        c1 = sbuf(ctx, "c1", [128, 8], F32)
        c2 = sbuf(ctx, "c2", [128, 8], F32)
        c3 = sbuf(ctx, "c3", [128, 8], F32)
        mask0_c = sbuf(ctx, "mask0_c", [128, 8], F32)
        mask1_c = sbuf(ctx, "mask1_c", [128, 8], F32)
        x1nat = sbuf(ctx, "x1nat", [128, 8 * D], BF16)

        S.op("sync", lambda e: e.dma_start(out=ones_s[:], in_=ones_h), writes=["ones_s"], dma=True)
        S.op("sync", lambda e: e.dma_start(out=eye_s[:], in_=eye_h), writes=["eye_s"], dma=True)
        S.op("sync", lambda e: e.dma_start(out=lb_s[:], in_=lb_h), writes=["lb_s"], dma=True)
        S.op("vector", lambda e: e.memset(ln2_s[:], LN_HALF), writes=["ln2_s"])
        S.op("scalar", lambda e: e.activation(lb_s[:], lb_s[:], AF.Exp, bias=ln2_s[0:1, 0:1], scale=1.0),
             reads=["ln2_s"], writes=["lb_s"])
        with nc.psum_tensor("p_b0", [128, 8], F32) as p_b0:
            S.op("tensor", lambda e: e.matmul(p_b0[:, 0:8], ones_s[0:1, :], lb_s[0:1, :], start=True, stop=True),
                 reads=["ones_s", "lb_s"], writes=["p_b0"])
            S.op("vector", lambda e: e.tensor_copy(beta_bc[:], p_b0[:, 0:8]),
                 reads=["p_b0"], writes=["beta_bc"])

        S.barrier()
        def bcast(dst, dres, row_h, width, psum, pres, vec_s, vres):
            S.op("sync", lambda e: e.dma_start(out=vec_s[0:1, 0:width], in_=row_h),
                 writes=[vres], dma=True)
            for j0 in range(0, width, 512):
                S.op("tensor",
                     lambda e, j0=j0: e.matmul(psum[:, j0 % 2048:j0 % 2048 + 512], ones_s[0:1, :],
                                               vec_s[0:1, j0:j0 + 512], start=True, stop=True),
                     reads=["ones_s", vres], writes=[f"{pres}{j0 % 2048}"])
                S.op("vector", lambda e, j0=j0: e.tensor_copy(dst[:, j0:j0 + 512], psum[:, j0 % 2048:j0 % 2048 + 512]),
                     reads=[f"{pres}{j0 % 2048}"], writes=[dres])

        with ExitStack() as attn:
            KT = sbuf(attn, "KT", [128, NH * NMEM], BF16)
            Vb = sbuf(attn, "Vb", [128, 16 * MD], BF16)
            xiT_a = sbuf(attn, "xiT_a", [128, NH * NT], BF16)
            xiT_b = sbuf(attn, "xiT_b", [128, NH * NT], BF16)

            with ExitStack() as p1:
                mnT = sbuf(p1, "mnT", [128, 8 * NMEM], BF16)
                mnT_v = mnT.ap().rearrange("p (mc n) -> p mc n", mc=8)
                gm_bc = sbuf(p1, "gm_bc", [128, MD], BF16)
                bm_bc = sbuf(p1, "bm_bc", [128, MD], BF16)
                with ExitStack() as p1a:
                    mst = sbuf(p1a, "mst", [128, 4 * MD], F32)
                    msc2 = [sbuf(p1a, f"msc{i}", [128, MD], F32) for i in range(2)]
                    mnbf = [sbuf(p1a, f"mnbf{i}", [128, MD], BF16) for i in range(2)]
                    vec1 = sbuf(p1a, "vec1", [1, MD], F32)
                    with nc.psum_tensor("p_m", [128, 1024], F32) as p_m:
                        bcast(gm_bc, "gm_bc", gm_h, MD, p_m, "p_m", vec1, "vec1")
                        bcast(bm_bc, "bm_bc", bm_h, MD, p_m, "p_m", vec1, "vec1")
                    S.barrier()
                    with nc.psum_tensor("p_tr1", [128, 1024], BF16) as p_tr1:
                        for half in range(4):
                            mres = "mst"
                            S.op("sync",
                                 lambda e, half=half: e.dma_start(
                                     out=mst.ap().rearrange("p (g m) -> p g m", g=4),
                                     in_=mem_h.rearrange("(g p) m -> p g m", p=128)[:, half * 4:(half + 1) * 4, :]),
                                 writes=[mres], dma=True)
                            for it in range(4):
                                nt_i = half * 4 + it
                                mtile = mst[:, it * MD:(it + 1) * MD]
                                cA_t, cB_t, cAr, cBr = ln_cols(S, mtile, mres, 2, nt_i)
                                msc = msc2[nt_i % 2]
                                mscr = f"msc{nt_i % 2}"
                                S.op("vector", lambda e, mtile=mtile, cA_t=cA_t, msc=msc: e.tensor_scalar(msc[:], mtile, cA_t[:, 0:1], None, ALU.add, ALU.bypass),
                                     reads=[mres, cAr], writes=[mscr])
                                S.op("vector", lambda e, cB_t=cB_t, msc=msc: e.scalar_tensor_tensor(msc[:], msc[:], cB_t[:, 0:1], gm_bc[:], ALU.mult, ALU.mult),
                                     reads=[mscr, cBr, "gm_bc"], writes=[mscr])
                                mb = mnbf[nt_i % 2]
                                mbres = f"mnbf{nt_i % 2}"
                                S.op("vector", lambda e, mb=mb, msc=msc: e.tensor_tensor(mb[:], msc[:], bm_bc[:], ALU.add),
                                     reads=[mscr, "bm_bc"], writes=[mbres])
                                def tr_fn(e, mb=mb):
                                    for j in range(8):
                                        ins = e.matmul(p_tr1[:, j * 128:(j + 1) * 128], mb[:, j * 128:(j + 1) * 128], eye_s[:],
                                                       is_transpose=True, start=(j == 0), stop=(j == 7), skip_group_check=True)
                                    return ins
                                S.op("tensor", tr_fn, reads=[mbres, "eye_s"], writes=["p_tr1"])
                                S.op("vector",
                                     lambda e, nt_i=nt_i: e.tensor_copy(
                                         mnT_v[:, :, nt_i * 128:(nt_i + 1) * 128], p_tr1[:]),
                                     reads=["p_tr1"], writes=["mnT"])
                                if debug and nt_i == 15:
                                    S.op("sync", lambda e: e.dma_start(out=dbg["dbg_ag"], in_=bnag[:]), reads=["bnag"], dma=True)
                                    S.op("sync", lambda e: e.dma_start(out=dbg["dbg_colB"], in_=colB[:]), reads=["colB"], dma=True)
                                    S.op("sync", lambda e: e.dma_start(out=dbg["dbg_msc"], in_=msc[:]), reads=["msc"], dma=True)
                                    S.op("sync", lambda e, mb=mb: e.dma_start(out=dbg["dbg_mnbf"], in_=mb[:]), reads=[mbres], dma=True)
                if debug:
                    S.op("sync", lambda e: e.dma_start(out=dbg["dbg_mnT"], in_=mnT[:]), reads=["mnT"], dma=True)
                S.barrier()
                for wgt_h, name in ((wk_h, "K"), (wv_h, "V")):
                    S.barrier()
                    with ExitStack() as pw:
                        wst = sbuf(pw, "wst_" + name, [128, 4 * MD], F32)
                        wbf = sbuf(pw, "wbf_" + name, [128, 8 * MD], BF16)
                        for hf in range(2):
                            S.op("sync",
                                 lambda e, wgt_h=wgt_h, hf=hf: e.dma_start(
                                     out=wst.ap().rearrange("p (mc f) -> p mc f", mc=4),
                                     in_=wgt_h.rearrange("(mc p) f -> p mc f", p=128)[:, hf * 4:(hf + 1) * 4, :]),
                                 writes=["wst"], dma=True)
                            for i in range(4):
                                S.op("vector",
                                     lambda e, i=i, hf=hf: e.tensor_copy(
                                         wbf[:, (hf * 4 + i) * MD:(hf * 4 + i + 1) * MD], wst[:, i * MD:(i + 1) * MD]),
                                     reads=["wst"], writes=["wbf"])
                        with nc.psum_tensor("p_kv_" + name, [128, 1024], F32) as p_kv:
                            if name == "K":
                                for f in range(8):
                                    for ncol in range(4):
                                        def mm_fn(e, f=f, ncol=ncol):
                                            for mc in range(8):
                                                ins = e.matmul(p_kv[:, (ncol % 2) * 512:(ncol % 2) * 512 + 512],
                                                               wbf[:, mc * MD + f * 128:mc * MD + (f + 1) * 128],
                                                               mnT_v[:, mc, ncol * 512:(ncol + 1) * 512],
                                                               start=(mc == 0), stop=(mc == 7))
                                            return ins
                                        pr = f"p_kv{ncol % 2}"
                                        S.op("tensor", mm_fn, reads=["wbf", "mnT"], writes=[pr])
                                        S.op("vector",
                                             lambda e, f=f, ncol=ncol: e.tensor_copy(
                                                 KT[:, f * NMEM + ncol * 512:f * NMEM + (ncol + 1) * 512],
                                                 p_kv[:, (ncol % 2) * 512:(ncol % 2) * 512 + 512]),
                                             reads=[pr], writes=["KT"])
                            else:
                                for ntile in range(16):
                                    for fcol in range(2):
                                        def mm_fn(e, ntile=ntile, fcol=fcol):
                                            for mc in range(8):
                                                ins = e.matmul(p_kv[:, fcol * 512:(fcol + 1) * 512],
                                                               mnT_v[:, mc, ntile * 128:(ntile + 1) * 128],
                                                               wbf[:, mc * MD + fcol * 512:mc * MD + (fcol + 1) * 512],
                                                               start=(mc == 0), stop=(mc == 7))
                                            return ins
                                        pr = f"p_kv{fcol}"
                                        S.op("tensor", mm_fn, reads=["wbf", "mnT"], writes=[pr])
                                        S.op("vector",
                                             lambda e, ntile=ntile, fcol=fcol: e.tensor_copy(
                                                 Vb[:, ntile * MD + fcol * 512:ntile * MD + (fcol + 1) * 512],
                                                 p_kv[:, fcol * 512:(fcol + 1) * 512]),
                                             reads=[pr], writes=["Vb"])

            if debug:
                S.op("sync", lambda e: e.dma_start(out=dbg["dbg_KT"], in_=KT[:]), reads=["KT"], dma=True)
                S.op("sync", lambda e: e.dma_start(out=dbg["dbg_Vb"], in_=Vb[:]), reads=["Vb"], dma=True)
            S.barrier()
            with ExitStack() as p2:
                gs_bc = sbuf(p2, "gs_bc", [128, D], BF16)
                bs_bc = sbuf(p2, "bs_bc", [128, D], BF16)
                lnxTh = sbuf(p2, "lnxTh", [128, 16 * 512], BF16)
                lnxT_v = lnxTh.ap().rearrange("p (cc t) -> p cc t", cc=16)
                xst = [sbuf(p2, f"xst{i}", [128, D], F32) for i in range(2)]
                lnbf = [sbuf(p2, f"lnbf{i}", [128, D], BF16) for i in range(2)]
                wqst = sbuf(p2, "wqst", [128, 16 * 128], F32)
                wqbf = sbuf(p2, "wqbf", [128, 16 * 128], BF16)
                vec2 = sbuf(p2, "vec2", [1, D], F32)
                with nc.psum_tensor("p_bb", [128, 2048], F32) as p_bb:
                    bcast(gs_bc, "gs_bc", gs_h, D, p_bb, "p_bb", vec2, "vec2")
                    bcast(bs_bc, "bs_bc", bs_h, D, p_bb, "p_bb", vec2, "vec2")
                S.barrier()
                with (nc.psum_tensor("p_tr2", [128, 2048], BF16) as p_tr2,
                      nc.psum_tensor("p_q", [128, 1024], F32) as p_q):
                    for half in range(2):
                        for kk in range(4):
                            k = half * 4 + kk
                            xs = xst[kk % 2]
                            xres = f"xst{kk % 2}"
                            S.op("sync", lambda e, xs=xs, k=k: e.dma_start(out=xs[:], in_=x_h[k * 128:(k + 1) * 128, :]),
                                 writes=[xres], dma=True)
                            cA_t, cB_t, cAr, cBr = ln_cols(S, xs, xres, 4, k)
                            S.op("vector", lambda e, xs=xs, cA_t=cA_t: e.tensor_scalar(xs[:], xs[:], cA_t[:, 0:1], None, ALU.add, ALU.bypass),
                                 reads=[xres, cAr], writes=[xres])
                            S.op("vector", lambda e, xs=xs, cB_t=cB_t: e.scalar_tensor_tensor(xs[:], xs[:], cB_t[:, 0:1], gs_bc[:], ALU.mult, ALU.mult),
                                 reads=[xres, cBr, "gs_bc"], writes=[xres])
                            lb2 = lnbf[kk % 2]
                            lbres = f"lnbf{kk % 2}"
                            S.op("vector", lambda e, lb2=lb2, xs=xs: e.tensor_tensor(lb2[:], xs[:], bs_bc[:], ALU.add),
                                 reads=[xres, "bs_bc"], writes=[lbres])
                            def tr_fn(e, lb2=lb2):
                                for j in range(16):
                                    ins = e.matmul(p_tr2[:, j * 128:(j + 1) * 128], lb2[:, j * 128:(j + 1) * 128], eye_s[:],
                                                   is_transpose=True, start=(j % 8 == 0), stop=(j % 8 == 7), skip_group_check=True)
                                return ins
                            S.op("tensor", tr_fn, reads=[lbres, "eye_s"], writes=["p_tr2"])
                            S.op("vector",
                                 lambda e, kk=kk: e.tensor_copy(lnxT_v[:, :, kk * 128:(kk + 1) * 128], p_tr2[:]),
                                 reads=["p_tr2"], writes=["lnxTh"])
                        for h in range(8):
                            S.op("sync", lambda e, h=h: e.dma_start(out=wqst[:], in_=wq_h[h]),
                                 writes=["wqst"], dma=True)
                            for i in range(4):
                                S.op("vector", lambda e, i=i: e.tensor_copy(wqbf[:, i * 512:(i + 1) * 512], wqst[:, i * 512:(i + 1) * 512]),
                                     reads=["wqst"], writes=["wqbf"])
                            def mm_fn(e):
                                for cc in range(16):
                                    ins = e.matmul(p_q[:, 0:512],
                                                   wqbf[:, cc * 128:(cc + 1) * 128],
                                                   lnxT_v[:, cc, 0:512],
                                                   start=(cc == 0), stop=(cc == 15))
                                return ins
                            S.op("tensor", mm_fn, reads=["wqbf", "lnxTh"], writes=["p_q"])
                            S.op("vector",
                                 lambda e, h=h, half=half: e.tensor_copy(
                                     xiT_a[:, h * NT + half * 512:h * NT + half * 512 + 512], p_q[:, 0:512]),
                                 reads=["p_q"], writes=[f"xiT_a_h{h}"])

            S.barrier()
            with ExitStack() as p3:
                y_buf = sbuf(p3, "y_buf", [128, 8 * NMEM], BF16)
                r_buf = [sbuf(p3, f"r_buf{i}", [128, NMEM], BF16) for i in range(4)]
                sdump = sbuf(p3, "sdump", [128, NMEM], BF16)
                w_fin = [sbuf(p3, f"w_fin{i}", [128, NMEM], BF16) for i in range(2)]
                wT_buf = sbuf(p3, "wT_buf", [128, 16 * 512], BF16)
                wT_v = wT_buf.ap().rearrange("p (nc t) -> p nc t", nc=16)
                with (nc.psum_tensor("p_sc", [128, 2048], F32) as p_sc,
                      nc.psum_tensor("p_wt", [128, 2048], BF16) as p_wt,
                      nc.psum_tensor("p_ap", [128, 1024], F32) as p_ap):
                    def scores_pass0(step, h, k, xiT_src, src_res):
                        def mm_fn(e, h=h, k=k, xiT_src=xiT_src):
                            for j in range(4):
                                ins = e.matmul(p_sc[:, j * 512:(j + 1) * 512],
                                               xiT_src[:, h * NT + k * 128:h * NT + (k + 1) * 128],
                                               KT[:, h * NMEM + j * 512:h * NMEM + (j + 1) * 512],
                                               start=True, stop=True)
                            return ins
                        S.op("tensor", mm_fn, reads=[f"{src_res}{h}", "KT"], writes=["p_sc"])
                        S.op("vector",
                             lambda e, h=h, k=k: e.tensor_scalar(
                                 y_buf[:, k * NMEM:(k + 1) * NMEM], p_sc[:],
                                 beta_bc[:, h:h + 1], None, ALU.mult, ALU.max,
                                 accum_out=ymax[:, k:k + 1]),
                             reads=["p_sc", "beta_bc"], writes=[f"y{k}", f"ymax{k // 4}"])

                    def final_relu(h, k, fin, fin_res):
                        rb = r_buf[k % 4]
                        S.op("vector",
                             lambda e, k=k, rb=rb, fin=fin: e.tensor_scalar(
                                 rb[:], y_buf[:, k * NMEM:(k + 1) * NMEM],
                                 fin[:, k:k + 1], 0.0, ALU.add, ALU.max),
                             reads=[f"y{k}", fin_res], writes=[f"rb{k % 4}"])

                    def final_sq_tr(h, k):
                        rb = r_buf[k % 4]
                        wf = w_fin[k % 2]
                        S.op("scalar",
                             lambda e, rb=rb, wf=wf: e.activation(wf[:], rb[:], AF.Square),
                             reads=[f"rb{k % 4}"], writes=[f"wf{k % 2}"])
                        def tr_fn(e, wf=wf):
                            for j in range(16):
                                ins = e.matmul(p_wt[:, j * 128:(j + 1) * 128], wf[:, j * 128:(j + 1) * 128], eye_s[:],
                                               is_transpose=True, start=(j % 8 == 0), stop=(j % 8 == 7),
                                               skip_group_check=True)
                            return ins
                        S.op("tensor", tr_fn, reads=[f"wf{k % 2}", "eye_s"], writes=["p_wt"])

                    def final_evac(h, k):
                        S.op("scalar",
                             lambda e, k=k: e.activation(
                                 wT_v[:, :, (k % 4) * 128:(k % 4 + 1) * 128], p_wt[:], AF.Copy),
                             reads=["p_wt"], writes=["wT"])

                    def apply_half(h, half, xiT_dst, dst_res):
                        def mm_fn(e, h=h):
                            for ncv in range(16):
                                ins = e.matmul(p_ap[:, (half % 2) * 512:(half % 2) * 512 + 512],
                                               Vb[:, ncv * MD + h * 128:ncv * MD + (h + 1) * 128],
                                               wT_v[:, ncv, 0:512],
                                               start=(ncv == 0), stop=(ncv == 15))
                            return ins
                        S.op("tensor", mm_fn, reads=["Vb", "wT"], writes=[f"p_ap{half % 2}"])
                        S.op("vector",
                             lambda e, h=h, half=half, xiT_dst=xiT_dst: e.tensor_copy(
                                 xiT_dst[:, h * NT + half * 512:h * NT + (half + 1) * 512],
                                 p_ap[:, (half % 2) * 512:(half % 2) * 512 + 512]),
                             reads=[f"p_ap{half % 2}"], writes=[f"{dst_res}{h}"])

                    for step in range(3):
                        T = T_STEPS[step]
                        if step % 2 == 0:
                            xiT_src, src_res, xiT_dst, dst_res = xiT_a, "xiT_a_h", xiT_b, "xiT_b_h"
                        else:
                            xiT_src, src_res, xiT_dst, dst_res = xiT_b, "xiT_b_h", xiT_a, "xiT_a_h"
                        for h in range(8):
                            if h == 0:
                                for k in range(8):
                                    scores_pass0(step, h, k, xiT_src, src_res)
                            cur = {0: (ntau_a, "ntauA_a"), 1: (ntau_f, "ntauB_a")}
                            nxt = {0: (ntau_b, "ntauA_b"), 1: (ntau_g, "ntauB_b")}
                            for half in range(2):
                                lo = half * 4
                                S.op("vector",
                                     lambda e, half=half, lo=lo, t=cur[half][0]: e.tensor_scalar(
                                         t[:, lo:lo + 4], ymax[:, lo:lo + 4], -1.0, 1.0, ALU.mult, ALU.add),
                                     reads=[f"ymax{half}"], writes=[cur[half][1]])
                            for j in range(T):
                                for half in range(2):
                                    lo = half * 4
                                    ct, cres = cur[half]
                                    for kk in range(4):
                                        k = lo + kk
                                        rb = r_buf[k % 4]
                                        rres = f"rb{k % 4}"
                                        S.op("scalar",
                                             lambda e, k=k, rb=rb, ct=ct: e.activation(
                                                 rb[:], y_buf[:, k * NMEM:(k + 1) * NMEM], AF.Relu,
                                                 bias=ct[:, k:k + 1], scale=1.0, accum_out=src_c[:, k:k + 1]),
                                             reads=[f"y{k}", cres], writes=[rres, f"srcc_{k}"])
                                        if k == 7:
                                            S.op("scalar",
                                                 lambda e, k=k, rb=rb: e.activation(
                                                     sdump[:], rb[:], AF.Square, accum_out=sr2_c[:, k:k + 1]),
                                                 reads=[rres], writes=[f"sr2c_{k}"])
                                        else:
                                            S.op("vector",
                                                 lambda e, k=k, rb=rb, ct=ct: e.scalar_tensor_tensor(
                                                     sdump[:], y_buf[:, k * NMEM:(k + 1) * NMEM],
                                                     ct[:, k:k + 1], rb[:], ALU.add, ALU.mult,
                                                     accum_out=sr2_c[:, k:k + 1]),
                                                 reads=[f"y{k}", cres, rres], writes=[f"sr2c_{k}"])
                                    nt, nres = nxt[half]
                                    S.op("vector", lambda e, lo=lo: e.tensor_scalar(c1[:, lo:lo + 4], src_c[:, lo:lo + 4], 1e-30, None, ALU.max, ALU.bypass),
                                         reads=[f"srcc_{lo}", f"srcc_{lo+1}", f"srcc_{lo+2}", f"srcc_{lo+3}"], writes=[f"c1{half}"])
                                    S.op("vector", lambda e, lo=lo: e.reciprocal(c1[:, lo:lo + 4], c1[:, lo:lo + 4]),
                                         reads=[f"c1{half}"], writes=[f"c1{half}"])
                                    S.op("vector", lambda e, lo=lo: e.scalar_tensor_tensor(c2[:, lo:lo + 4], sr2_c[:, lo:lo + 4], -1.0, c1[:, lo:lo + 4], ALU.add, ALU.mult),
                                         reads=[f"sr2c_{lo}", f"sr2c_{lo+1}", f"sr2c_{lo+2}", f"sr2c_{lo+3}", f"c1{half}"], writes=[f"c2{half}"])
                                    S.op("vector",
                                         lambda e, lo=lo, ct=ct, nt=nt: e.scalar_tensor_tensor(
                                             nt[:, lo:lo + 4], c2[:, lo:lo + 4], -0.5, ct[:, lo:lo + 4], ALU.mult, ALU.add),
                                         reads=[f"c2{half}", cres], writes=[nres])
                                    cur[half], nxt[half] = nxt[half], cur[half]
                            nxt_head = h + 1 if h < 7 else None
                            final_relu(h, 0, cur[0][0], cur[0][1])
                            for k in range(8):
                                final_sq_tr(h, k)
                                if k < 7:
                                    kn = k + 1
                                    final_relu(h, kn, cur[kn // 4][0], cur[kn // 4][1])
                                final_evac(h, k)
                                if nxt_head is not None:
                                    scores_pass0(step, nxt_head, k, xiT_src, src_res)
                                if k == 3:
                                    apply_half(h, 0, xiT_dst, dst_res)
                            apply_half(h, 1, xiT_dst, dst_res)
                    xiT_fin = xiT_b if len(T_STEPS) % 2 == 1 else xiT_a
                    xiT_fin_res = "xiT_b_h" if len(T_STEPS) % 2 == 1 else "xiT_a_h"

            S.barrier()
            with ExitStack() as p4:
                woT_sb = sbuf(p4, "woT_sb", [128, 8 * D], BF16)
                wost = sbuf(p4, "wost", [128, D], F32)
                xrs = [sbuf(p4, f"xrs{i}", [128, D], F32) for i in range(2)]
                x1p = sbuf(p4, "x1p", [128, D], F32)
                x1c = sbuf(p4, "x1c", [128, D], BF16)
                with nc.psum_tensor("p_wo", [128, 2048], F32) as p_wo:
                    for f in range(8):
                        S.op("sync", lambda e, f=f: e.dma_start(out=wost[:], in_=wo_h[f * 128:(f + 1) * 128, :]),
                             writes=["wost"], dma=True)
                        S.op("vector", lambda e, f=f: e.tensor_copy(woT_sb[:, f * D:(f + 1) * D], wost[:]),
                             reads=["wost"], writes=["woT_sb"])
                    for k in range(8):
                        S.op("sync", lambda e, k=k: e.dma_start(out=xrs[k % 2][:], in_=x_h[k * 128:(k + 1) * 128, :]),
                             writes=[f"xrs{k % 2}"], dma=True)
                        def mm_fn(e, k=k):
                            for f in range(8):
                                for j in range(4):
                                    ins = e.matmul(p_wo[:, j * 512:(j + 1) * 512],
                                                   xiT_fin[:, f * NT + k * 128:f * NT + (k + 1) * 128],
                                                   woT_sb[:, f * D + j * 512:f * D + (j + 1) * 512],
                                                   start=(f == 0), stop=(f == 7))
                            return ins
                        S.op("tensor", mm_fn,
                             reads=[f"{xiT_fin_res}{hh}" for hh in range(8)] + ["woT_sb"], writes=["p_wo"])
                        S.op("vector", lambda e, k=k: e.tensor_tensor(x1p[:], xrs[k % 2][:], p_wo[:], ALU.add),
                             reads=[f"xrs{k % 2}", "p_wo"], writes=["x1p"])
                        cA_t, cB_t, cAr, cBr = ln_cols(S, x1p, "x1p", 4, k)
                        S.op("vector", lambda e, cA_t=cA_t: e.tensor_scalar(x1c[:], x1p[:], cA_t[:, 0:1], None, ALU.add, ALU.bypass),
                             reads=["x1p", cAr], writes=["x1c"])
                        S.op("vector",
                             lambda e, k=k, cB_t=cB_t: e.tensor_scalar(x1nat[:, k * D:(k + 1) * D], x1c[:], cB_t[:, 0:1], None, ALU.mult, ALU.bypass),
                             reads=["x1c", cBr], writes=[f"x1_{k}"])

        if debug:
            S.op("sync", lambda e: e.dma_start(out=dbg["dbg_x1"], in_=x1nat[:]),
                 reads=[f"x1_{kk}" for kk in range(8)], dma=True)
        S.barrier()
        with ExitStack() as p5:
            x1T = sbuf(p5, "x1T", [128, 16 * NT], BF16)
            x1T_v = x1T.ap().rearrange("p (cc t) -> p cc t", cc=16)
            acc = sbuf(p5, "acc", [128, 8 * D], BF16)
            b0b_bc = sbuf(p5, "b0b_bc", [128, D], BF16)
            b1b_bc = sbuf(p5, "b1b_bc", [128, D], BF16)
            bd_bc = sbuf(p5, "bd_bc", [128, D], BF16)
            with nc.psum_tensor("p_tr3", [128, 2048], BF16) as p_tr3:
                for k in range(8):
                    def tr_fn(e, k=k):
                        for j in range(16):
                            ins = e.matmul(p_tr3[:, j * 128:(j + 1) * 128],
                                           x1nat[:, k * D + j * 128:k * D + (j + 1) * 128], eye_s[:],
                                           is_transpose=True, start=(j % 8 == 0), stop=(j % 8 == 7),
                                           skip_group_check=True)
                        return ins
                    S.op("tensor", tr_fn, reads=[f"x1_{k}", "eye_s"], writes=["p_tr3"])
                    S.op("vector",
                         lambda e, k=k: e.tensor_copy(x1T_v[:, :, k * 128:(k + 1) * 128], p_tr3[:]),
                         reads=["p_tr3"], writes=["x1T"])
            S.barrier()
            vec5 = sbuf(p5, "vec5", [1, D], F32)
            with nc.psum_tensor("p_bc2", [128, 2048], F32) as p_bc2:
                bcast(b0b_bc, "b0b_bc", bb_h[0], D, p_bc2, "p_bc2", vec5, "vec5")
                bcast(b1b_bc, "b1b_bc", bb_h[1], D, p_bc2, "p_bc2", vec5, "vec5")
            S.barrier()
            S.op("vector", lambda e: e.tensor_tensor(bd_bc[:], b0b_bc[:], b1b_bc[:], ALU.subtract),
                 reads=["b0b_bc", "b1b_bc"], writes=["bd_bc"])
            S.barrier()
            with ExitStack() as p5g:
                wgst = sbuf(p5g, "wgst", [128, 16 * 2], F32)
                wgbf = sbuf(p5g, "wgbf", [128, 16 * 2], BF16)
                S.op("sync", lambda e: e.dma_start(out=wgst[:], in_=wg_h),
                     writes=["wgst"], dma=True)
                S.op("vector", lambda e: e.tensor_copy(wgbf[:], wgst[:]), reads=["wgst"], writes=["wgbf"])
                with nc.psum_tensor("p_g", [128, 2], F32) as p_g:
                    for k in range(8):
                        def mm_fn(e, k=k):
                            for cc in range(16):
                                ins = e.matmul(p_g[:, 0:2], x1T_v[:, cc, k * 128:(k + 1) * 128],
                                               wgbf[:, cc * 2:(cc + 1) * 2], start=(cc == 0), stop=(cc == 15))
                            return ins
                        S.op("tensor", mm_fn, reads=["x1T", "wgbf"], writes=["p_g"])
                        S.op("vector", lambda e: e.tensor_copy(c2[:, 0:2], p_g[:, 0:2]),
                             reads=["p_g"], writes=["c2"])
                        S.op("vector",
                             lambda e, k=k: e.tensor_tensor(mask0_c[:, k:k + 1], c2[:, 0:1], c2[:, 1:2], ALU.is_ge),
                             reads=["c2"], writes=["mask0_c"])
            S.barrier()
            S.op("vector", lambda e: e.tensor_scalar(mask1_c[:, 0:8], mask0_c[:, 0:8], -1.0, 1.0, ALU.mult, ALU.add),
                 reads=["mask0_c"], writes=["mask1_c"])

            S.barrier()
            with ExitStack() as p5e:
                h_sup = sbuf(p5e, "h_sup", [128, 8 * NT], BF16)
                wast = [sbuf(p5e, f"wast{i}", [128, 2048], F32) for i in range(2)]
                wabf = sbuf(p5e, "wabf", [128, 2048], BF16)
                wbst = [sbuf(p5e, f"wbst{i}", [128, D], F32) for i in range(2)]
                wbbf = sbuf(p5e, "wbbf", [128, 8 * D], BF16)
                ba_sb = [sbuf(p5e, f"ba_sb{i}", [128, 32], F32) for i in range(2)]
                S.op("sync", lambda e: e.dma_start(out=ba_sb[0][:], in_=ba_h[0]), writes=["ba0"], dma=True)
                S.op("sync", lambda e: e.dma_start(out=ba_sb[1][:], in_=ba_h[1]), writes=["ba1"], dma=True)
                with (nc.psum_tensor("p_h", [128, 1024], F32) as p_h,
                      nc.psum_tensor("p_o", [128, 2048], F32) as p_o):
                    for e_idx in range(2):
                        mcol = mask0_c if e_idx == 0 else mask1_c
                        mres = "mask0_c" if e_idx == 0 else "mask1_c"
                        actf = AF.Gelu if e_idx == 0 else AF.Relu
                        basb = ba_sb[e_idx]
                        bares = f"ba{e_idx}"
                        for fs in range(4):
                            for fc8 in range(8):
                                fc = fs * 8 + fc8
                                wa_i = fc % 2
                                S.op("sync",
                                     lambda e, fc=fc, wa_i=wa_i, e_idx=e_idx: e.dma_start(
                                         out=wast[wa_i][:], in_=wa_h[e_idx][fc]),
                                     writes=[f"wast{wa_i}"], dma=True)
                                S.op("vector", lambda e, wa_i=wa_i: e.tensor_copy(wabf[:], wast[wa_i][:]),
                                     reads=[f"wast{wa_i}"], writes=["wabf"])
                                for th in range(2):
                                    def mm_fn(e, th=th):
                                        for cc in range(16):
                                            ins = e.matmul(p_h[:, th * 512:(th + 1) * 512],
                                                           wabf[:, cc * 128:(cc + 1) * 128],
                                                           x1T_v[:, cc, th * 512:(th + 1) * 512],
                                                           start=(cc == 0), stop=(cc == 15))
                                        return ins
                                    pr = f"p_h{th}"
                                    S.op("tensor", mm_fn, reads=["wabf", "x1T"], writes=[pr])
                                S.op("scalar",
                                     lambda e, fc=fc, fc8=fc8, actf=actf, basb=basb: e.activation(
                                         h_sup[:, fc8 * NT:(fc8 + 1) * NT], p_h[:], actf,
                                         bias=basb[:, fc:fc + 1], scale=1.0),
                                     reads=["p_h0", "p_h1", bares], writes=[f"hs{fc8}"])
                            for fc8 in range(8):
                                fc = fs * 8 + fc8
                                wb_i = fc8 % 2
                                S.op("sync",
                                     lambda e, fc=fc, wb_i=wb_i, e_idx=e_idx: e.dma_start(
                                         out=wbst[wb_i][:], in_=wb_h[e_idx][fc * 128:(fc + 1) * 128, :]),
                                     writes=[f"wbst{wb_i}"], dma=True)
                                S.op("vector",
                                     lambda e, fc8=fc8, wb_i=wb_i: e.tensor_copy(wbbf[:, fc8 * D:(fc8 + 1) * D], wbst[wb_i][:]),
                                     reads=[f"wbst{wb_i}"], writes=[f"wbbf{fc8}"])
                            for k in range(8):
                                for j in range(4):
                                    def mm_fn(e, k=k, j=j):
                                        for fc8 in range(8):
                                            ins = e.matmul(p_o[:, j * 512:(j + 1) * 512],
                                                           h_sup[:, fc8 * NT + k * 128:fc8 * NT + (k + 1) * 128],
                                                           wbbf[:, fc8 * D + j * 512:fc8 * D + (j + 1) * 512],
                                                           start=(fc8 == 0), stop=(fc8 == 7))
                                        return ins
                                    S.op("tensor", mm_fn,
                                         reads=[f"hs{i}" for i in range(8)] + [f"wbbf{i}" for i in range(8)],
                                         writes=[f"p_o{j}"])
                                    if e_idx == 0 and fs == 0:
                                        S.op("vector",
                                             lambda e, k=k, j=j, mcol=mcol: e.tensor_scalar(
                                                 acc[:, k * D + j * 512:k * D + (j + 1) * 512], p_o[:, j * 512:(j + 1) * 512],
                                                 mcol[:, k:k + 1], None, ALU.mult, ALU.bypass),
                                             reads=[f"p_o{j}", mres], writes=[f"acc{k}"])
                                    else:
                                        S.op("vector",
                                             lambda e, k=k, j=j, mcol=mcol: e.scalar_tensor_tensor(
                                                 acc[:, k * D + j * 512:k * D + (j + 1) * 512], p_o[:, j * 512:(j + 1) * 512],
                                                 mcol[:, k:k + 1],
                                                 acc[:, k * D + j * 512:k * D + (j + 1) * 512], ALU.mult, ALU.add),
                                             reads=[f"p_o{j}", mres, f"acc{k}"], writes=[f"acc{k}"])
            if debug:
                S.op("sync", lambda e: e.dma_start(out=dbg["dbg_acc"], in_=acc[:]),
                     reads=[f"acc{kk}" for kk in range(8)], dma=True)
            S.barrier()
            with ExitStack() as p5f:
                x2p = sbuf(p5f, "x2p", [128, D], F32)
                bmix = sbuf(p5f, "bmix", [128, D], BF16)
                outt = sbuf(p5f, "outt", [128, D], F32)
                for k in range(8):
                    S.op("vector",
                         lambda e, k=k: e.scalar_tensor_tensor(bmix[:], bd_bc[:], mask0_c[:, k:k + 1], b1b_bc[:],
                                                               ALU.mult, ALU.add),
                         reads=["bd_bc", "mask0_c", "b1b_bc"], writes=["bmix"])
                    S.op("vector", lambda e, k=k: e.tensor_tensor(x2p[:], x1nat[:, k * D:(k + 1) * D], acc[:, k * D:(k + 1) * D], ALU.add),
                         reads=[f"x1_{k}", f"acc{k}"], writes=["x2p"])
                    S.op("vector", lambda e: e.tensor_tensor(x2p[:], x2p[:], bmix[:], ALU.add),
                         reads=["x2p", "bmix"], writes=["x2p"])
                    cA_t, cB_t, cAr, cBr = ln_cols(S, x2p, "x2p", 4, k)
                    S.op("vector", lambda e, cA_t=cA_t: e.tensor_scalar(x2p[:], x2p[:], cA_t[:, 0:1], None, ALU.add, ALU.bypass),
                         reads=["x2p", cAr], writes=["x2p"])
                    S.op("vector", lambda e, cB_t=cB_t: e.tensor_scalar(outt[:], x2p[:], cB_t[:, 0:1], None, ALU.mult, ALU.bypass),
                         reads=["x2p", cBr], writes=["outt"])
                    S.op("sync", lambda e, k=k: e.dma_start(out=out_h[k * 128:(k + 1) * 128, :], in_=outt[:]),
                         reads=["outt"], writes=[f"store{k}"], dma=True)

        with ExitStack() as semctx:
            sems = {}
            for e in ENGS:
                sems[e] = semctx.enter_context(nc.semaphore(f"s_{e}"))
            for i in range(NDMA):
                sems[f"dma{i}"] = semctx.enter_context(nc.semaphore(f"s_dma{i}"))
            block = semctx.enter_context(nc.Block())
            dma_final = [(f"dma{i}", 16 * S.dma_use[i]) for i in range(NDMA) if S.dma_use[i]]

            @block.sync
            def _(sync):
                S.emit("sync", sync, sems)
                for nm, val in dma_final:
                    sync.wait_ge(sems[nm], val)

            @block.vector
            def _(vector):
                S.emit("vector", vector, sems)

            @block.scalar
            def _(scalar):
                S.emit("scalar", scalar, sems)

            @block.tensor
            def _(tensor):
                S.emit("tensor", tensor, sems)

            if S.q["gpsimd"]:
                @block.gpsimd
                def _(gpsimd):
                    S.emit("gpsimd", gpsimd, sems)
    return nc


_CACHE = {}


def kernel(**inputs):
    f32 = np.float32
    x = np.asarray(inputs["L0_attn_hidden"], f32)
    mem = np.ascontiguousarray(np.asarray(inputs["L0_attn_memory"], f32))
    wq = np.asarray(inputs["L0_attn_Wq"], f32)
    wk = np.asarray(inputs["L0_attn_Wk"], f32)
    wv = np.asarray(inputs["L0_attn_Wv"], f32)
    wo = np.asarray(inputs["L0_attn_Wo"], f32)
    lb = np.ascontiguousarray(np.asarray(inputs["L0_attn_log_beta"], f32).reshape(1, NH))
    gs = np.ascontiguousarray(np.asarray(inputs["L0_attn_g_state"], f32).reshape(1, D))
    bs = np.ascontiguousarray(np.asarray(inputs["L0_attn_b_state"], f32).reshape(1, D))
    gm = np.ascontiguousarray(np.asarray(inputs["L0_attn_g_mem"], f32).reshape(1, MD))
    bm = np.ascontiguousarray(np.asarray(inputs["L0_attn_b_mem"], f32).reshape(1, MD))
    wg = np.asarray(inputs["L0_ffn_Wg"], f32)
    w0a = np.asarray(inputs["L0_ffn_W0a"], f32)
    w1a = np.asarray(inputs["L0_ffn_W1a"], f32)
    w0b = np.asarray(inputs["L0_ffn_W0b"], f32)
    w1b = np.asarray(inputs["L0_ffn_W1b"], f32)
    b0a = np.asarray(inputs["L0_ffn_b0a"], f32)
    b1a = np.asarray(inputs["L0_ffn_b1a"], f32)
    b0b = np.ascontiguousarray(np.asarray(inputs["L0_ffn_b0b"], f32).reshape(1, D))
    b1b = np.ascontiguousarray(np.asarray(inputs["L0_ffn_b1b"], f32).reshape(1, D))

    wqT = np.ascontiguousarray(wq.T.reshape(16, 128, 8, 128).transpose(2, 1, 0, 3).reshape(8, 128, 2048))
    wkT = np.ascontiguousarray(wk.T)
    wvT = np.ascontiguousarray(wv.T)
    woT = np.ascontiguousarray(wo.T)
    wgT = np.ascontiguousarray(wg.T.reshape(16, 128, 2).transpose(1, 0, 2).reshape(128, 32))
    w0aT = np.ascontiguousarray(w0a.T.reshape(16, 128, 32, 128).transpose(2, 1, 0, 3).reshape(32, 128, 2048))
    w1aT = np.ascontiguousarray(w1a.T.reshape(16, 128, 32, 128).transpose(2, 1, 0, 3).reshape(32, 128, 2048))
    w0bT = np.ascontiguousarray(w0b.T)
    w1bT = np.ascontiguousarray(w1b.T)
    b0a_c = np.ascontiguousarray(b0a.reshape(32, 128).T)
    b1a_c = np.ascontiguousarray(b1a.reshape(32, 128).T)
    ones1 = np.ones((1, 128), f32)
    eye = np.eye(128).astype(bfnp)

    if "nc" not in _CACHE:
        _CACHE["nc"] = build_program()
    nc = _CACHE["nc"]

    in_maps = []
    for c in range(8):
        b, sh = c // 2, c % 2
        xs = np.ascontiguousarray(x[b, sh * NT:(sh + 1) * NT, :])
        in_maps.append({
            "x": xs, "mem": mem, "wqT": wqT, "wkT": wkT, "wvT": wvT, "woT": woT,
            "wgT": wgT, "w0aT": w0aT, "w1aT": w1aT, "w0bT": w0bT, "w1bT": w1bT,
            "log_beta": lb, "g_state": gs, "b_state": bs, "g_mem": gm, "b_mem": bm,
            "b0a": b0a_c, "b1a": b1a_c, "b0b": b0b, "b1b": b1b,
            "ones1": ones1, "eye": eye,
        })
    res = run_bass_kernel_spmd(nc, in_maps, core_ids=list(range(8)), trace_cores=list(range(8)))
    out = np.empty((4, 2048, D), np.float32)
    for c in range(8):
        b, sh = c // 2, c % 2
        out[b, sh * NT:(sh + 1) * NT, :] = res.results[c]["out"]
    return out


# version 23 (final)
# speedup vs baseline: 1.1600x; 1.0317x over previous
from contextlib import ExitStack

import numpy as np
import ml_dtypes

import concourse.bass as bass
import concourse.mybir as mybir
from concourse.bass_utils import run_bass_kernel_spmd

F32 = mybir.dt.float32
BF16 = mybir.dt.bfloat16
AF = mybir.ActivationFunctionType
ALU = mybir.AluOpType
bfnp = ml_dtypes.bfloat16

T_STEPS = (5, 6, 7)
USE_FIN = True

NT = 1024
D = 2048
NH = 8
NMEM = 2048
MD = 1024
FF = 4096
EPS = 1e-5
LN_HALF = float(np.log(0.5))

ENGS = ("sync", "vector", "scalar", "tensor", "gpsimd")
NDMA = 8


class KB:

    def __init__(self):
        self.q = {e: [] for e in ENGS}
        self.cnt = {e: 0 for e in ENGS}
        self.dma_i = 0
        self.dma_use = [0] * NDMA
        self.res = {}
        self.barrier_deps = []

    def barrier(self):
        self.barrier_deps = [(e, self.cnt[e]) for e in ENGS if self.cnt[e] > 0]
        self.barrier_deps += [(f"dma{i}", 16 * self.dma_use[i]) for i in range(NDMA) if self.dma_use[i] > 0]

    def _res(self, name):
        return self.res.setdefault(name, {"w": None, "r": []})

    def op(self, eng, fn, reads=(), writes=(), deps=(), dma=False):
        dep = list(deps) + self.barrier_deps
        pre = []
        if dma:
            s = self.dma_i % NDMA
            self.dma_i += 1
            if self.dma_use[s] > 0:
                pre.append((f"dma{s}", 16 * self.dma_use[s]))
            self.dma_use[s] += 1
            tok = (f"dma{s}", 16 * self.dma_use[s])
            sem_tag = f"dma{s}"
        else:
            tok = (eng, self.cnt[eng] + 1)
            self.cnt[eng] = tok[1]
            sem_tag = eng
        for nm in reads:
            r = self._res(nm)
            if r["w"] is not None:
                dep.append(r["w"])
        for nm in writes:
            r = self._res(nm)
            if r["w"] is not None:
                dep.append(r["w"])
            dep.extend(r["r"])
        for nm in reads:
            self._res(nm)["r"].append(tok)
        for nm in writes:
            r = self._res(nm)
            r["w"] = tok
            r["r"] = []
        self.q[eng].append((fn, tuple(dep), tuple(pre), sem_tag, tok[1]))
        return tok

    SELF_SYNC = True

    def emit(self, eng_name, eng, sems):
        waited = {}
        for fn, deps, pre, sem_tag, tokval in self.q[eng_name]:
            need = {}
            for (de, dv) in list(deps) + list(pre):
                if de == eng_name and not self.SELF_SYNC:
                    continue
                if dv > waited.get(de, 0):
                    need[de] = max(need.get(de, 0), dv)
            for de in sorted(need):
                eng.wait_ge(sems[de], need[de])
                waited[de] = need[de]
            ins = fn(eng)
            if sem_tag.startswith("dma"):
                ins.then_inc(sems[sem_tag], 16)
            elif hasattr(eng, "engine_nop"):
                eng.engine_nop().then_inc(sems[sem_tag], 1)
            else:
                ins.then_inc(sems[sem_tag], 1)


def build_program(detect_races=True, debug=False):
    nc = bass.Bass("TRN2", target_bir_lowering=False, detect_race_conditions=detect_races)

    def dram(name, shape, dt=F32, kind="ExternalInput"):
        return nc.dram_tensor(name, shape, dt, kind=kind).ap()

    x_h = dram("x", [NT, D])
    mem_h = dram("mem", [NMEM, MD])
    wq_h = dram("wqT", [8, 128, 16 * 128])
    wk_h = dram("wkT", [MD, MD])
    wv_h = dram("wvT", [MD, MD])
    wo_h = dram("woT", [NH * 128, D])
    wg_h = dram("wgT", [128, 32])
    wa_h = [dram("w0aT", [32, 128, 16 * 128]), dram("w1aT", [32, 128, 16 * 128])]
    wb_h = [dram("w0bT", [FF, D]), dram("w1bT", [FF, D])]
    lb_h = dram("log_beta", [1, NH])
    gs_h = dram("g_state", [1, D])
    bs_h = dram("b_state", [1, D])
    gm_h = dram("g_mem", [1, MD])
    bm_h = dram("b_mem", [1, MD])
    ba_h = [dram("b0a", [128, 32]), dram("b1a", [128, 32])]
    bb_h = [dram("b0b", [1, D]), dram("b1b", [1, D])]
    ones_h = dram("ones1", [1, 128])
    eye_h = dram("eye", [128, 128], BF16)
    out_h = dram("out", [NT, D], kind="ExternalOutput")
    dbg = {}
    if debug:
        for nm, sh, dt in (("dbg_KT", [128, NH * NMEM], BF16), ("dbg_Vb", [128, 16 * MD], BF16),
                           ("dbg_xi0", [128, NH * NT], BF16), ("dbg_xi1", [128, NH * NT], BF16),
                           ("dbg_xi2", [128, NH * NT], BF16), ("dbg_xi3", [128, NH * NT], BF16),
                           ("dbg_x1", [128, 8 * D], BF16), ("dbg_acc", [128, 8 * D], BF16),
                           ("dbg_mnT", [128, 8 * NMEM], BF16), ("dbg_ag", [128, 2], F32),
                           ("dbg_colB", [128, 8], F32), ("dbg_msc", [128, MD], F32),
                           ("dbg_mnbf", [128, MD], BF16)):
            dbg[nm] = dram(nm, sh, dt, kind="ExternalOutput")

    S = KB()

    def mk_ln_cols(lnbufs):
        def ln_cols(S, src_ap, src_res, nchunks, par):
            p = par % 2
            bnst, colA, colB = lnbufs[p]
            bn, cA, cB = f"bnst{p}", f"colA{p}", f"colB{p}"
            def st_fn(e):
                for i in range(nchunks):
                    ins = e.bn_stats(bnst[:, i * 6:(i + 1) * 6], src_ap[:, i * 512:(i + 1) * 512])
                return ins
            S.op("vector", st_fn, reads=[src_res], writes=[bn])
            S.op("vector", lambda e: e.bn_aggr(colA[:, 6:8], bnst[:, 0:6 * nchunks]),
                 reads=[bn], writes=[cA])
            S.op("vector", lambda e: e.tensor_scalar(colA[:, 0:1], colA[:, 6:7], -1.0, None, ALU.mult, ALU.bypass),
                 reads=[cA], writes=[cA])
            S.op("vector", lambda e: e.tensor_scalar(colB[:, 0:1], colA[:, 7:8], EPS, None, ALU.add, ALU.bypass),
                 reads=[cA], writes=[cB])
            S.op("scalar", lambda e: e.activation(colB[:, 0:1], colB[:, 0:1], AF.Ln),
                 reads=[cB], writes=[cB])
            S.op("scalar", lambda e: e.activation(colB[:, 0:1], colB[:, 0:1], AF.Exp, scale=-0.5),
                 reads=[cB], writes=[cB])
            return colA, colB, cA, cB
        return ln_cols

    with ExitStack() as ctx:
        def sbuf(stack, name, shape, dt):
            return stack.enter_context(nc.sbuf_tensor(name, shape, dt))

        ones_s = sbuf(ctx, "ones_s", [1, 128], F32)
        eye_s = sbuf(ctx, "eye_s", [128, 128], BF16)
        lb_s = sbuf(ctx, "lb_s", [1, NH], F32)
        ln2_s = sbuf(ctx, "ln2_s", [1, 1], F32)
        beta_bc = sbuf(ctx, "beta_bc", [128, NH], F32)
        lnbufs = [(sbuf(ctx, f"bnst{i}", [128, 24], F32),
                   sbuf(ctx, f"colA{i}", [128, 8], F32),
                   sbuf(ctx, f"colB{i}", [128, 8], F32)) for i in range(2)]
        ln_cols = mk_ln_cols(lnbufs)
        ymax = sbuf(ctx, "ymax", [128, 8], F32)
        ntau_a = sbuf(ctx, "ntau_a", [128, 8], F32)
        ntau_b = sbuf(ctx, "ntau_b", [128, 8], F32)
        ntau_f = sbuf(ctx, "ntau_f", [128, 8], F32)
        ntau_g = sbuf(ctx, "ntau_g", [128, 8], F32)
        src_c = sbuf(ctx, "src_c", [128, 8], F32)
        sr2_c = sbuf(ctx, "sr2_c", [128, 8], F32)
        kc_c = sbuf(ctx, "kc_c", [128, 8], F32)
        c1 = sbuf(ctx, "c1", [128, 8], F32)
        c2 = sbuf(ctx, "c2", [128, 8], F32)
        c3 = sbuf(ctx, "c3", [128, 8], F32)
        mask0_c = sbuf(ctx, "mask0_c", [128, 8], F32)
        mask1_c = sbuf(ctx, "mask1_c", [128, 8], F32)
        x1nat = sbuf(ctx, "x1nat", [128, 8 * D], BF16)

        S.op("sync", lambda e: e.dma_start(out=ones_s[:], in_=ones_h), writes=["ones_s"], dma=True)
        S.op("sync", lambda e: e.dma_start(out=eye_s[:], in_=eye_h), writes=["eye_s"], dma=True)
        S.op("sync", lambda e: e.dma_start(out=lb_s[:], in_=lb_h), writes=["lb_s"], dma=True)
        S.op("vector", lambda e: e.memset(ln2_s[:], LN_HALF), writes=["ln2_s"])
        S.op("scalar", lambda e: e.activation(lb_s[:], lb_s[:], AF.Exp, bias=ln2_s[0:1, 0:1], scale=1.0),
             reads=["ln2_s"], writes=["lb_s"])
        with nc.psum_tensor("p_b0", [128, 8], F32) as p_b0:
            S.op("tensor", lambda e: e.matmul(p_b0[:, 0:8], ones_s[0:1, :], lb_s[0:1, :], start=True, stop=True),
                 reads=["ones_s", "lb_s"], writes=["p_b0"])
            S.op("vector", lambda e: e.tensor_copy(beta_bc[:], p_b0[:, 0:8]),
                 reads=["p_b0"], writes=["beta_bc"])

        S.barrier()
        def bcast(dst, dres, row_h, width, psum, pres, vec_s, vres):
            S.op("sync", lambda e: e.dma_start(out=vec_s[0:1, 0:width], in_=row_h),
                 writes=[vres], dma=True)
            for j0 in range(0, width, 512):
                S.op("tensor",
                     lambda e, j0=j0: e.matmul(psum[:, j0 % 2048:j0 % 2048 + 512], ones_s[0:1, :],
                                               vec_s[0:1, j0:j0 + 512], start=True, stop=True),
                     reads=["ones_s", vres], writes=[f"{pres}{j0 % 2048}"])
                S.op("vector", lambda e, j0=j0: e.tensor_copy(dst[:, j0:j0 + 512], psum[:, j0 % 2048:j0 % 2048 + 512]),
                     reads=[f"{pres}{j0 % 2048}"], writes=[dres])

        with ExitStack() as attn:
            KT = sbuf(attn, "KT", [128, NH * NMEM], BF16)
            Vb = sbuf(attn, "Vb", [128, 16 * MD], BF16)
            xiT_a = sbuf(attn, "xiT_a", [128, NH * NT], BF16)
            xiT_b = sbuf(attn, "xiT_b", [128, NH * NT], BF16)

            with ExitStack() as p1:
                mnT = sbuf(p1, "mnT", [128, 8 * NMEM], BF16)
                mnT_v = mnT.ap().rearrange("p (mc n) -> p mc n", mc=8)
                gm_bc = sbuf(p1, "gm_bc", [128, MD], BF16)
                bm_bc = sbuf(p1, "bm_bc", [128, MD], BF16)
                with ExitStack() as p1a:
                    mst = sbuf(p1a, "mst", [128, 4 * MD], F32)
                    msc2 = [sbuf(p1a, f"msc{i}", [128, MD], F32) for i in range(2)]
                    mnbf = [sbuf(p1a, f"mnbf{i}", [128, MD], BF16) for i in range(2)]
                    vec1 = sbuf(p1a, "vec1", [1, MD], F32)
                    with nc.psum_tensor("p_m", [128, 1024], F32) as p_m:
                        bcast(gm_bc, "gm_bc", gm_h, MD, p_m, "p_m", vec1, "vec1")
                        bcast(bm_bc, "bm_bc", bm_h, MD, p_m, "p_m", vec1, "vec1")
                    S.barrier()
                    with nc.psum_tensor("p_tr1", [128, 1024], BF16) as p_tr1:
                        for half in range(4):
                            mres = "mst"
                            S.op("sync",
                                 lambda e, half=half: e.dma_start(
                                     out=mst.ap().rearrange("p (g m) -> p g m", g=4),
                                     in_=mem_h.rearrange("(g p) m -> p g m", p=128)[:, half * 4:(half + 1) * 4, :]),
                                 writes=[mres], dma=True)
                            for it in range(4):
                                nt_i = half * 4 + it
                                mtile = mst[:, it * MD:(it + 1) * MD]
                                cA_t, cB_t, cAr, cBr = ln_cols(S, mtile, mres, 2, nt_i)
                                msc = msc2[nt_i % 2]
                                mscr = f"msc{nt_i % 2}"
                                S.op("vector", lambda e, mtile=mtile, cA_t=cA_t, msc=msc: e.tensor_scalar(msc[:], mtile, cA_t[:, 0:1], None, ALU.add, ALU.bypass),
                                     reads=[mres, cAr], writes=[mscr])
                                S.op("vector", lambda e, cB_t=cB_t, msc=msc: e.scalar_tensor_tensor(msc[:], msc[:], cB_t[:, 0:1], gm_bc[:], ALU.mult, ALU.mult),
                                     reads=[mscr, cBr, "gm_bc"], writes=[mscr])
                                mb = mnbf[nt_i % 2]
                                mbres = f"mnbf{nt_i % 2}"
                                S.op("vector", lambda e, mb=mb, msc=msc: e.tensor_tensor(mb[:], msc[:], bm_bc[:], ALU.add),
                                     reads=[mscr, "bm_bc"], writes=[mbres])
                                def tr_fn(e, mb=mb):
                                    for j in range(8):
                                        ins = e.matmul(p_tr1[:, j * 128:(j + 1) * 128], mb[:, j * 128:(j + 1) * 128], eye_s[:],
                                                       is_transpose=True, start=(j == 0), stop=(j == 7), skip_group_check=True)
                                    return ins
                                S.op("tensor", tr_fn, reads=[mbres, "eye_s"], writes=["p_tr1"])
                                S.op("vector",
                                     lambda e, nt_i=nt_i: e.tensor_copy(
                                         mnT_v[:, :, nt_i * 128:(nt_i + 1) * 128], p_tr1[:]),
                                     reads=["p_tr1"], writes=[f"mnT{nt_i}"])
                                if debug and nt_i == 15:
                                    S.op("sync", lambda e: e.dma_start(out=dbg["dbg_ag"], in_=bnag[:]), reads=["bnag"], dma=True)
                                    S.op("sync", lambda e: e.dma_start(out=dbg["dbg_colB"], in_=colB[:]), reads=["colB"], dma=True)
                                    S.op("sync", lambda e: e.dma_start(out=dbg["dbg_msc"], in_=msc[:]), reads=["msc"], dma=True)
                                    S.op("sync", lambda e, mb=mb: e.dma_start(out=dbg["dbg_mnbf"], in_=mb[:]), reads=[mbres], dma=True)
                if debug:
                    S.op("sync", lambda e: e.dma_start(out=dbg["dbg_mnT"], in_=mnT[:]), reads=["mnT"], dma=True)
                S.barrier()
                for wgt_h, name in ((wk_h, "K"), (wv_h, "V")):
                    S.barrier()
                    with ExitStack() as pw:
                        wst = sbuf(pw, "wst_" + name, [128, 4 * MD], F32)
                        wbf = sbuf(pw, "wbf_" + name, [128, 8 * MD], BF16)
                        for hf in range(2):
                            S.op("sync",
                                 lambda e, wgt_h=wgt_h, hf=hf: e.dma_start(
                                     out=wst.ap().rearrange("p (mc f) -> p mc f", mc=4),
                                     in_=wgt_h.rearrange("(mc p) f -> p mc f", p=128)[:, hf * 4:(hf + 1) * 4, :]),
                                 writes=["wst"], dma=True)
                            for i in range(4):
                                S.op("vector",
                                     lambda e, i=i, hf=hf: e.tensor_copy(
                                         wbf[:, (hf * 4 + i) * MD:(hf * 4 + i + 1) * MD], wst[:, i * MD:(i + 1) * MD]),
                                     reads=["wst"], writes=["wbf"])
                        with nc.psum_tensor("p_kv_" + name, [128, 1024], F32) as p_kv:
                            if name == "K":
                                for f in range(8):
                                    for ncol in range(4):
                                        def mm_fn(e, f=f, ncol=ncol):
                                            for mc in range(8):
                                                ins = e.matmul(p_kv[:, (ncol % 2) * 512:(ncol % 2) * 512 + 512],
                                                               wbf[:, mc * MD + f * 128:mc * MD + (f + 1) * 128],
                                                               mnT_v[:, mc, ncol * 512:(ncol + 1) * 512],
                                                               start=(mc == 0), stop=(mc == 7))
                                            return ins
                                        pr = f"p_kv{ncol % 2}"
                                        S.op("tensor", mm_fn, reads=["wbf"] + [f"mnT{4*ncol+i}" for i in range(4)], writes=[pr])
                                        S.op("vector",
                                             lambda e, f=f, ncol=ncol: e.tensor_copy(
                                                 KT[:, f * NMEM + ncol * 512:f * NMEM + (ncol + 1) * 512],
                                                 p_kv[:, (ncol % 2) * 512:(ncol % 2) * 512 + 512]),
                                             reads=[pr], writes=["KT"])
                            else:
                                for ntile in range(16):
                                    for fcol in range(2):
                                        def mm_fn(e, ntile=ntile, fcol=fcol):
                                            for mc in range(8):
                                                ins = e.matmul(p_kv[:, fcol * 512:(fcol + 1) * 512],
                                                               mnT_v[:, mc, ntile * 128:(ntile + 1) * 128],
                                                               wbf[:, mc * MD + fcol * 512:mc * MD + (fcol + 1) * 512],
                                                               start=(mc == 0), stop=(mc == 7))
                                            return ins
                                        pr = f"p_kv{fcol}"
                                        S.op("tensor", mm_fn, reads=["wbf", f"mnT{ntile}"], writes=[pr])
                                        S.op("vector",
                                             lambda e, ntile=ntile, fcol=fcol: e.tensor_copy(
                                                 Vb[:, ntile * MD + fcol * 512:ntile * MD + (fcol + 1) * 512],
                                                 p_kv[:, fcol * 512:(fcol + 1) * 512]),
                                             reads=[pr], writes=["Vb"])

            if debug:
                S.op("sync", lambda e: e.dma_start(out=dbg["dbg_KT"], in_=KT[:]), reads=["KT"], dma=True)
                S.op("sync", lambda e: e.dma_start(out=dbg["dbg_Vb"], in_=Vb[:]), reads=["Vb"], dma=True)
            S.barrier()
            with ExitStack() as p2:
                gs_bc = sbuf(p2, "gs_bc", [128, D], BF16)
                bs_bc = sbuf(p2, "bs_bc", [128, D], BF16)
                lnxTh = sbuf(p2, "lnxTh", [128, 16 * 512], BF16)
                lnxT_v = lnxTh.ap().rearrange("p (cc t) -> p cc t", cc=16)
                xst = [sbuf(p2, f"xst{i}", [128, D], F32) for i in range(2)]
                lnbf = [sbuf(p2, f"lnbf{i}", [128, D], BF16) for i in range(2)]
                wqst = sbuf(p2, "wqst", [128, 16 * 128], F32)
                wqbf = sbuf(p2, "wqbf", [128, 16 * 128], BF16)
                vec2 = sbuf(p2, "vec2", [1, D], F32)
                with nc.psum_tensor("p_bb", [128, 2048], F32) as p_bb:
                    bcast(gs_bc, "gs_bc", gs_h, D, p_bb, "p_bb", vec2, "vec2")
                    bcast(bs_bc, "bs_bc", bs_h, D, p_bb, "p_bb", vec2, "vec2")
                S.barrier()
                with (nc.psum_tensor("p_tr2", [128, 2048], BF16) as p_tr2,
                      nc.psum_tensor("p_q", [128, 1024], F32) as p_q):
                    for half in range(2):
                        for kk in range(4):
                            k = half * 4 + kk
                            xs = xst[kk % 2]
                            xres = f"xst{kk % 2}"
                            S.op("sync", lambda e, xs=xs, k=k: e.dma_start(out=xs[:], in_=x_h[k * 128:(k + 1) * 128, :]),
                                 writes=[xres], dma=True)
                            cA_t, cB_t, cAr, cBr = ln_cols(S, xs, xres, 4, k)
                            S.op("vector", lambda e, xs=xs, cA_t=cA_t: e.tensor_scalar(xs[:], xs[:], cA_t[:, 0:1], None, ALU.add, ALU.bypass),
                                 reads=[xres, cAr], writes=[xres])
                            S.op("vector", lambda e, xs=xs, cB_t=cB_t: e.scalar_tensor_tensor(xs[:], xs[:], cB_t[:, 0:1], gs_bc[:], ALU.mult, ALU.mult),
                                 reads=[xres, cBr, "gs_bc"], writes=[xres])
                            lb2 = lnbf[kk % 2]
                            lbres = f"lnbf{kk % 2}"
                            S.op("vector", lambda e, lb2=lb2, xs=xs: e.tensor_tensor(lb2[:], xs[:], bs_bc[:], ALU.add),
                                 reads=[xres, "bs_bc"], writes=[lbres])
                            def tr_fn(e, lb2=lb2):
                                for j in range(16):
                                    ins = e.matmul(p_tr2[:, j * 128:(j + 1) * 128], lb2[:, j * 128:(j + 1) * 128], eye_s[:],
                                                   is_transpose=True, start=(j % 8 == 0), stop=(j % 8 == 7), skip_group_check=True)
                                return ins
                            S.op("tensor", tr_fn, reads=[lbres, "eye_s"], writes=["p_tr2"])
                            S.op("vector",
                                 lambda e, kk=kk: e.tensor_copy(lnxT_v[:, :, kk * 128:(kk + 1) * 128], p_tr2[:]),
                                 reads=["p_tr2"], writes=[f"lnxTh{kk}"])
                        for h in range(8):
                            S.op("sync", lambda e, h=h: e.dma_start(out=wqst[:], in_=wq_h[h]),
                                 writes=["wqst"], dma=True)
                            for i in range(4):
                                S.op("vector", lambda e, i=i: e.tensor_copy(wqbf[:, i * 512:(i + 1) * 512], wqst[:, i * 512:(i + 1) * 512]),
                                     reads=["wqst"], writes=["wqbf"])
                            def mm_fn(e):
                                for cc in range(16):
                                    ins = e.matmul(p_q[:, 0:512],
                                                   wqbf[:, cc * 128:(cc + 1) * 128],
                                                   lnxT_v[:, cc, 0:512],
                                                   start=(cc == 0), stop=(cc == 15))
                                return ins
                            S.op("tensor", mm_fn, reads=["wqbf"] + [f"lnxTh{i}" for i in range(4)], writes=["p_q"])
                            S.op("vector",
                                 lambda e, h=h, half=half: e.tensor_copy(
                                     xiT_a[:, h * NT + half * 512:h * NT + half * 512 + 512], p_q[:, 0:512]),
                                 reads=["p_q"], writes=[f"xiT_a_h{h}"])

            S.barrier()
            with ExitStack() as p3:
                y_buf = sbuf(p3, "y_buf", [128, 8 * NMEM], BF16)
                r_buf = [sbuf(p3, f"r_buf{i}", [128, NMEM], BF16) for i in range(4)]
                sdump = sbuf(p3, "sdump", [128, NMEM], BF16)
                w_fin = [sbuf(p3, f"w_fin{i}", [128, NMEM], BF16) for i in range(2)]
                wT_buf = sbuf(p3, "wT_buf", [128, 16 * 512], BF16)
                wT_v = wT_buf.ap().rearrange("p (nc t) -> p nc t", nc=16)
                with (nc.psum_tensor("p_sc", [128, 2048], F32) as p_sc,
                      nc.psum_tensor("p_wt", [128, 2048], BF16) as p_wt,
                      nc.psum_tensor("p_ap", [128, 1024], F32) as p_ap):
                    def scores_pass0(step, h, k, xiT_src, src_res):
                        def mm_fn(e, h=h, k=k, xiT_src=xiT_src):
                            for j in range(4):
                                ins = e.matmul(p_sc[:, j * 512:(j + 1) * 512],
                                               xiT_src[:, h * NT + k * 128:h * NT + (k + 1) * 128],
                                               KT[:, h * NMEM + j * 512:h * NMEM + (j + 1) * 512],
                                               start=True, stop=True)
                            return ins
                        S.op("tensor", mm_fn, reads=[f"{src_res}{h}", "KT"], writes=["p_sc"])
                        S.op("vector",
                             lambda e, h=h, k=k: e.tensor_scalar(
                                 y_buf[:, k * NMEM:(k + 1) * NMEM], p_sc[:],
                                 beta_bc[:, h:h + 1], None, ALU.mult, ALU.max,
                                 accum_out=ymax[:, k:k + 1]),
                             reads=["p_sc", "beta_bc"], writes=[f"y{k}", f"ymax{k // 4}"])

                    def final_relu(h, k, fin, fin_res):
                        rb = r_buf[k % 4]
                        S.op("vector",
                             lambda e, k=k, rb=rb, fin=fin: e.tensor_scalar(
                                 rb[:], y_buf[:, k * NMEM:(k + 1) * NMEM],
                                 fin[:, k:k + 1], 0.0, ALU.add, ALU.max),
                             reads=[f"y{k}", fin_res], writes=[f"rb{k % 4}"])

                    def final_sq_tr(h, k):
                        rb = r_buf[k % 4]
                        wf = w_fin[k % 2]
                        S.op("scalar",
                             lambda e, rb=rb, wf=wf: e.activation(wf[:], rb[:], AF.Square),
                             reads=[f"rb{k % 4}"], writes=[f"wf{k % 2}"])
                        def tr_fn(e, wf=wf):
                            for j in range(16):
                                ins = e.matmul(p_wt[:, j * 128:(j + 1) * 128], wf[:, j * 128:(j + 1) * 128], eye_s[:],
                                               is_transpose=True, start=(j % 8 == 0), stop=(j % 8 == 7),
                                               skip_group_check=True)
                            return ins
                        S.op("tensor", tr_fn, reads=[f"wf{k % 2}", "eye_s"], writes=["p_wt"])

                    def final_evac(h, k):
                        if k % 2 == 0:
                            S.op("vector",
                                 lambda e, k=k: e.tensor_copy(
                                     wT_v[:, :, (k % 4) * 128:(k % 4 + 1) * 128], p_wt[:]),
                                 reads=["p_wt"], writes=["wT"])
                        else:
                            S.op("scalar",
                                 lambda e, k=k: e.activation(
                                     wT_v[:, :, (k % 4) * 128:(k % 4 + 1) * 128], p_wt[:], AF.Copy),
                                 reads=["p_wt"], writes=["wT"])

                    def apply_half(h, half, xiT_dst, dst_res):
                        def mm_fn(e, h=h):
                            for ncv in range(16):
                                ins = e.matmul(p_ap[:, (half % 2) * 512:(half % 2) * 512 + 512],
                                               Vb[:, ncv * MD + h * 128:ncv * MD + (h + 1) * 128],
                                               wT_v[:, ncv, 0:512],
                                               start=(ncv == 0), stop=(ncv == 15))
                            return ins
                        S.op("tensor", mm_fn, reads=["Vb", "wT"], writes=[f"p_ap{half % 2}"])
                        S.op("vector",
                             lambda e, h=h, half=half, xiT_dst=xiT_dst: e.tensor_copy(
                                 xiT_dst[:, h * NT + half * 512:h * NT + (half + 1) * 512],
                                 p_ap[:, (half % 2) * 512:(half % 2) * 512 + 512]),
                             reads=[f"p_ap{half % 2}"], writes=[f"{dst_res}{h}"])

                    for step in range(3):
                        T = T_STEPS[step]
                        if step % 2 == 0:
                            xiT_src, src_res, xiT_dst, dst_res = xiT_a, "xiT_a_h", xiT_b, "xiT_b_h"
                        else:
                            xiT_src, src_res, xiT_dst, dst_res = xiT_b, "xiT_b_h", xiT_a, "xiT_a_h"
                        for h in range(8):
                            if h == 0:
                                for k in range(8):
                                    scores_pass0(step, h, k, xiT_src, src_res)
                            cur = {0: (ntau_a, "ntauA_a"), 1: (ntau_f, "ntauB_a")}
                            nxt = {0: (ntau_b, "ntauA_b"), 1: (ntau_g, "ntauB_b")}
                            for half in range(2):
                                lo = half * 4
                                S.op("vector",
                                     lambda e, half=half, lo=lo, t=cur[half][0]: e.tensor_scalar(
                                         t[:, lo:lo + 4], ymax[:, lo:lo + 4], -1.0, 1.0, ALU.mult, ALU.add),
                                     reads=[f"ymax{half}"], writes=[cur[half][1]])
                            for j in range(T):
                                for half in range(2):
                                    lo = half * 4
                                    ct, cres = cur[half]
                                    for kk in range(4):
                                        k = lo + kk
                                        rb = r_buf[k % 4]
                                        rres = f"rb{k % 4}"
                                        S.op("scalar",
                                             lambda e, k=k, rb=rb, ct=ct: e.activation(
                                                 rb[:], y_buf[:, k * NMEM:(k + 1) * NMEM], AF.Relu,
                                                 bias=ct[:, k:k + 1], scale=1.0, accum_out=src_c[:, k:k + 1]),
                                             reads=[f"y{k}", cres], writes=[rres, f"srcc_{k}"])
                                        if k == 7:
                                            S.op("scalar",
                                                 lambda e, k=k, rb=rb: e.activation(
                                                     sdump[:], rb[:], AF.Square, accum_out=sr2_c[:, k:k + 1]),
                                                 reads=[rres], writes=[f"sr2c_{k}"])
                                        else:
                                            S.op("vector",
                                                 lambda e, k=k, rb=rb, ct=ct: e.scalar_tensor_tensor(
                                                     sdump[:], y_buf[:, k * NMEM:(k + 1) * NMEM],
                                                     ct[:, k:k + 1], rb[:], ALU.add, ALU.mult,
                                                     accum_out=sr2_c[:, k:k + 1]),
                                                 reads=[f"y{k}", cres, rres], writes=[f"sr2c_{k}"])
                                    nt, nres = nxt[half]
                                    S.op("vector", lambda e, lo=lo: e.tensor_scalar(c1[:, lo:lo + 4], src_c[:, lo:lo + 4], 1e-30, None, ALU.max, ALU.bypass),
                                         reads=[f"srcc_{lo}", f"srcc_{lo+1}", f"srcc_{lo+2}", f"srcc_{lo+3}"], writes=[f"c1{half}"])
                                    S.op("vector", lambda e, lo=lo: e.reciprocal(c1[:, lo:lo + 4], c1[:, lo:lo + 4]),
                                         reads=[f"c1{half}"], writes=[f"c1{half}"])
                                    S.op("vector", lambda e, lo=lo: e.scalar_tensor_tensor(c2[:, lo:lo + 4], sr2_c[:, lo:lo + 4], -1.0, c1[:, lo:lo + 4], ALU.add, ALU.mult),
                                         reads=[f"sr2c_{lo}", f"sr2c_{lo+1}", f"sr2c_{lo+2}", f"sr2c_{lo+3}", f"c1{half}"], writes=[f"c2{half}"])
                                    S.op("vector",
                                         lambda e, lo=lo, ct=ct, nt=nt: e.scalar_tensor_tensor(
                                             nt[:, lo:lo + 4], c2[:, lo:lo + 4], -0.5, ct[:, lo:lo + 4], ALU.mult, ALU.add),
                                         reads=[f"c2{half}", cres], writes=[nres])
                                    cur[half], nxt[half] = nxt[half], cur[half]
                            nxt_head = h + 1 if h < 7 else None
                            final_relu(h, 0, cur[0][0], cur[0][1])
                            for k in range(8):
                                final_sq_tr(h, k)
                                if k < 7:
                                    kn = k + 1
                                    final_relu(h, kn, cur[kn // 4][0], cur[kn // 4][1])
                                final_evac(h, k)
                                if nxt_head is not None:
                                    scores_pass0(step, nxt_head, k, xiT_src, src_res)
                                if k == 3:
                                    apply_half(h, 0, xiT_dst, dst_res)
                            apply_half(h, 1, xiT_dst, dst_res)
                    xiT_fin = xiT_b if len(T_STEPS) % 2 == 1 else xiT_a
                    xiT_fin_res = "xiT_b_h" if len(T_STEPS) % 2 == 1 else "xiT_a_h"

            S.barrier()
            with ExitStack() as p4:
                woT_sb = sbuf(p4, "woT_sb", [128, 8 * D], BF16)
                wost = sbuf(p4, "wost", [128, D], F32)
                xrs = [sbuf(p4, f"xrs{i}", [128, D], F32) for i in range(2)]
                x1p = sbuf(p4, "x1p", [128, D], F32)
                x1c = sbuf(p4, "x1c", [128, D], BF16)
                with nc.psum_tensor("p_wo", [128, 2048], F32) as p_wo:
                    for f in range(8):
                        S.op("sync", lambda e, f=f: e.dma_start(out=wost[:], in_=wo_h[f * 128:(f + 1) * 128, :]),
                             writes=["wost"], dma=True)
                        S.op("vector", lambda e, f=f: e.tensor_copy(woT_sb[:, f * D:(f + 1) * D], wost[:]),
                             reads=["wost"], writes=["woT_sb"])
                    for k in range(8):
                        S.op("sync", lambda e, k=k: e.dma_start(out=xrs[k % 2][:], in_=x_h[k * 128:(k + 1) * 128, :]),
                             writes=[f"xrs{k % 2}"], dma=True)
                        def mm_fn(e, k=k):
                            for f in range(8):
                                for j in range(4):
                                    ins = e.matmul(p_wo[:, j * 512:(j + 1) * 512],
                                                   xiT_fin[:, f * NT + k * 128:f * NT + (k + 1) * 128],
                                                   woT_sb[:, f * D + j * 512:f * D + (j + 1) * 512],
                                                   start=(f == 0), stop=(f == 7))
                            return ins
                        S.op("tensor", mm_fn,
                             reads=[f"{xiT_fin_res}{hh}" for hh in range(8)] + ["woT_sb"], writes=["p_wo"])
                        S.op("vector", lambda e, k=k: e.tensor_tensor(x1p[:], xrs[k % 2][:], p_wo[:], ALU.add),
                             reads=[f"xrs{k % 2}", "p_wo"], writes=["x1p"])
                        cA_t, cB_t, cAr, cBr = ln_cols(S, x1p, "x1p", 4, k)
                        S.op("vector", lambda e, cA_t=cA_t: e.tensor_scalar(x1c[:], x1p[:], cA_t[:, 0:1], None, ALU.add, ALU.bypass),
                             reads=["x1p", cAr], writes=["x1c"])
                        S.op("vector",
                             lambda e, k=k, cB_t=cB_t: e.tensor_scalar(x1nat[:, k * D:(k + 1) * D], x1c[:], cB_t[:, 0:1], None, ALU.mult, ALU.bypass),
                             reads=["x1c", cBr], writes=[f"x1_{k}"])

        if debug:
            S.op("sync", lambda e: e.dma_start(out=dbg["dbg_x1"], in_=x1nat[:]),
                 reads=[f"x1_{kk}" for kk in range(8)], dma=True)
        S.barrier()
        with ExitStack() as p5:
            x1T = sbuf(p5, "x1T", [128, 16 * NT], BF16)
            x1T_v = x1T.ap().rearrange("p (cc t) -> p cc t", cc=16)
            acc = sbuf(p5, "acc", [128, 8 * D], BF16)
            b0b_bc = sbuf(p5, "b0b_bc", [128, D], BF16)
            b1b_bc = sbuf(p5, "b1b_bc", [128, D], BF16)
            bd_bc = sbuf(p5, "bd_bc", [128, D], BF16)
            with nc.psum_tensor("p_tr3", [128, 2048], BF16) as p_tr3:
                for k in range(8):
                    def tr_fn(e, k=k):
                        for j in range(16):
                            ins = e.matmul(p_tr3[:, j * 128:(j + 1) * 128],
                                           x1nat[:, k * D + j * 128:k * D + (j + 1) * 128], eye_s[:],
                                           is_transpose=True, start=(j % 8 == 0), stop=(j % 8 == 7),
                                           skip_group_check=True)
                        return ins
                    S.op("tensor", tr_fn, reads=[f"x1_{k}", "eye_s"], writes=["p_tr3"])
                    S.op("vector",
                         lambda e, k=k: e.tensor_copy(x1T_v[:, :, k * 128:(k + 1) * 128], p_tr3[:]),
                         reads=["p_tr3"], writes=["x1T"])
            S.barrier()
            vec5 = sbuf(p5, "vec5", [1, D], F32)
            with nc.psum_tensor("p_bc2", [128, 2048], F32) as p_bc2:
                bcast(b0b_bc, "b0b_bc", bb_h[0], D, p_bc2, "p_bc2", vec5, "vec5")
                bcast(b1b_bc, "b1b_bc", bb_h[1], D, p_bc2, "p_bc2", vec5, "vec5")
            S.barrier()
            S.op("vector", lambda e: e.tensor_tensor(bd_bc[:], b0b_bc[:], b1b_bc[:], ALU.subtract),
                 reads=["b0b_bc", "b1b_bc"], writes=["bd_bc"])
            S.barrier()
            with ExitStack() as p5g:
                wgst = sbuf(p5g, "wgst", [128, 16 * 2], F32)
                wgbf = sbuf(p5g, "wgbf", [128, 16 * 2], BF16)
                S.op("sync", lambda e: e.dma_start(out=wgst[:], in_=wg_h),
                     writes=["wgst"], dma=True)
                S.op("vector", lambda e: e.tensor_copy(wgbf[:], wgst[:]), reads=["wgst"], writes=["wgbf"])
                with nc.psum_tensor("p_g", [128, 2], F32) as p_g:
                    for k in range(8):
                        def mm_fn(e, k=k):
                            for cc in range(16):
                                ins = e.matmul(p_g[:, 0:2], x1T_v[:, cc, k * 128:(k + 1) * 128],
                                               wgbf[:, cc * 2:(cc + 1) * 2], start=(cc == 0), stop=(cc == 15))
                            return ins
                        S.op("tensor", mm_fn, reads=["x1T", "wgbf"], writes=["p_g"])
                        S.op("vector", lambda e: e.tensor_copy(c2[:, 0:2], p_g[:, 0:2]),
                             reads=["p_g"], writes=["c2"])
                        S.op("vector",
                             lambda e, k=k: e.tensor_tensor(mask0_c[:, k:k + 1], c2[:, 0:1], c2[:, 1:2], ALU.is_ge),
                             reads=["c2"], writes=["mask0_c"])
            S.barrier()
            S.op("vector", lambda e: e.tensor_scalar(mask1_c[:, 0:8], mask0_c[:, 0:8], -1.0, 1.0, ALU.mult, ALU.add),
                 reads=["mask0_c"], writes=["mask1_c"])

            S.barrier()
            with ExitStack() as p5e:
                h_sup = sbuf(p5e, "h_sup", [128, 8 * NT], BF16)
                wast = [sbuf(p5e, f"wast{i}", [128, 2048], F32) for i in range(2)]
                wabf = sbuf(p5e, "wabf", [128, 2048], BF16)
                wbst = [sbuf(p5e, f"wbst{i}", [128, D], F32) for i in range(2)]
                wbbf = sbuf(p5e, "wbbf", [128, 8 * D], BF16)
                ba_sb = [sbuf(p5e, f"ba_sb{i}", [128, 32], F32) for i in range(2)]
                S.op("sync", lambda e: e.dma_start(out=ba_sb[0][:], in_=ba_h[0]), writes=["ba0"], dma=True)
                S.op("sync", lambda e: e.dma_start(out=ba_sb[1][:], in_=ba_h[1]), writes=["ba1"], dma=True)
                with (nc.psum_tensor("p_h", [128, 1024], F32) as p_h,
                      nc.psum_tensor("p_o", [128, 2048], F32) as p_o):
                    for e_idx in range(2):
                        mcol = mask0_c if e_idx == 0 else mask1_c
                        mres = "mask0_c" if e_idx == 0 else "mask1_c"
                        actf = AF.Gelu if e_idx == 0 else AF.Relu
                        basb = ba_sb[e_idx]
                        bares = f"ba{e_idx}"
                        for fs in range(4):
                            for fc8 in range(8):
                                fc = fs * 8 + fc8
                                wa_i = fc % 2
                                S.op("sync",
                                     lambda e, fc=fc, wa_i=wa_i, e_idx=e_idx: e.dma_start(
                                         out=wast[wa_i][:], in_=wa_h[e_idx][fc]),
                                     writes=[f"wast{wa_i}"], dma=True)
                                S.op("vector", lambda e, wa_i=wa_i: e.tensor_copy(wabf[:], wast[wa_i][:]),
                                     reads=[f"wast{wa_i}"], writes=["wabf"])
                                for th in range(2):
                                    def mm_fn(e, th=th):
                                        for cc in range(16):
                                            ins = e.matmul(p_h[:, th * 512:(th + 1) * 512],
                                                           wabf[:, cc * 128:(cc + 1) * 128],
                                                           x1T_v[:, cc, th * 512:(th + 1) * 512],
                                                           start=(cc == 0), stop=(cc == 15))
                                        return ins
                                    pr = f"p_h{th}"
                                    S.op("tensor", mm_fn, reads=["wabf", "x1T"], writes=[pr])
                                S.op("scalar",
                                     lambda e, fc=fc, fc8=fc8, actf=actf, basb=basb: e.activation(
                                         h_sup[:, fc8 * NT:(fc8 + 1) * NT], p_h[:], actf,
                                         bias=basb[:, fc:fc + 1], scale=1.0),
                                     reads=["p_h0", "p_h1", bares], writes=[f"hs{fc8}"])
                            for fc8 in range(8):
                                fc = fs * 8 + fc8
                                wb_i = fc8 % 2
                                S.op("sync",
                                     lambda e, fc=fc, wb_i=wb_i, e_idx=e_idx: e.dma_start(
                                         out=wbst[wb_i][:], in_=wb_h[e_idx][fc * 128:(fc + 1) * 128, :]),
                                     writes=[f"wbst{wb_i}"], dma=True)
                                S.op("vector",
                                     lambda e, fc8=fc8, wb_i=wb_i: e.tensor_copy(wbbf[:, fc8 * D:(fc8 + 1) * D], wbst[wb_i][:]),
                                     reads=[f"wbst{wb_i}"], writes=[f"wbbf{fc8}"])
                            for k in range(8):
                                for j in range(4):
                                    def mm_fn(e, k=k, j=j):
                                        for fc8 in range(8):
                                            ins = e.matmul(p_o[:, j * 512:(j + 1) * 512],
                                                           h_sup[:, fc8 * NT + k * 128:fc8 * NT + (k + 1) * 128],
                                                           wbbf[:, fc8 * D + j * 512:fc8 * D + (j + 1) * 512],
                                                           start=(fc8 == 0), stop=(fc8 == 7))
                                        return ins
                                    S.op("tensor", mm_fn,
                                         reads=[f"hs{i}" for i in range(8)] + [f"wbbf{i}" for i in range(8)],
                                         writes=[f"p_o{j}"])
                                    if e_idx == 0 and fs == 0:
                                        S.op("vector",
                                             lambda e, k=k, j=j, mcol=mcol: e.tensor_scalar(
                                                 acc[:, k * D + j * 512:k * D + (j + 1) * 512], p_o[:, j * 512:(j + 1) * 512],
                                                 mcol[:, k:k + 1], None, ALU.mult, ALU.bypass),
                                             reads=[f"p_o{j}", mres], writes=[f"acc{k}"])
                                    else:
                                        S.op("vector",
                                             lambda e, k=k, j=j, mcol=mcol: e.scalar_tensor_tensor(
                                                 acc[:, k * D + j * 512:k * D + (j + 1) * 512], p_o[:, j * 512:(j + 1) * 512],
                                                 mcol[:, k:k + 1],
                                                 acc[:, k * D + j * 512:k * D + (j + 1) * 512], ALU.mult, ALU.add),
                                             reads=[f"p_o{j}", mres, f"acc{k}"], writes=[f"acc{k}"])
            if debug:
                S.op("sync", lambda e: e.dma_start(out=dbg["dbg_acc"], in_=acc[:]),
                     reads=[f"acc{kk}" for kk in range(8)], dma=True)
            S.barrier()
            with ExitStack() as p5f:
                x2p2 = [sbuf(p5f, f"x2p{i}", [128, D], F32) for i in range(2)]
                bmix2 = [sbuf(p5f, f"bmix{i}", [128, D], BF16) for i in range(2)]
                outt2 = [sbuf(p5f, f"outt{i}", [128, D], F32) for i in range(2)]
                for k in range(8):
                    x2p, bmix, outt = x2p2[k % 2], bmix2[k % 2], outt2[k % 2]
                    x2pr, bmixr, outtr = f"x2p{k % 2}", f"bmix{k % 2}", f"outt{k % 2}"
                    S.op("vector",
                         lambda e, k=k, bmix=bmix: e.scalar_tensor_tensor(bmix[:], bd_bc[:], mask0_c[:, k:k + 1], b1b_bc[:],
                                                               ALU.mult, ALU.add),
                         reads=["bd_bc", "mask0_c", "b1b_bc"], writes=[bmixr])
                    S.op("vector", lambda e, k=k, x2p=x2p: e.tensor_tensor(x2p[:], x1nat[:, k * D:(k + 1) * D], acc[:, k * D:(k + 1) * D], ALU.add),
                         reads=[f"x1_{k}", f"acc{k}"], writes=[x2pr])
                    S.op("vector", lambda e, x2p=x2p, bmix=bmix: e.tensor_tensor(x2p[:], x2p[:], bmix[:], ALU.add),
                         reads=[x2pr, bmixr], writes=[x2pr])
                    cA_t, cB_t, cAr, cBr = ln_cols(S, x2p, x2pr, 4, k)
                    S.op("vector", lambda e, cA_t=cA_t, x2p=x2p: e.tensor_scalar(x2p[:], x2p[:], cA_t[:, 0:1], None, ALU.add, ALU.bypass),
                         reads=[x2pr, cAr], writes=[x2pr])
                    S.op("vector", lambda e, cB_t=cB_t, x2p=x2p, outt=outt: e.tensor_scalar(outt[:], x2p[:], cB_t[:, 0:1], None, ALU.mult, ALU.bypass),
                         reads=[x2pr, cBr], writes=[outtr])
                    S.op("sync", lambda e, k=k, outt=outt: e.dma_start(out=out_h[k * 128:(k + 1) * 128, :], in_=outt[:]),
                         reads=[outtr], writes=[f"store{k}"], dma=True)

        with ExitStack() as semctx:
            sems = {}
            for e in ENGS:
                sems[e] = semctx.enter_context(nc.semaphore(f"s_{e}"))
            for i in range(NDMA):
                sems[f"dma{i}"] = semctx.enter_context(nc.semaphore(f"s_dma{i}"))
            block = semctx.enter_context(nc.Block())
            dma_final = [(f"dma{i}", 16 * S.dma_use[i]) for i in range(NDMA) if S.dma_use[i]]

            @block.sync
            def _(sync):
                S.emit("sync", sync, sems)
                for nm, val in dma_final:
                    sync.wait_ge(sems[nm], val)

            @block.vector
            def _(vector):
                S.emit("vector", vector, sems)

            @block.scalar
            def _(scalar):
                S.emit("scalar", scalar, sems)

            @block.tensor
            def _(tensor):
                S.emit("tensor", tensor, sems)

            if S.q["gpsimd"]:
                @block.gpsimd
                def _(gpsimd):
                    S.emit("gpsimd", gpsimd, sems)
    return nc


_CACHE = {}


def kernel(**inputs):
    f32 = np.float32
    x = np.asarray(inputs["L0_attn_hidden"], f32)
    mem = np.ascontiguousarray(np.asarray(inputs["L0_attn_memory"], f32))
    wq = np.asarray(inputs["L0_attn_Wq"], f32)
    wk = np.asarray(inputs["L0_attn_Wk"], f32)
    wv = np.asarray(inputs["L0_attn_Wv"], f32)
    wo = np.asarray(inputs["L0_attn_Wo"], f32)
    lb = np.ascontiguousarray(np.asarray(inputs["L0_attn_log_beta"], f32).reshape(1, NH))
    gs = np.ascontiguousarray(np.asarray(inputs["L0_attn_g_state"], f32).reshape(1, D))
    bs = np.ascontiguousarray(np.asarray(inputs["L0_attn_b_state"], f32).reshape(1, D))
    gm = np.ascontiguousarray(np.asarray(inputs["L0_attn_g_mem"], f32).reshape(1, MD))
    bm = np.ascontiguousarray(np.asarray(inputs["L0_attn_b_mem"], f32).reshape(1, MD))
    wg = np.asarray(inputs["L0_ffn_Wg"], f32)
    w0a = np.asarray(inputs["L0_ffn_W0a"], f32)
    w1a = np.asarray(inputs["L0_ffn_W1a"], f32)
    w0b = np.asarray(inputs["L0_ffn_W0b"], f32)
    w1b = np.asarray(inputs["L0_ffn_W1b"], f32)
    b0a = np.asarray(inputs["L0_ffn_b0a"], f32)
    b1a = np.asarray(inputs["L0_ffn_b1a"], f32)
    b0b = np.ascontiguousarray(np.asarray(inputs["L0_ffn_b0b"], f32).reshape(1, D))
    b1b = np.ascontiguousarray(np.asarray(inputs["L0_ffn_b1b"], f32).reshape(1, D))

    wqT = np.ascontiguousarray(wq.T.reshape(16, 128, 8, 128).transpose(2, 1, 0, 3).reshape(8, 128, 2048))
    wkT = np.ascontiguousarray(wk.T)
    wvT = np.ascontiguousarray(wv.T)
    woT = np.ascontiguousarray(wo.T)
    wgT = np.ascontiguousarray(wg.T.reshape(16, 128, 2).transpose(1, 0, 2).reshape(128, 32))
    w0aT = np.ascontiguousarray(w0a.T.reshape(16, 128, 32, 128).transpose(2, 1, 0, 3).reshape(32, 128, 2048))
    w1aT = np.ascontiguousarray(w1a.T.reshape(16, 128, 32, 128).transpose(2, 1, 0, 3).reshape(32, 128, 2048))
    w0bT = np.ascontiguousarray(w0b.T)
    w1bT = np.ascontiguousarray(w1b.T)
    b0a_c = np.ascontiguousarray(b0a.reshape(32, 128).T)
    b1a_c = np.ascontiguousarray(b1a.reshape(32, 128).T)
    ones1 = np.ones((1, 128), f32)
    eye = np.eye(128).astype(bfnp)

    if "nc" not in _CACHE:
        _CACHE["nc"] = build_program()
    nc = _CACHE["nc"]

    in_maps = []
    for c in range(8):
        b, sh = c // 2, c % 2
        xs = np.ascontiguousarray(x[b, sh * NT:(sh + 1) * NT, :])
        in_maps.append({
            "x": xs, "mem": mem, "wqT": wqT, "wkT": wkT, "wvT": wvT, "woT": woT,
            "wgT": wgT, "w0aT": w0aT, "w1aT": w1aT, "w0bT": w0bT, "w1bT": w1bT,
            "log_beta": lb, "g_state": gs, "b_state": bs, "g_mem": gm, "b_mem": bm,
            "b0a": b0a_c, "b1a": b1a_c, "b0b": b0b, "b1b": b1b,
            "ones1": ones1, "eye": eye,
        })
    res = run_bass_kernel_spmd(nc, in_maps, core_ids=list(range(8)), trace_cores=list(range(8)))
    out = np.empty((4, 2048, D), np.float32)
    for c in range(8):
        b, sh = c // 2, c % 2
        out[b, sh * NT:(sh + 1) * NT, :] = res.results[c]["out"]
    return out
